# version 08
# baseline (speedup 1.0000x reference)
import contextlib
import numpy as np

import concourse.bass as bass
import concourse.mybir as mybir
from concourse.bass_utils import run_bass_kernel_spmd
from concourse.alu_op_type import AluOpType as OP

F32 = mybir.dt.float32
BF16 = mybir.dt.bfloat16
I32 = mybir.dt.int32
AFT = mybir.ActivationFunctionType
AX = mybir.AxisListType.X

B, S, D = 2, 8192, 1024
H, DH, HPC = 16, 64, 4
TOK = 2048
INNER = 4096
EPS = 1e-5
SCALE_A = 0.125
SCALE_B = 0.03125
MASKVAL = -1.0e7

UPTO = 6
TAPS = 0
P1SUB = 9
P1PARTS = 9
P1GRPS = 16


class Sem:
    def __init__(self, h):
        self.h = h
        self.c = 0


def inc(instr, s, n=1):
    instr.then_inc(s.h, n)
    s.c += n
    return s.c


def dma(e, q, out, in_):
    e.dma_start(out=out, in_=in_).then_inc(q.h, 16)
    q.c += 16
    return q.c


class Bld:
    def __init__(self, nc, es):
        self.nc = nc
        self.es = es
        self._sems = []
        self._chans = {}
        self.sT = self._mk("sT")
        self.sV = self._mk("sV")
        self.sA = self._mk("sA")
        self.sG = self._mk("sG")
        self.cc = self._mk("cc")

    def _mk(self, name):
        s = Sem(self.es.enter_context(self.nc.semaphore(name)))
        self._sems.append(s)
        return s

    def chan(self, name):
        if name not in self._chans:
            self._chans[name] = self._mk("q_" + name)
        return self._chans[name]

    def all_sems(self):
        return list(self._sems)


def vgap(e, smalls, n=2):
    for _ in range(n):
        e.tensor_copy(out=smalls[:, 254:255], in_=smalls[:, 253:254])


def barrier(b, block):
    snap = [(s, s.c) for s in b.all_sems() if s.c > 0]
    for en in ("sync", "vector", "scalar", "tensor", "gpsimd"):
        @getattr(block, en)
        def _(e, snap=snap):
            for s, v in snap:
                e.wait_ge(s.h, v)



def emit_ln_group(b, block, srcs, dsts, smalls, sq8, pre_wait_v=(), pre_wait_a=()):
    n = len(srcs)
    stats = {nm: smalls[:, i * 4:i * 4 + 4] for i, nm in
             enumerate(["s1", "s2", "mean", "vpe", "rstd", "nbias"])}
    col = lambda t, k: t[:, k:k + 1]

    @block.vector
    def _(e):
        for s, v in pre_wait_v:
            e.wait_ge(s.h, v)
        for k in range(n):
            e.reduce_sum(out=col(stats["s1"], k), in_=srcs[k], axis=AX)
        instr = None
        for k in range(n):
            instr = e.tensor_scalar(out=col(stats["mean"], k), in0=col(stats["s1"], k),
                                    scalar1=1.0 / D, scalar2=None, op0=OP.mult)
        inc(instr, b.sV)
    t_v1 = b.sV.c
    t_v2, t_v3 = t_v1 + 1, t_v1 + 2

    @block.scalar
    def _(e):
        for s, v in pre_wait_a:
            e.wait_ge(s.h, v)
        instr = None
        for k in range(n):
            instr = e.activation(out=sq8[:, k * 1024:(k + 1) * 1024], in_=srcs[k],
                                 func=AFT.Square, accum_out=col(stats["s2"], k))
        inc(instr, b.sA)
        t_s2 = b.sA.c
        e.wait_ge(b.sV.h, t_v2)
        instr = None
        for k in range(n):
            instr = e.activation(out=col(stats["s2"], k), in_=col(stats["vpe"], k), func=AFT.Sqrt)
        inc(instr, b.sA)
        e.wait_ge(b.sV.h, t_v3)
        instr = None
        for k in range(n):
            instr = e.activation(out=dsts[k], in_=srcs[k], func=AFT.Identity,
                                 scale=col(stats["rstd"], k), bias=col(stats["nbias"], k))
        inc(instr, b.sA)
    t_s2 = b.sA.c - 2
    t_sq = b.sA.c - 1

    @block.vector
    def _(e):
        e.wait_ge(b.sA.h, t_s2)
        for k in range(n):
            e.tensor_scalar(out=col(stats["vpe"], k), in0=col(stats["s2"], k),
                            scalar1=1.0 / D, scalar2=None, op0=OP.mult)
        for k in range(n):
            e.tensor_scalar(out=col(stats["s1"], k), in0=col(stats["mean"], k),
                            scalar1=col(stats["mean"], k), scalar2=None, op0=OP.mult)
        instr = None
        for k in range(n):
            instr = e.tensor_scalar(out=col(stats["vpe"], k), in0=col(stats["vpe"], k),
                                    scalar1=col(stats["s1"], k), scalar2=EPS,
                                    op0=OP.subtract, op1=OP.add)
        inc(instr, b.sV)
    assert b.sV.c == t_v2

    @block.vector
    def _(e):
        e.wait_ge(b.sA.h, t_sq)
        for k in range(n):
            e.reciprocal(out=col(stats["rstd"], k), in_=col(stats["s2"], k))
        instr = None
        for k in range(n):
            instr = e.tensor_scalar(out=col(stats["nbias"], k), in0=col(stats["mean"], k),
                                    scalar1=col(stats["rstd"], k), scalar2=-1.0,
                                    op0=OP.mult, op1=OP.mult)
        inc(instr, b.sV)
    assert b.sV.c == t_v3
    return b.sA.c



def build_nc(upto=None, taps=None):
    upto = UPTO if upto is None else upto
    taps = TAPS if taps is None else taps
    nc = bass.Bass("TRN2", target_bir_lowering=False, detect_race_conditions=False)

    dT = {}
    def din(name, shape, dt=F32):
        dT[name] = nc.dram_tensor(name, shape, dt, kind="ExternalInput").ap()
    din("x_full", [S, D]); din("x_own", [TOK, D])
    din("wqaT", [D, HPC * DH]); din("wkaT", [D, HPC * DH]); din("wvaT", [D, HPC * DH])
    din("wfaT", [D, HPC]); din("bfa", [HPC, 1]); din("wogT", [HPC * DH, D])
    din("wqbT", [D, D]); din("wkbT", [D, D]); din("wvbT", [D, D])
    din("bqb", [D, 1]); din("bkb", [D, 1]); din("bvb", [1, D])
    din("wgT", [D, INNER]); din("wuT", [D, INNER]); din("wdT", [INNER, D])
    din("masks", [4 * 128, 512]); din("ident", [128, 128], BF16)

    out_own = nc.dram_tensor("out_own", [TOK, D], F32, kind="ExternalOutput").ap()

    qa = nc.dram_tensor("qa_dram", [HPC * 72, S], BF16).ap()
    ka = nc.dram_tensor("ka_dram", [HPC * 72, S], BF16).ap()
    fD = nc.dram_tensor("f_dram", [HPC, S], F32).ap()
    rs_in = nc.dram_tensor("rs_in", [S, D], BF16)
    rs_out = nc.dram_tensor("rs_out", [TOK, D], BF16)
    wg_q = nc.dram_tensor("wg_q", [D, INNER], BF16).ap()
    wu_q = nc.dram_tensor("wu_q", [D, INNER], BF16).ap()
    wd_q = nc.dram_tensor("wd_q", [INNER, D], BF16).ap()
    hD = nc.dram_tensor("h_dram", [TOK, INNER], F32).ap()
    qbD = nc.dram_tensor("qb_dram", [D, TOK], BF16).ap()
    agk_in = nc.dram_tensor("agk_in", [D, TOK], BF16)
    agk_out = nc.dram_tensor("agk_out", [4 * D, TOK], BF16)
    agv_in = nc.dram_tensor("agv_in", [TOK, D], BF16)
    agv_out = nc.dram_tensor("agv_out", [S, D], BF16)

    tap = {}
    if taps:
        for p in (2, 3, 4):
            tap[p] = nc.dram_tensor(f"tap_resid{p}", [TOK, D], F32, kind="ExternalOutput").ap()
    if taps >= 9:
        tap["smalls"] = nc.dram_tensor("tap_smalls", [128, 256], F32, kind="ExternalOutput").ap()
        tap["h"] = nc.dram_tensor("tap_h", [TOK, INNER], F32, kind="ExternalOutput").ap()
        tap["wgq"] = nc.dram_tensor("tap_wgq", [INNER, D], BF16, kind="ExternalOutput").ap()
        tap["xqt"] = nc.dram_tensor("tap_xqt", [128, 4096], BF16, kind="ExternalOutput").ap()
        tap["rsin"] = nc.dram_tensor("tap_rsin", [S, D], F32, kind="ExternalOutput").ap()
        tap["f"] = nc.dram_tensor("tap_f", [HPC, S], F32, kind="ExternalOutput").ap()
        tap["qa"] = nc.dram_tensor("tap_qa", [HPC * 72, S], BF16, kind="ExternalOutput").ap()
        tap["ka"] = nc.dram_tensor("tap_ka", [HPC * 72, S], BF16, kind="ExternalOutput").ap()

    es = contextlib.ExitStack()
    with es:
        b = Bld(nc, es)
        sb = lambda n, sh, dt: es.enter_context(nc.sbuf_tensor(n, sh, dt))
        resid = sb("resid", [128, 16 * 1024], F32)
        AB = sb("AB", [128, 45312], BF16)
        AF = sb("AF", [128, 10240], F32)
        AI = sb("AI", [128, 1024], I32)
        sq8 = sb("sq8", [128, 4096], mybir.dt.int8)
        ident = sb("ident_s", [128, 128], BF16)
        smalls = sb("smalls", [128, 256], F32)

        st = dict(dT=dT, out_own=out_own, qa=qa, ka=ka, fD=fD,
                  rs_in=rs_in, rs_out=rs_out, wg_q=wg_q, wu_q=wu_q,
                  wd_q=wd_q, hD=hD, qbD=qbD, agk_in=agk_in, agk_out=agk_out,
                  agv_in=agv_in, agv_out=agv_out, resid=resid, AB=AB, AF=AF,
                  AI=AI, sq8=sq8, ident=ident, smalls=smalls, tap=tap, nc=nc, es=es)

        block = es.enter_context(nc.Block())
        st["block"] = block

        st["upto"] = upto
        phase1(b, st)
        if upto >= 2:
            phase2a(b, st)
        if upto >= 3:
            quant_weights(b, st)
        if "rsin" in tap:
            qT9 = b.chan("t9")

            @block.gpsimd
            def _(e):
                dma(e, qT9, tap["rsin"], rs_in.ap()[:, :])
                dma(e, qT9, tap["f"], fD[:, :])
                dma(e, qT9, tap["qa"], qa[:, :])
                dma(e, qT9, tap["ka"], ka[:, :])
                e.wait_ge(qT9.h, qT9.c)
            barrier(b, st["block"])
        if upto >= 2:
            phase2(b, st)
            emit_tap(b, st, 2)
        if upto >= 3:
            phase_ffn(b, st, first=True)
            if "smalls" in tap:
                qT8 = b.chan("t8")

                @block.gpsimd
                def _(e):
                    dma(e, qT8, tap["smalls"], smalls[:, :])
                    dma(e, qT8, tap["h"], st["hD"][:, :])
                    dma(e, qT8, tap["wgq"], st["wd_q"][:, :])
                    dma(e, qT8, tap["xqt"], AB[:, 32768:45056 + 0][:, 8192:8192 + 4096] if False else AB[:, 40960:45056])
                    e.wait_ge(qT8.h, qT8.c)
                barrier(b, st["block"])
            emit_tap(b, st, 3)
        if upto >= 4:
            phase4(b, st)
            emit_tap(b, st, 4)
        if upto >= 5:
            phase_ffn(b, st, first=False)
        phase_out(b, st)

    return nc


def emit_tap(b, st, p):
    if p not in st["tap"]:
        return
    block, resid = st["block"], st["resid"]
    barrier(b, st["block"])

    qTap = b.chan("tap")

    @block.scalar
    def _(e):
        for t in range(16):
            dma(e, qTap, st["tap"][p][t * 128:(t + 1) * 128, :],
                resid[:, t * 1024:(t + 1) * 1024])
    barrier(b, st["block"])



def phase1(b, st):
    nc, block = st["nc"], st["block"]
    es = contextlib.ExitStack()
    st["es"] = es

    dT, AB, AF, AI = st["dT"], st["AB"], st["AF"], st["AI"]
    smalls, ident = st["smalls"], st["ident"]
    qa, ka, fD = st["qa"], st["ka"], st["fD"]

    WQ0, WK0, WV0, WF0 = 0, 2048, 4096, 6144
    XNT0 = 6176
    XNB0 = XNT0 + 4096
    EV0 = XNB0 + 4096
    ONE0 = EV0 + 1024
    SPL0 = ONE0 + 8192
    VA0 = 28672
    VEND = VA0 + HPC * 64 * 65
    assert VEND <= 45312, VEND
    KB0 = 0
    QR0 = 8192
    PR0 = 9216
    UN0 = 11264
    UGT0 = 12288

    XS0 = 0
    FEV0 = 4096
    BC0 = 4608
    MK0 = 8192

    sT, sV, sA = b.sT, b.sV, b.sA
    qC = b.chan("c1"); qWa = b.chan("wa"); qX = b.chan("x")
    qKd = [b.chan("qkd0"), b.chan("qkd1")]; qFd = b.chan("fd")
    qOn = b.chan("ones"); qSp = b.chan("split"); qKb = b.chan("kb")
    qQs = [b.chan("qs0"), b.chan("qs1")]; qL = b.chan("lrow")
    qU = [b.chan("ugt0"), b.chan("ugt1")]
    qY = [b.chan("y0"), b.chan("y1")]

    pU = [es.enter_context(nc.psum_tensor(f"pU{i}", [65, 512], F32)) for i in range(2)]
    es_a = contextlib.ExitStack()
    pQK = [es_a.enter_context(nc.psum_tensor(f"pQK{i}", [128, 512], F32)) for i in range(2)]
    pV = es_a.enter_context(nc.psum_tensor("pV", [128, 512], F32))
    pF = es_a.enter_context(nc.psum_tensor("pF", [4, 512], F32))
    psb = es_a.enter_context(nc.psum_tensor("psb", [128, 2048], BF16))

    tok = {}

    @block.sync
    def _(e):
        dma(e, qC, ident[:, :], dT["ident"][:, :])
        dma(e, qC, AF[:, MK0:MK0 + 2048].rearrange("p (m n) -> p m n", n=512),
            dT["masks"].rearrange("(m p) n -> p m n", p=128))
        tok["consts"] = dma(e, qC, smalls[0:HPC, 96:97], dT["bfa"][:, :])

    for name, dst0, ncol in (("wqaT", WQ0, 256), ("wkaT", WK0, 256),
                             ("wvaT", WV0, 256), ("wfaT", WF0, 4)):
        @block.sync
        def _(e, name=name, ncol=ncol):
            prev = tok.get("wcast_prev")
            if prev:
                e.wait_ge(sV.h, prev)
            tok[name] = dma(e, qWa, AF[:, XS0:XS0 + 8 * ncol].rearrange("p (c n) -> p c n", n=ncol),
                            dT[name].rearrange("(c p) n -> p c n", p=128))

        @block.vector
        def _(e, name=name, dst0=dst0, ncol=ncol):
            e.wait_ge(qWa.h, tok[name])
            instr = e.tensor_copy(out=AB[:, dst0:dst0 + 8 * ncol], in_=AF[:, XS0:XS0 + 8 * ncol])
            inc(instr, b.sV)
            tok["wcast_prev"] = b.sV.c

    @block.vector
    def _(e):
        e.memset(smalls[:, 253:254], 0.0)
        e.memset(smalls[:, 254:255], 0.0)
        e.memset(AF[0:64, BC0:BC0 + 512], 0.0)
        e.memset(AB[0:1, ONE0:ONE0 + 8192], 1.0)
        vones = AB[:, VA0:VEND].rearrange("p (t c) -> p t c", c=65)
        instr = e.memset(vones[:, :, 64:65], 1.0)
        inc(instr, b.sV)
        tok["wcast"] = b.sV.c

    @block.gpsimd
    def _(e):
        e.wait_ge(sV.h, tok["wcast"])
        ka3 = ka.rearrange("(h rr) n -> h rr n", rr=72)
        qa3 = qa.rearrange("(h rr) n -> h rr n", rr=72)
        for r in range(3):
            for h in range(HPC):
                dma(e, qOn, ka3[h:h + 1, 64 + r, :], AB[0:1, ONE0:ONE0 + 8192])
                dma(e, qOn, qa3[h:h + 1, 67 + r, :], AB[0:1, ONE0:ONE0 + 8192])
        tok["onesrows"] = qOn.c

    NG = P1GRPS
    for grp in range(NG):
        t0 = grp * 512
        xs = [AF[:, XS0 + k * 1024: XS0 + (k + 1) * 1024] for k in range(4)]
        xnb = [AB[:, XNB0 + k * 1024: XNB0 + (k + 1) * 1024] for k in range(4)]
        xnt = lambda ic: AB[:, XNT0 + ic * 512: XNT0 + (ic + 1) * 512]

        @block.sync
        def _(e, grp=grp, t0=t0, xs=xs):
            if grp > 0:
                e.wait_ge(sA.h, tok[("xn", grp - 1)])
            else:
                e.wait_ge(sV.h, tok["wcast"])
            for k in range(4):
                dma(e, qX, xs[k], dT["x_full"][t0 + k * 128: t0 + (k + 1) * 128, :])
            tok[("x", grp)] = qX.c

        pwv = [(qX, tok[("x", grp)])]
        pwa = [(qX, tok[("x", grp)]), (qC, tok["consts"])]
        if grp == 0:
            pwa.append((sV, tok["wcast"]))
            pwv.append((sV, tok["wcast"]))
        elif P1PARTS >= 2:
            pwa.append((sT, tok[("tp", grp - 1)]))
        tok[("xn", grp)] = emit_ln_group(b, block, xs, xnb, smalls, st["sq8"],
                                         pre_wait_v=pwv, pre_wait_a=pwa)

        for k in range(4 if P1PARTS >= 2 else 0):
            for ic in range(8):
                idx = k * 8 + ic

                @block.tensor
                def _(e, grp=grp, k=k, ic=ic, idx=idx, xnb=xnb):
                    if idx == 0:
                        e.wait_ge(sA.h, tok[("xn", grp)])
                        if grp > 0:
                            e.wait_ge(sV.h, tok[("tpev", grp - 1, 31)])
                    if idx >= 2:
                        e.wait_ge(sV.h, tok[("tpev", grp, idx - 2)])
                    instr = e.transpose(psb[:, (idx % 2) * 1024:(idx % 2) * 1024 + 128],
                                        xnb[k][:, ic * 128:(ic + 1) * 128], ident[:, :])
                    inc(instr, b.sT)
                    tok[("tp1", grp, idx)] = b.sT.c
                    if idx == 31:
                        tok[("tp", grp)] = b.sT.c

                @block.vector
                def _(e, grp=grp, k=k, ic=ic, idx=idx):
                    e.wait_ge(sT.h, tok[("tp1", grp, idx)])
                    instr = e.tensor_copy(out=xnt(ic)[:, k * 128:(k + 1) * 128],
                                          in_=psb[:, (idx % 2) * 1024:(idx % 2) * 1024 + 128])
                    inc(instr, b.sV)
                    tok[("tpev", grp, idx)] = b.sV.c

        for proj, w0, scl, dram in ((("q", WQ0, SCALE_A, qa), ("k", WK0, 1.0, ka)) if P1PARTS >= 3 else ()):
            @block.tensor
            def _(e, grp=grp, proj=proj, w0=w0):
                if proj == "q":
                    e.wait_ge(sV.h, tok[("tpev", grp, 31)])
                for oc in range(2):
                    prev = tok.get(("qkev", grp, "q", oc)) if proj == "k" else \
                           (tok.get(("qkev", grp - 1, "k", oc)) if grp > 0 else None)
                    if prev:
                        e.wait_ge(sA.h, prev)
                    instr = None
                    for ic in range(8):
                        instr = e.matmul(pQK[oc][:, :],
                                         AB[:, w0 + ic * 256 + oc * 128: w0 + ic * 256 + (oc + 1) * 128],
                                         xnt(ic), start=(ic == 0), stop=(ic == 7))
                    inc(instr, b.sT)
                    tok[("qk", grp, proj, oc)] = b.sT.c

            @block.scalar
            def _(e, grp=grp, proj=proj, scl=scl):
                for oc in range(2):
                    e.wait_ge(sT.h, tok[("qk", grp, proj, oc)])
                    prev = tok.get(("qkdma", grp, "q", oc)) if proj == "k" else \
                           (tok.get(("qkdma", grp - 1, "k", oc)) if grp > 0 else None)
                    if prev:
                        e.wait_ge(qKd[oc].h, prev)
                    instr = e.activation(out=AB[:, EV0 + oc * 512: EV0 + (oc + 1) * 512],
                                         in_=pQK[oc][:, :], func=AFT.Identity, scale=scl)
                    inc(instr, b.sA)
                    tok[("qkev", grp, proj, oc)] = b.sA.c

            @block.gpsimd
            def _(e, grp=grp, proj=proj, dram=dram, t0=t0):
                for oc in range(2):
                    e.wait_ge(sA.h, tok[("qkev", grp, proj, oc)])
                    for hh in range(2):
                        h = oc * 2 + hh
                        dma(e, qKd[oc], dram[h * 72: h * 72 + 64, t0:t0 + 512],
                            AB[hh * 64:(hh + 1) * 64, EV0 + oc * 512: EV0 + (oc + 1) * 512])
                    tok[("qkdma", grp, proj, oc)] = qKd[oc].c

        for k in range(4 if P1PARTS >= 4 else 0):
            @block.tensor
            def _(e, grp=grp, k=k):
                prev = tok.get(("vev", grp, k - 1)) if k > 0 else \
                       (tok.get(("vev", grp - 1, 3)) if grp > 0 else None)
                if prev:
                    e.wait_ge(sV.h, prev)
                instr = None
                for ic in range(8):
                    instr = e.matmul(pV[:, 0:256], xnt(ic)[:, k * 128:(k + 1) * 128],
                                     AB[:, WV0 + ic * 256: WV0 + (ic + 1) * 256],
                                     start=(ic == 0), stop=(ic == 7))
                inc(instr, b.sT)
                tok[("v", grp, k)] = b.sT.c

            @block.vector
            def _(e, grp=grp, k=k):
                e.wait_ge(sT.h, tok[("v", grp, k)])
                kt = grp * 4 + k
                instr = None
                for h in range(HPC):
                    c0 = VA0 + (h * 64 + kt) * 65
                    instr = e.tensor_copy(out=AB[:, c0:c0 + 64], in_=pV[:, h * 64:(h + 1) * 64])
                inc(instr, b.sV)
                tok[("vev", grp, k)] = b.sV.c

        if P1PARTS < 5:
            continue

        @block.tensor
        def _(e, grp=grp):
            if grp > 0:
                e.wait_ge(sA.h, tok[("fev", grp - 1)])
            instr = None
            for ic in range(8):
                instr = e.matmul(pF[:, :], AB[:, WF0 + ic * 4: WF0 + (ic + 1) * 4],
                                 xnt(ic), start=(ic == 0), stop=(ic == 7))
            inc(instr, b.sT)
            tok[("f", grp)] = b.sT.c

        @block.scalar
        def _(e, grp=grp):
            e.wait_ge(sT.h, tok[("f", grp)])
            if grp > 0:
                e.wait_ge(qFd.h, tok[("fdma", grp - 1)])
            instr = e.activation(out=AF[0:4, FEV0:FEV0 + 512], in_=pF[:, :],
                                 func=AFT.Sigmoid, bias=smalls[0:4, 96:97], scale=1.0)
            inc(instr, b.sA)
            tok[("fev", grp)] = b.sA.c

        @block.gpsimd
        def _(e, grp=grp, t0=t0):
            e.wait_ge(sA.h, tok[("fev", grp)])
            tok[("fdma", grp)] = dma(e, qFd, fD[:, t0:t0 + 512], AF[0:4, FEV0:FEV0 + 512])

    barrier(b, block)
    if P1SUB < 2:
        es.close()
        return

    fr = AF[0:4, 0:2048]
    Cw = [AF[0:4, 2048:4096], AF[0:4, 4096:6144]]
    mw = AF[0:4, 6144:8192]
    hi = AB[0:4, 0:2048]
    mid = AB[0:4, 2048:4096]
    lo = AB[0:4, 4096:6144]

    for qtr in range(4):
        c0 = qtr * 2048
        Cr = Cw[qtr % 2]

        @block.sync
        def _(e, qtr=qtr, c0=c0):
            if qtr > 0:
                e.wait_ge(sV.h, tok[("fp2", qtr - 1)])
            tok[("fload", qtr)] = dma(e, qX, fr, fD[:, c0:c0 + 2048])

        @block.vector
        def _(e, qtr=qtr):
            e.wait_ge(qX.h, tok[("fload", qtr)])
            instr = e.tensor_scalar(out=fr, in0=fr, scalar1=1e-6, scalar2=None, op0=OP.add)
            inc(instr, b.sV)
            tok[("fp", qtr)] = b.sV.c

        @block.scalar
        def _(e, qtr=qtr):
            e.wait_ge(sV.h, tok[("fp", qtr)])
            instr = e.activation(out=fr, in_=fr, func=AFT.Ln)
            inc(instr, b.sA)
            tok[("lnf", qtr)] = b.sA.c

        @block.vector
        def _(e, qtr=qtr, Cr=Cr):
            e.wait_ge(sA.h, tok[("lnf", qtr)])
            if qtr == 0:
                e.memset(smalls[0:4, 100:101], 0.0)
                e.memset(smalls[0:4, 101:102], 0.0)
                init = smalls[0:4, 100:101]
            else:
                init = Cw[(qtr - 1) % 2][:, 2047:2048]
            if qtr >= 1:
                e.wait_ge(qSp.h, tok[("kadma", qtr - 1)])
            e.tensor_tensor_scan(out=Cr, data0=fr, data1=fr, initial=init,
                                 op0=OP.add, op1=OP.bypass)
            vgap(e, smalls)
            e.tensor_copy(out=hi, in_=Cr)
            vgap(e, smalls)
            e.tensor_copy(out=mw, in_=hi)
            vgap(e, smalls)
            e.tensor_tensor(out=fr, in0=Cr, in1=mw, op=OP.subtract)
            vgap(e, smalls)
            e.tensor_copy(out=mid, in_=fr)
            vgap(e, smalls)
            e.tensor_copy(out=mw, in_=mid)
            vgap(e, smalls)
            e.tensor_tensor(out=fr, in0=fr, in1=mw, op=OP.subtract)
            vgap(e, smalls)
            instr = e.tensor_copy(out=lo, in_=fr)
            inc(instr, b.sV)
            tok[("csplit", qtr)] = b.sV.c
            tok[("fp2", qtr)] = b.sV.c

        @block.gpsimd
        def _(e, qtr=qtr, c0=c0):
            e.wait_ge(sV.h, tok[("csplit", qtr)])
            qa3 = qa.rearrange("(h rr) n -> h rr n", rr=72)
            for r, t in ((0, hi), (1, mid), (2, lo)):
                dma(e, qSp, qa3[:, 64 + r, c0:c0 + 2048], t)
            tok[("qadma", qtr)] = qSp.c

        @block.vector
        def _(e, qtr=qtr):
            e.wait_ge(qSp.h, tok[("qadma", qtr)])
            instr = None
            for t in (hi, mid, lo):
                instr = e.tensor_scalar(out=t, in0=t, scalar1=-1.0, scalar2=None, op0=OP.mult)
            inc(instr, b.sV)
            tok[("nsplit", qtr)] = b.sV.c

        @block.gpsimd
        def _(e, qtr=qtr, c0=c0):
            e.wait_ge(sV.h, tok[("nsplit", qtr)])
            ka3 = ka.rearrange("(h rr) n -> h rr n", rr=72)
            for r, t in ((0, hi), (1, mid), (2, lo)):
                dma(e, qSp, ka3[:, 67 + r, c0:c0 + 2048], t)
            tok[("kadma", qtr)] = qSp.c

    barrier(b, block)
    es_a.close()
    pS2 = [es.enter_context(nc.psum_tensor(f"pS2{i}", [128, 1024], F32)) for i in range(2)]
    pY = [es.enter_context(nc.psum_tensor(f"pY{i}", [128, 512], F32)) for i in range(2)]
    if P1SUB < 3:
        es.close()
        return

    kslot = lambda kt: AB[:, KB0 + kt * 128: KB0 + (kt + 1) * 128]
    qslot = lambda it: AB[:, QR0 + (it % 2) * 512: QR0 + (it % 2 + 1) * 512]
    pslot = lambda gi: AB[:, PR0 + (gi % 4) * 512: PR0 + (gi % 4 + 1) * 512]
    unslot = lambda it: AB[0:64, UN0 + (it % 2) * 512: UN0 + (it % 2 + 1) * 512]

    seq = [(h, qc) for h in range(HPC) for qc in range(16)]
    sA0 = b.sA.c
    ptot = sum((4 * qc + 4) // 2 for h, qc in seq)
    for gp in range(ptot):
        tok[("expp", gp)] = sA0 + gp + 1
    pslot2 = lambda gp: AB[:, PR0 + (gp % 2) * 1024: PR0 + (gp % 2 + 1) * 1024]

    gbase = 0
    for it, (h, qc) in enumerate(seq):
        kmax = 4 * qc + 4
        pbase = gbase // 2

        @block.sync
        def _(e, it=it, h=h, qc=qc):
            if qc == 0:
                if it > 0:
                    e.wait_ge(sT.h, tok[("pvlast", it - 1)])
                tok[("kld", h)] = dma(e, qKb, AB[0:70, KB0:KB0 + 8192], ka[h * 72: h * 72 + 70, :])
            if it >= 2:
                e.wait_ge(sT.h, tok[("qfree", it - 2)])
            tok[("qld", it)] = dma(e, qQs[it % 2], qslot(it)[0:70, :],
                                   qa[h * 72: h * 72 + 70, qc * 512:(qc + 1) * 512])

        @block.tensor
        def _(e, it=it, h=h, qc=qc, kmax=kmax, gbase=gbase, pbase=pbase):
            e.wait_ge(qQs[it % 2].h, tok[("qld", it)])
            if qc == 0:
                e.wait_ge(qKb.h, tok[("kld", h)])
            for kp in range(kmax // 2):
                gp = pbase + kp
                if gp >= 2:
                    e.wait_ge(sA.h, tok[("expp", gp - 2)])
                instr = None
                for half in range(2):
                    kt = 2 * kp + half
                    instr = e.matmul(pS2[gp % 2][:, half * 512:(half + 1) * 512],
                                     kslot(kt)[0:70, :], qslot(it)[0:70, :],
                                     start=True, stop=True)
                inc(instr, b.sT)
                tok[("Sp", gp)] = b.sT.c
                if kp == kmax // 2 - 1:
                    tok[("qfree", it)] = b.sT.c
                if kp >= 1:
                    _pv2(e, b, tok, it, h, kp - 1, gbase, pbase, kmax, pU, AB, VA0, pslot2)
            _pv2(e, b, tok, it, h, kmax // 2 - 1, gbase, pbase, kmax, pU, AB, VA0, pslot2)
            tok[("pvlast", it)] = b.sT.c

        @block.vector
        def _(e, it=it, qc=qc, kmax=kmax, gbase=gbase, pbase=pbase):
            for kt in range(kmax - 4, kmax):
                gp = pbase + kt // 2
                d_idx = kt - (kmax - 4)
                half = kt % 2
                e.wait_ge(sT.h, tok[("Sp", gp)])
                instr = e.tensor_tensor(out=pS2[gp % 2][:, half * 512:(half + 1) * 512],
                                        in0=pS2[gp % 2][:, half * 512:(half + 1) * 512],
                                        in1=AF[:, MK0 + d_idx * 512: MK0 + (d_idx + 1) * 512],
                                        op=OP.add)
                inc(instr, b.sV)
                tok[("mask", gp, half)] = b.sV.c

        @block.scalar
        def _(e, it=it, qc=qc, kmax=kmax, gbase=gbase, pbase=pbase):
            for kp in range(kmax // 2):
                gp = pbase + kp
                diag = kp >= kmax // 2 - 2
                if diag:
                    e.wait_ge(sV.h, tok[("mask", gp, 1)])
                else:
                    e.wait_ge(sT.h, tok[("Sp", gp)])
                if gp >= 2:
                    e.wait_ge(sT.h, tok[("pv", 2 * (gp - 2) + 1)])
                instr = e.activation(out=pslot2(gp), in_=pS2[gp % 2][:, :], func=AFT.Exp)
                got = inc(instr, b.sA)
                assert got == tok[("expp", gp)], (got, tok[("expp", gp)])

        @block.vector
        def _(e, it=it):
            e.wait_ge(sT.h, tok[("pvlast", it)])
            if it >= 2:
                e.wait_ge(sV.h, tok[("unorm", it - 2)])
            instr = e.tensor_copy(out=AF[64:65, BC0:BC0 + 512], in_=pU[it % 2][64:65, :])
            inc(instr, b.sV)
            tok[("lcp", it)] = b.sV.c

        @block.gpsimd
        def _(e, it=it):
            e.wait_ge(sV.h, tok[("lcp", it)])
            dma(e, qL, AF[0:1, BC0:BC0 + 512], AF[64:65, BC0:BC0 + 512])
            dma(e, qL, AF[32:33, BC0:BC0 + 512], AF[64:65, BC0:BC0 + 512])
            tok[("lrow", it)] = qL.c

        @block.vector
        def _(e, it=it):
            e.wait_ge(qL.h, tok[("lrow", it)])
            e.reciprocal(out=AF[0:1, BC0:BC0 + 512], in_=AF[0:1, BC0:BC0 + 512])
            e.reciprocal(out=AF[32:33, BC0:BC0 + 512], in_=AF[32:33, BC0:BC0 + 512])
            vgap(e, smalls)
            e.stream_shuffle(out=AF[0:64, BC0:BC0 + 512], in_=AF[0:64, BC0:BC0 + 512],
                             mask=[0] * 32)
            if it >= 2:
                e.wait_ge(qU[it % 2].h, tok[("undma", it - 2)])
            vgap(e, smalls)
            instr = e.tensor_tensor(out=unslot(it), in0=pU[it % 2][0:64, :],
                                    in1=AF[0:64, BC0:BC0 + 512], op=OP.mult)
            inc(instr, b.sV)
            tok[("unorm", it)] = b.sV.c

        @block.gpsimd
        def _(e, it=it, h=h, qc=qc):
            e.wait_ge(sV.h, tok[("unorm", it)])
            c0 = UGT0 + (h // 2) * 8192 + qc * 512
            r0 = (h % 2) * 64
            tok[("undma", it)] = dma(e, qU[it % 2], AB[r0:r0 + 64, c0:c0 + 512], unslot(it))

        gbase += kmax

    barrier(b, block)
    if P1SUB < 4:
        es.close()
        return

    @block.sync
    def _(e):
        tok["wogld"] = dma(e, qWa, AF[:, XS0:XS0 + 2048].rearrange("p (c n) -> p c n", n=1024),
                           dT["wogT"].rearrange("(c p) n -> p c n", p=128))

    @block.vector
    def _(e):
        e.wait_ge(qWa.h, tok["wogld"])
        instr = e.tensor_copy(out=AB[:, QR0:QR0 + 2048], in_=AF[:, XS0:XS0 + 2048])
        inc(instr, b.sV)
        tok["wogcast"] = b.sV.c

    if P1SUB == 4:
        es.close()
        return
    for t in range(64 if P1SUB >= 6 else 1):
        @block.tensor
        def _(e, t=t):
            if t == 0:
                e.wait_ge(sV.h, tok["wogcast"])
            if t > 0:
                e.wait_ge(sV.h, tok[("yev", t - 1)])
            instr = None
            for half in range(2):
                for c2 in range(2):
                    lhs = AB[:, UGT0 + c2 * 8192 + t * 128: UGT0 + c2 * 8192 + (t + 1) * 128]
                    instr = e.matmul(pY[half][:, :], lhs,
                                     AB[:, QR0 + c2 * 1024 + half * 512: QR0 + c2 * 1024 + half * 512 + 512],
                                     start=(c2 == 0), stop=(c2 == 1))
            inc(instr, b.sT)
            tok[("yp", t)] = b.sT.c

        @block.vector
        def _(e, t=t):
            e.wait_ge(sT.h, tok[("yp", t)])
            if t >= 2:
                e.wait_ge(qY[t % 2].h, tok[("ydma", t - 2)])
            yb = AB[:, VA0 + (t % 2) * 1024: VA0 + (t % 2 + 1) * 1024]
            e.tensor_copy(out=yb[:, 0:512], in_=pY[0][:, :])
            instr = e.tensor_copy(out=yb[:, 512:1024], in_=pY[1][:, :])
            inc(instr, b.sV)
            tok[("yev", t)] = b.sV.c

        @block.gpsimd
        def _(e, t=t):
            e.wait_ge(sV.h, tok[("yev", t)])
            yb = AB[:, VA0 + (t % 2) * 1024: VA0 + (t % 2 + 1) * 1024]
            tok[("ydma", t)] = dma(e, qY[t % 2], st["rs_in"].ap()[t * 128:(t + 1) * 128, :], yb)

    barrier(b, block)
    es.close()


def _pv2(e, b, tok, it, h, kp, gbase, pbase, kmax, pU, AB, VA0, pslot2):
    gp = pbase + kp
    e.wait_ge(b.sA.h, tok[("expp", gp)])
    if kp == 0 and it >= 2:
        e.wait_ge(b.sV.h, tok[("unorm", it - 2)])
    for half in range(2):
        kt = 2 * kp + half
        c0 = VA0 + (h * 64 + kt) * 65
        instr = e.matmul(pU[it % 2][:, :], AB[:, c0:c0 + 65],
                         pslot2(gp)[:, half * 512:(half + 1) * 512],
                         start=(kt == 0), stop=(kt == kmax - 1))
        inc(instr, b.sT)
        tok[("pv", 2 * gp + half)] = b.sT.c



def phase2a(b, st):
    block = st["block"]

    @block.gpsimd
    def _(e):
        instr = e.collective_compute(
            "ReduceScatter", OP.add, replica_groups=[[0, 1, 2, 3], [4, 5, 6, 7]],
            ins=[st["rs_in"].ap().opt()], outs=[st["rs_out"].ap().opt()])
        inc(instr, b.cc)
        e.wait_ge(b.cc.h, b.cc.c)
        e.sem_inc(b.sG.h, 1)
        b.sG.c += 1
        st["tok_rs"] = b.sG.c


def phase2(b, st):
    nc, block = st["nc"], st["block"]
    dT, AF = st["dT"], st["AF"]
    resid, smalls = st["resid"], st["smalls"]
    sV = b.sV
    qP2 = [b.chan("p2a"), b.chan("p2b")]
    tok = {"rs": st["tok_rs"]}

    AB = st["AB"]
    for t in range(16):
        @block.sync
        def _(e, t=t):
            if t == 0:
                e.wait_ge(b.sG.h, tok["rs"])
            if t >= 2:
                e.wait_ge(sV.h, tok[("radd", t - 2)])
            s = (t % 2) * 2048
            dma(e, qP2[t % 2], AB[:, s:s + 1024], st["rs_out"].ap()[t * 128:(t + 1) * 128, :])
            dma(e, qP2[t % 2], AF[:, s:s + 1024], dT["x_own"][t * 128:(t + 1) * 128, :])
            tok[("ld", t)] = qP2[t % 2].c

        @block.vector
        def _(e, t=t):
            e.wait_ge(qP2[t % 2].h, tok[("ld", t)])
            s = (t % 2) * 2048
            e.tensor_copy(out=resid[:, t * 1024:(t + 1) * 1024], in_=AB[:, s:s + 1024])
            vgap(e, smalls)
            instr = e.tensor_tensor(out=resid[:, t * 1024:(t + 1) * 1024],
                                    in0=resid[:, t * 1024:(t + 1) * 1024],
                                    in1=AF[:, s:s + 1024], op=OP.add)
            inc(instr, b.sV)
            tok[("radd", t)] = b.sV.c

    barrier(b, block)




def quant_weights(b, st):
    nc, block, es0 = st["nc"], st["block"], None
    import contextlib as _ct
    es = _ct.ExitStack()
    dT, AB, AF, AI = st["dT"], st["AB"], st["AF"], st["AI"]
    smalls = st["smalls"]
    sT, sV, sA = b.sT, b.sV, b.sA
    qWq = [b.chan("wq0"), b.chan("wq1")]; qWst = b.chan("wst"); qPb = b.chan("pb")
    tok = {}
    GS0 = 32768
    OSC = {"g": 104, "u": 105, "d": 106}
    SW = {"g": 107, "u": 108, "d": 109}
    pPar = es.enter_context(nc.psum_tensor("pPar", [1, 8], F32))

    for wname, wdst, key in (("wgT", st["wg_q"], "g"), ("wuT", st["wu_q"], "u"),
                             ("wdT", st["wd_q"], "d")):
        wn = dT[wname].shape[1]
        wview = dT[wname].rearrange("(c p) n -> p c n", p=128)
        nview = wdst.rearrange("(c p) n -> p c n", p=128)
        npiece = 8
        cpp = wview.shape[1] // npiece
        pw = cpp * wn
        slot = lambda ch: AF[:, (ch % 2) * 4096:(ch % 2) * 4096 + 4096]

        for ch in range(npiece):
            @block.sync
            def _(e, ch=ch, wview=wview, pw=pw, cpp=cpp, wn=wn, key=key):
                if ch >= 2:
                    e.wait_ge(sA.h, tok[(key, "abs", ch - 2)])
                elif ch == 0 and key != "g":
                    e.wait_ge(sV.h, tok[({"u": "g", "d": "u"}[key], "clip", 7)])
                elif ch == 1 and key != "g":
                    e.wait_ge(sV.h, tok[({"u": "g", "d": "u"}[key], "clip", 7)])
                tok[(key, "wld", ch)] = dma(
                    e, qWq[ch % 2], slot(ch)[:, 0:pw].rearrange("p (c n) -> p c n", n=wn),
                    wview[:, ch * cpp:(ch + 1) * cpp, :])

            @block.scalar
            def _(e, ch=ch, pw=pw, key=key):
                e.wait_ge(qWq[ch % 2].h, tok[(key, "wld", ch)])
                if ch == 0 and key != "g":
                    e.wait_ge(sV.h, tok[({"u": "g", "d": "u"}[key], "rsum")])
                instr = None
                for cq in range(4):
                    instr = e.activation(out=st["sq8"][:, cq * 1024:(cq + 1) * 1024],
                                         in_=slot(ch)[:, cq * 1024:(cq + 1) * 1024],
                                         func=AFT.Abs,
                                         accum_out=smalls[:, 160 + ch * 4 + cq: 161 + ch * 4 + cq])
                inc(instr, b.sA)
                tok[(key, "abs", ch)] = b.sA.c

        @block.vector
        def _(e, key=key):
            e.wait_ge(sA.h, tok[(key, "abs", 7)])
            instr = e.reduce_sum(out=smalls[:, 126:127], in_=smalls[:, 160:192], axis=AX)
            inc(instr, b.sV)
            tok[(key, "rsum")] = b.sV.c

        @block.vector
        def _(e, key=key):
            if key == "g":
                e.memset(smalls[:, 252:253], 1.0)
            e.sem_inc(sV.h, 1)
            b.sV.c += 1
            tok[(key, "ones")] = b.sV.c

        @block.tensor
        def _(e, key=key):
            e.wait_ge(sV.h, tok[(key, "rsum")])
            e.wait_ge(sV.h, tok[(key, "ones")])
            if key != "g":
                e.wait_ge(sV.h, tok[({"u": "g", "d": "u"}[key], "scal")])
            instr = e.matmul(pPar[0:1, 0:1], smalls[:, 252:253], smalls[:, 126:127],
                             start=True, stop=True)
            inc(instr, b.sT)
            tok[(key, "par")] = b.sT.c

        @block.vector
        def _(e, key=key):
            e.wait_ge(sT.h, tok[(key, "par")])
            instr = e.tensor_copy(out=smalls[0:1, 127:128], in_=pPar[0:1, 0:1])
            inc(instr, b.sV)
            tok[(key, "parc")] = b.sV.c

        @block.sync
        def _(e, key=key):
            e.wait_ge(sV.h, tok[(key, "parc")])
            for p in (32, 64, 96):
                dma(e, qPb, smalls[p:p + 1, 127:128], smalls[0:1, 127:128])
            tok[(key, "parb")] = qPb.c

        @block.vector
        def _(e, key=key):
            e.wait_ge(qPb.h, tok[(key, "parb")])
            vgap(e, smalls)
            e.stream_shuffle(out=smalls[:, 127:128], in_=smalls[:, 127:128], mask=[0] * 32)
            vgap(e, smalls)
            e.tensor_scalar(out=smalls[:, OSC[key]:OSC[key] + 1], in0=smalls[:, 127:128],
                            scalar1=1.0 / (INNER * D), scalar2=1e-5, op0=OP.mult, op1=OP.max)
            vgap(e, smalls)
            instr = e.reciprocal(out=smalls[:, SW[key]:SW[key] + 1],
                                 in_=smalls[:, OSC[key]:OSC[key] + 1])
            inc(instr, b.sV)
            tok[(key, "scal")] = b.sV.c

        for ch in range(npiece):
            @block.sync
            def _(e, ch=ch, wview=wview, pw=pw, cpp=cpp, wn=wn, key=key):
                e.wait_ge(sV.h, tok[(key, "scal")])
                if ch >= 2:
                    e.wait_ge(sV.h, tok[(key, "clip", ch - 2)])
                tok[(key, "wld2", ch)] = dma(
                    e, qWq[ch % 2], slot(ch)[:, 0:pw].rearrange("p (c n) -> p c n", n=wn),
                    wview[:, ch * cpp:(ch + 1) * cpp, :])

            @block.vector
            def _(e, ch=ch, pw=pw, key=key):
                e.wait_ge(qWq[ch % 2].h, tok[(key, "wld2", ch)])
                if ch > 0:
                    e.wait_ge(qWst.h, tok[(key, "wst", ch - 1)])
                e.tensor_scalar(out=slot(ch)[:, 0:pw], in0=slot(ch)[:, 0:pw],
                                scalar1=smalls[:, SW[key]:SW[key] + 1], scalar2=None,
                                op0=OP.mult)
                instr = None
                vgap(e, smalls)
                for cq in range(4):
                    e.tensor_copy(out=AI[:, 0:1024], in_=slot(ch)[:, cq * 1024:(cq + 1) * 1024])
                    vgap(e, smalls)
                    instr = e.tensor_scalar(out=AB[0:128, GS0 + cq * 1024: GS0 + (cq + 1) * 1024],
                                            in0=AI[:, 0:1024], scalar1=-1.0, scalar2=1.0,
                                            op0=OP.max, op1=OP.min)
                    vgap(e, smalls)
                inc(instr, b.sV)
                tok[(key, "clip", ch)] = b.sV.c

            @block.sync
            def _(e, ch=ch, nview=nview, pw=pw, cpp=cpp, wn=wn, key=key):
                e.wait_ge(sV.h, tok[(key, "clip", ch)])
                tok[(key, "wst", ch)] = dma(
                    e, qWst, nview[:, ch * cpp:(ch + 1) * cpp, :],
                    AB[:, GS0:GS0 + pw].rearrange("p (c n) -> p c n", n=wn))

    barrier(b, block)
    es.close()




def phase_ffn(b, st, first):
    nc, block = st["nc"], st["block"]
    es = contextlib.ExitStack()
    dT, AB, AF, AI = st["dT"], st["AB"], st["AF"], st["AI"]
    resid, smalls = st["resid"], st["smalls"]
    sT, sV, sA = b.sT, b.sV, b.sA
    qWq = b.chan("wq"); qWst = b.chan("wst"); qGU = b.chan("gu")
    qH = [b.chan("h0"), b.chan("h1")]; qHL = b.chan("hl")
    tag = "f1" if first else "f2"
    tok = {}

    XQT0 = 0
    WG0 = 16384
    WU0 = 24576
    GS0 = 32768
    US0 = GS0 + 1024
    XQ0 = US0 + 1024
    WD0 = 0
    HQT0 = 32768
    HQ0 = 40960

    XN0 = 4096
    HB0 = 0

    AMX, S127, RS = 48, 64, 80
    OSC = {"g": 104, "u": 105, "d": 106}
    SW = {"g": 107, "u": 108, "d": 109}
    AMXH, S127H, RSH = 144, 160, 176

    psA = [es.enter_context(nc.psum_tensor(f"psA{tag}{i}", [128, 512], F32)) for i in range(4)]
    psb2 = es.enter_context(nc.psum_tensor(f"psb{tag}", [128, 2048], BF16))



    for t4 in range(4):
        srcs = [resid[:, (t4 * 4 + k) * 1024: (t4 * 4 + k + 1) * 1024] for k in range(4)]
        dsts = [AF[:, XN0 + k * 1024: XN0 + (k + 1) * 1024] for k in range(4)]
        pwv = []
        if t4 > 0:
            pwv.append((sT, tok[("xqtp", t4 - 1, 3)]))
        tok[("xn", t4)] = emit_ln_group(b, block, srcs, dsts, smalls, st["sq8"],
                                        pre_wait_v=pwv, pre_wait_a=pwv)

        @block.vector
        def _(e, t4=t4, dsts=dsts):
            e.wait_ge(sA.h, tok[("xn", t4)])
            for k in range(4):
                e.reduce_max(out=smalls[:, AMX + t4 * 4 + k: AMX + t4 * 4 + k + 1],
                             in_=dsts[k], axis=AX, apply_absolute_value=True)
            for k in range(4):
                c = t4 * 4 + k
                e.tensor_scalar(out=smalls[:, S127 + c: S127 + c + 1],
                                in0=smalls[:, AMX + c: AMX + c + 1],
                                scalar1=1e-5, scalar2=None, op0=OP.max)
            for k in range(4):
                c = t4 * 4 + k
                e.tensor_scalar(out=smalls[:, RS + c: RS + c + 1],
                                in0=smalls[:, S127 + c: S127 + c + 1],
                                scalar1=1.0 / 127.0, scalar2=None, op0=OP.mult)
            for k in range(4):
                c = t4 * 4 + k
                e.reciprocal(out=smalls[:, S127 + c: S127 + c + 1],
                             in_=smalls[:, S127 + c: S127 + c + 1])
            instr = None
            for k in range(4):
                c = t4 * 4 + k
                instr = e.tensor_scalar(out=smalls[:, S127 + c: S127 + c + 1],
                                        in0=smalls[:, S127 + c: S127 + c + 1],
                                        scalar1=127.0, scalar2=None, op0=OP.mult)
            inc(instr, b.sV)
            tok[("scx", t4)] = b.sV.c

        for k in range(4):
            t = t4 * 4 + k

            @block.vector
            def _(e, t4=t4, k=k, t=t, dsts=dsts):
                e.wait_ge(sV.h, tok[("scx", t4)])
                e.tensor_scalar(out=dsts[k], in0=dsts[k],
                                scalar1=smalls[:, S127 + t: S127 + t + 1], scalar2=None,
                                op0=OP.mult)
                if t > 0:
                    e.wait_ge(sA.h, tok[("xqa", t - 1)])
                vgap(e, smalls)
                instr = e.tensor_copy(out=AI[:, 0:1024], in_=dsts[k])
                inc(instr, b.sV)
                tok[("xqi", t)] = b.sV.c

            @block.scalar
            def _(e, t=t):
                e.wait_ge(sV.h, tok[("xqi", t)])
                if t >= 2:
                    e.wait_ge(sT.h, tok[("xqtpd", t - 2)])
                instr = e.activation(out=AB[:, XQ0 + (t % 2) * 1024: XQ0 + (t % 2 + 1) * 1024],
                                     in_=AI[:, 0:1024], func=AFT.Identity,
                                     scale=smalls[:, RS + t: RS + t + 1])
                inc(instr, b.sA)
                tok[("xqa", t)] = b.sA.c

            for ic in range(8):
                @block.tensor
                def _(e, t=t, ic=ic, t4=t4, k=k):
                    if ic == 0:
                        e.wait_ge(sA.h, tok[("xqa", t)])
                    idx = (t * 8 + ic)
                    if idx >= 2:
                        e.wait_ge(sV.h, tok[("xqev", idx - 2)])
                    instr = e.transpose(psb2[:, (idx % 2) * 1024:(idx % 2) * 1024 + 128],
                                        AB[:, XQ0 + (t % 2) * 1024 + ic * 128: XQ0 + (t % 2) * 1024 + (ic + 1) * 128],
                                        st["ident"][:, :])
                    inc(instr, b.sT)
                    tok[("xqtp1", idx)] = b.sT.c
                    if ic == 7:
                        tok[("xqtpd", t)] = b.sT.c
                        tok[("xqtp", t4, k)] = b.sT.c

                @block.vector
                def _(e, t=t, ic=ic):
                    idx = (t * 8 + ic)
                    e.wait_ge(sT.h, tok[("xqtp1", idx)])
                    instr = e.tensor_copy(
                        out=AB[:, XQT0 + ic * 2048 + t * 128: XQT0 + ic * 2048 + (t + 1) * 128],
                        in_=psb2[:, (idx % 2) * 1024:(idx % 2) * 1024 + 128])
                    inc(instr, b.sV)
                    tok[("xqev", idx)] = b.sV.c

    barrier(b, block)

    @block.vector
    def _(e):
        instr = e.memset(smalls[:, AMXH:AMXH + 16], 0.0)
        inc(instr, b.sV)
        tok["amxh0"] = b.sV.c

    gview = st["wg_q"].rearrange("(c p) n -> p c n", p=128)
    uview = st["wu_q"].rearrange("(c p) n -> p c n", p=128)
    for q4 in range(4):
        sT0, sA0, sV0 = b.sT.c, b.sA.c, b.sV.c
        qhb = [qH[0].c, qH[1].c]
        for t in range(16):
            tok[("gu", q4, t)] = sT0 + t + 1
            tok[("guev", q4, t)] = sA0 + t + 1
            tok[("hmul", q4, t)] = sV0 + t + 1
            tok[("hst", q4, t)] = qhb[t % 2] + 16 * (t // 2 + 1)

        @block.sync
        def _(e, q4=q4):
            if q4 > 0:
                e.wait_ge(sT.h, tok[("gu", q4 - 1, 15)])
            abg = AB[:, WG0:WG0 + 8192].rearrange("p (c n) -> p c n", n=1024)
            abu = AB[:, WU0:WU0 + 8192].rearrange("p (c n) -> p c n", n=1024)
            dma(e, qGU, abg, gview[:, :, q4 * 1024:(q4 + 1) * 1024])
            dma(e, qGU, abu, uview[:, :, q4 * 1024:(q4 + 1) * 1024])
            tok[("guld", q4)] = qGU.c

        @block.tensor
        def _(e, q4=q4):
            e.wait_ge(qGU.h, tok[("guld", q4)])
            if q4 == 0:
                e.wait_ge(sV.h, tok["amxh0"])
            for t in range(16):
                prev = tok[("guev", q4, t - 1)] if t > 0 else \
                       (tok.get(("guev", q4 - 1, 15)) if q4 > 0 else None)
                if prev:
                    e.wait_ge(sA.h, prev)
                instr = None
                for w0, p0 in ((WG0, 0), (WU0, 2)):
                    for ic in range(8):
                        lhs = AB[:, XQT0 + ic * 2048 + t * 128: XQT0 + ic * 2048 + (t + 1) * 128]
                        w = AB[:, w0 + ic * 1024: w0 + (ic + 1) * 1024]
                        st_, sp = (ic == 0), (ic == 7)
                        e.matmul(psA[p0][:, :], lhs, w[:, 0:512], start=st_, stop=sp)
                        instr = e.matmul(psA[p0 + 1][:, :], lhs, w[:, 512:1024], start=st_, stop=sp)
                got = inc(instr, b.sT)
                assert got == tok[("gu", q4, t)]

        @block.scalar
        def _(e, q4=q4):
            for t in range(16):
                e.wait_ge(sT.h, tok[("gu", q4, t)])
                prev = tok[("hmul", q4, t - 1)] if t > 0 else \
                       (tok.get(("hmul", q4 - 1, 15)) if q4 > 0 else None)
                if prev:
                    e.wait_ge(sV.h, prev)
                instr = None
                for hf in range(2):
                    e.activation(out=AB[:, GS0 + hf * 512: GS0 + (hf + 1) * 512],
                                 in_=psA[hf][:, :], func=AFT.Silu,
                                 scale=smalls[:, OSC["g"]:OSC["g"] + 1])
                    instr = e.activation(out=AB[:, US0 + hf * 512: US0 + (hf + 1) * 512],
                                         in_=psA[2 + hf][:, :], func=AFT.Identity,
                                         scale=smalls[:, OSC["u"]:OSC["u"] + 1])
                got = inc(instr, b.sA)
                assert got == tok[("guev", q4, t)]

        @block.vector
        def _(e, q4=q4):
            for t in range(16):
                e.wait_ge(sA.h, tok[("guev", q4, t)])
                if t >= 2:
                    e.wait_ge(qH[t % 2].h, tok[("hst", q4, t - 2)])
                hb = AF[:, HB0 + (t % 2) * 1024: HB0 + (t % 2 + 1) * 1024]
                e.tensor_tensor(out=hb, in0=AB[:, GS0:GS0 + 1024],
                                in1=AB[:, US0:US0 + 1024], op=OP.mult)
                vgap(e, smalls)
                e.reduce_max(out=smalls[:, 192:193], in_=hb, axis=AX,
                             apply_absolute_value=True)
                vgap(e, smalls)
                instr = e.tensor_scalar(out=smalls[:, AMXH + t: AMXH + t + 1],
                                        in0=smalls[:, AMXH + t: AMXH + t + 1],
                                        scalar1=smalls[:, 192:193], scalar2=None, op0=OP.max)
                got = inc(instr, b.sV)
                assert got == tok[("hmul", q4, t)]

        @block.gpsimd
        def _(e, q4=q4):
            for t in range(16):
                e.wait_ge(sV.h, tok[("hmul", q4, t)])
                hb = AF[:, HB0 + (t % 2) * 1024: HB0 + (t % 2 + 1) * 1024]
                got = dma(e, qH[t % 2], st["hD"][t * 128:(t + 1) * 128,
                                                 q4 * 1024:(q4 + 1) * 1024], hb)
                assert got == tok[("hst", q4, t)]

    barrier(b, block)

    @block.vector
    def _(e):
        e.tensor_scalar(out=smalls[:, 196:212], in0=smalls[:, AMXH:AMXH + 16],
                        scalar1=1e-5, scalar2=None, op0=OP.max)
        e.memset(smalls[:, 194:195], 0.0)
        e.memset(smalls[:, 195:196], 0.0)
        e.tensor_scalar(out=smalls[:, RSH:RSH + 16], in0=smalls[:, 196:212],
                        scalar1=1.0 / 127.0, scalar2=None, op0=OP.mult)
        e.reciprocal(out=smalls[:, S127H:S127H + 16], in_=smalls[:, 196:212])
        e.memset(smalls[:, 194:195], 0.0)
        e.memset(smalls[:, 195:196], 0.0)
        instr = e.tensor_scalar(out=smalls[:, S127H:S127H + 16], in0=smalls[:, S127H:S127H + 16],
                                scalar1=127.0, scalar2=None, op0=OP.mult)
        inc(instr, b.sV)
        tok["sch"] = b.sV.c

    dview = st["wd_q"].rearrange("(c p) n -> p c n", p=128)

    @block.sync
    def _(e):
        abd = AB[:, WD0:WD0 + 32768].rearrange("p (c n) -> p c n", n=1024)
        dma(e, qGU, abd, dview[:, :, :])
        tok["wdld"] = qGU.c

    for t in range(16):
        @block.sync
        def _(e, t=t):
            if t > 0:
                e.wait_ge(sV.h, tok[("hqi", t - 1, 3)])
            tok[("hld", t)] = dma(e, qHL, AF[:, 2048:2048 + 4096], st["hD"][t * 128:(t + 1) * 128, :])

        sAb = b.sA.c
        for cq in range(4):
            tok[("hqa", t, cq)] = sAb + cq + 1

        @block.vector
        def _(e, t=t):
            e.wait_ge(qHL.h, tok[("hld", t)])
            if t == 0:
                e.wait_ge(sV.h, tok["sch"])
            hl = AF[:, 2048:2048 + 4096]
            e.tensor_scalar(out=hl, in0=hl, scalar1=smalls[:, S127H + t: S127H + t + 1],
                            scalar2=None, op0=OP.mult)
            if t > 0:
                e.wait_ge(sA.h, tok[("hqa", t - 1, 3)])
                e.wait_ge(sT.h, tok[("hqtp", (t - 1) * 32 + 31)])
            vgap(e, smalls)
            for cq in range(4):
                if cq > 0:
                    e.wait_ge(sA.h, tok[("hqa", t, cq - 1)])
                instr = e.tensor_copy(out=AI[:, 0:1024], in_=hl[:, cq * 1024:(cq + 1) * 1024])
                inc(instr, b.sV)
                tok[("hqi", t, cq)] = b.sV.c
            tok[("hq_y", t)] = b.sV.c

        @block.scalar
        def _(e, t=t):
            for cq in range(4):
                e.wait_ge(sV.h, tok[("hqi", t, cq)])
                instr = e.activation(out=AB[:, HQ0 + cq * 1024: HQ0 + (cq + 1) * 1024],
                                     in_=AI[:, 0:1024], func=AFT.Identity,
                                     scale=smalls[:, RSH + t: RSH + t + 1])
                got = inc(instr, b.sA)
                assert got == tok[("hqa", t, cq)]
        for i2 in range(32):
            @block.tensor
            def _(e, t=t, i2=i2):
                if i2 % 8 == 0:
                    e.wait_ge(sA.h, tok[("hqa", t, i2 // 8)])
                idx = t * 32 + i2
                if idx >= 2:
                    e.wait_ge(sV.h, tok[("hqev", idx - 2)])
                instr = e.transpose(psb2[:, (idx % 2) * 1024:(idx % 2) * 1024 + 128],
                                    AB[:, HQ0 + i2 * 128: HQ0 + (i2 + 1) * 128],
                                    st["ident"][:, :])
                inc(instr, b.sT)
                tok[("hqtp", idx)] = b.sT.c

            @block.vector
            def _(e, t=t, i2=i2):
                idx = t * 32 + i2
                e.wait_ge(sT.h, tok[("hqtp", idx)])
                if t >= 2 and i2 == 0:
                    e.wait_ge(sT.h, tok[("down", t - 2)])
                instr = e.tensor_copy(out=AB[:, HQT0 + (t % 2) * 4096 + i2 * 128:
                                              HQT0 + (t % 2) * 4096 + (i2 + 1) * 128],
                                      in_=psb2[:, (idx % 2) * 1024:(idx % 2) * 1024 + 128])
                inc(instr, b.sV)
                tok[("hqev", idx)] = b.sV.c

        @block.tensor
        def _(e, t=t):
            e.wait_ge(sV.h, tok[("hqev", t * 32 + 31)])
            if t == 0:
                e.wait_ge(qGU.h, tok["wdld"])
            if t >= 2:
                e.wait_ge(sV.h, tok[("dadd", t - 2)])
            instr = None
            for i2 in range(32):
                lhs = AB[:, HQT0 + (t % 2) * 4096 + i2 * 128: HQT0 + (t % 2) * 4096 + (i2 + 1) * 128]
                wd = AB[:, WD0 + i2 * 1024: WD0 + (i2 + 1) * 1024]
                e.matmul(psA[2 * (t % 2)][:, :], lhs, wd[:, 0:512],
                         start=(i2 == 0), stop=(i2 == 31))
                instr = e.matmul(psA[2 * (t % 2) + 1][:, :], lhs, wd[:, 512:1024],
                                 start=(i2 == 0), stop=(i2 == 31))
            inc(instr, b.sT)
            tok[("down", t)] = b.sT.c

        @block.vector
        def _(e, t=t):
            e.wait_ge(sT.h, tok[("down", t)])
            instr = None
            for hf in range(2):
                instr = e.scalar_tensor_tensor(
                    out=resid[:, t * 1024 + hf * 512: t * 1024 + (hf + 1) * 512],
                    in0=psA[2 * (t % 2) + hf][:, :],
                    scalar=smalls[:, OSC["d"]:OSC["d"] + 1],
                    in1=resid[:, t * 1024 + hf * 512: t * 1024 + (hf + 1) * 512],
                    op0=OP.mult, op1=OP.add)
            inc(instr, b.sV)
            tok[("dadd", t)] = b.sV.c

    barrier(b, block)
    es.close()



def phase4(b, st):
    nc, block = st["nc"], st["block"]
    es = contextlib.ExitStack()
    dT, AB, AF, AI = st["dT"], st["AB"], st["AF"], st["AI"]
    resid, smalls, ident = st["resid"], st["smalls"], st["ident"]
    sT, sV, sA = b.sT, b.sV, b.sA
    qC4 = b.chan("c4"); qWb = b.chan("wb"); qPJ = b.chan("pj")
    qVW = b.chan("vw"); qQT = b.chan("qt")
    qKr = [b.chan("k40"), b.chan("k41")]
    qVr = [b.chan(f"v4{i}") for i in range(4)]
    tok = {}

    XNT0 = 0
    WB0 = 16384
    EV0 = 24576
    XNB0 = 26624
    WS0 = 0
    QT0 = 32768
    KR0 = 34816
    VR0 = 36864
    BCB = 6144
    XN0 = 4096

    pS = [es.enter_context(nc.psum_tensor(f"p4s{i}", [128, 512], F32)) for i in range(2)]
    pPV = [es.enter_context(nc.psum_tensor(f"p4v{i}", [128, 512], F32)) for i in range(4)]
    psb4 = es.enter_context(nc.psum_tensor("psb4", [128, 2048], BF16))

    for t4 in range(4):
        srcs = [resid[:, (t4 * 4 + k) * 1024: (t4 * 4 + k + 1) * 1024] for k in range(4)]
        dsts = [AB[:, XNB0 + k * 1024: XNB0 + (k + 1) * 1024] for k in range(4)]
        pwv = []
        if t4 > 0:
            pwv.append((sT, tok[("tp", t4 - 1, 31)]))
        tok[("xn", t4)] = emit_ln_group(b, block, srcs, dsts, smalls, st["sq8"],
                                        pre_wait_v=pwv, pre_wait_a=pwv)

        for k in range(4):
            for ic in range(8):
                idx = k * 8 + ic
                t = t4 * 4 + k

                @block.tensor
                def _(e, t4=t4, k=k, ic=ic, idx=idx, t=t):
                    if idx == 0:
                        e.wait_ge(sA.h, tok[("xn", t4)])
                    gidx = t4 * 32 + idx
                    if gidx >= 2:
                        e.wait_ge(sV.h, tok[("tpev", gidx - 2)])
                    instr = e.transpose(psb4[:, (gidx % 2) * 1024:(gidx % 2) * 1024 + 128],
                                        AB[:, XNB0 + k * 1024 + ic * 128: XNB0 + k * 1024 + (ic + 1) * 128],
                                        ident[:, :])
                    inc(instr, b.sT)
                    tok[("tp1", gidx)] = b.sT.c
                    if idx == 31:
                        tok[("tp", t4, 31)] = b.sT.c

                @block.vector
                def _(e, t4=t4, k=k, ic=ic, idx=idx, t=t):
                    gidx = t4 * 32 + idx
                    e.wait_ge(sT.h, tok[("tp1", gidx)])
                    instr = e.tensor_copy(
                        out=AB[:, XNT0 + ic * 2048 + t * 128: XNT0 + ic * 2048 + (t + 1) * 128],
                        in_=psb4[:, (gidx % 2) * 1024:(gidx % 2) * 1024 + 128])
                    inc(instr, b.sV)
                    tok[("tpev", gidx)] = b.sV.c

    @block.sync
    def _(e):
        for bias_name, col in (("bqb", 128), ("bkb", 136)):
            e.dma_start(out=smalls[:, col:col + 8].rearrange("p (c o) -> p c o", o=1),
                        in_=dT[bias_name].rearrange("(c p) one -> p c one", p=128),
                        allow_slow_non_contiguous=True).then_inc(qC4.h, 16)
            qC4.c += 16
        tok["bld"] = dma(e, qC4, AF[0:1, BCB:BCB + 1024], dT["bvb"][:, :])

    @block.gpsimd
    def _(e):
        e.wait_ge(qC4.h, tok["bld"])
        for p in (32, 64, 96):
            dma(e, qC4, AF[p:p + 1, BCB:BCB + 1024], AF[0:1, BCB:BCB + 1024])
        tok["bvcp"] = qC4.c

    @block.vector
    def _(e):
        e.wait_ge(qC4.h, tok["bvcp"])
        instr = e.stream_shuffle(out=AF[:, BCB:BCB + 1024], in_=AF[:, BCB:BCB + 1024],
                                 mask=[0] * 32)
        inc(instr, b.sV)
        tok["bvbc"] = b.sV.c

    def pj_w_load(pj, wname, first_wait):
        for hf in range(2):
            @block.sync
            def _(e, pj=pj, wname=wname, hf=hf, first_wait=first_wait):
                if hf == 1:
                    e.wait_ge(sV.h, tok[("wcast", pj, 0)])
                elif first_wait is not None:
                    e.wait_ge(sV.h, tok[first_wait])
                tok[("wld", pj, hf)] = dma(
                    e, qWb, AF[:, 0:4096].rearrange("p (c n) -> p c n", n=1024),
                    dT[wname].rearrange("(c p) n -> p c n", p=128)[:, hf * 4:(hf + 1) * 4, :])

            @block.vector
            def _(e, pj=pj, hf=hf):
                e.wait_ge(qWb.h, tok[("wld", pj, hf)])
                if hf == 0 and pj == "v":
                    e.wait_ge(sT.h, tok[("pj", "k", 7)])
                if hf == 0 and pj == "q":
                    e.wait_ge(sT.h, tok[("vpj", 15)])
                instr = e.tensor_copy(out=AB[:, WB0 + hf * 4096: WB0 + (hf + 1) * 4096],
                                      in_=AF[:, 0:4096])
                inc(instr, b.sV)
                tok[("wcast", pj, hf)] = b.sV.c

    def pj_qk(pj, bcol):
        for oc in range(8):
            @block.tensor
            def _(e, pj=pj, oc=oc):
                if oc == 0:
                    e.wait_ge(sV.h, tok[("wcast", pj, 1)])
                    if pj == "k":
                        e.wait_ge(sV.h, tok[("tpev", 127)])
                    if pj == "q":
                        e.wait_ge(sV.h, tok[("vev", 15)])
                prev = tok.get(("pjev", pj, oc - 1)) if oc > 0 else None
                if prev:
                    e.wait_ge(sA.h, prev)
                instr = None
                for tp in range(2):
                    for ic in range(8):
                        lhs = AB[:, WB0 + ic * 1024 + oc * 128: WB0 + ic * 1024 + (oc + 1) * 128]
                        for tc in (2 * tp, 2 * tp + 1):
                            instr = e.matmul(pPV[tc][:, :], lhs,
                                             AB[:, XNT0 + ic * 2048 + tc * 512: XNT0 + ic * 2048 + (tc + 1) * 512],
                                             start=(ic == 0), stop=(ic == 7))
                inc(instr, b.sT)
                tok[("pj", pj, oc)] = b.sT.c

            @block.scalar
            def _(e, pj=pj, oc=oc, bcol=bcol):
                e.wait_ge(sT.h, tok[("pj", pj, oc)])
                prev = tok.get(("pjdma", pj, oc - 1)) if oc > 0 else \
                       (tok.get(("vdma", 15)) if pj == "q" else None)
                if prev:
                    e.wait_ge(qPJ.h if pj != "q" or oc > 0 else qVW.h, prev)
                instr = None
                for tc in range(4):
                    instr = e.activation(out=AB[:, EV0 + tc * 512: EV0 + (tc + 1) * 512],
                                         in_=pPV[tc][:, :], func=AFT.Identity,
                                         bias=smalls[:, bcol + oc: bcol + oc + 1])
                inc(instr, b.sA)
                tok[("pjev", pj, oc)] = b.sA.c

            @block.sync
            def _(e, pj=pj, oc=oc):
                e.wait_ge(sA.h, tok[("pjev", pj, oc)])
                dst = st["qbD"] if pj == "q" else st["agk_in"].ap()
                tok[("pjdma", pj, oc)] = dma(e, qPJ, dst[oc * 128:(oc + 1) * 128, :],
                                             AB[:, EV0:EV0 + 2048])

    pj_w_load("k", "wkbT", None)
    pj_qk("k", 136)
    pj_w_load("v", "wvbT", None)
    for t in range(16):
        @block.tensor
        def _(e, t=t):
            if t == 0:
                e.wait_ge(sV.h, tok[("wcast", "v", 1)])
            prev = tok.get(("vev", t - 1)) if t > 0 else tok.get(("pjev", "k", 7))
            if prev:
                e.wait_ge(sV.h if t > 0 else sA.h, prev)
            instr = None
            for ic in range(8):
                lhs = AB[:, XNT0 + ic * 2048 + t * 128: XNT0 + ic * 2048 + (t + 1) * 128]
                e.matmul(pPV[0][:, :], lhs, AB[:, WB0 + ic * 1024: WB0 + ic * 1024 + 512],
                         start=(ic == 0), stop=(ic == 7))
                instr = e.matmul(pPV[1][:, :], lhs, AB[:, WB0 + ic * 1024 + 512: WB0 + (ic + 1) * 1024],
                                 start=(ic == 0), stop=(ic == 7))
            inc(instr, b.sT)
            tok[("vpj", t)] = b.sT.c

        @block.vector
        def _(e, t=t):
            e.wait_ge(sT.h, tok[("vpj", t)])
            e.wait_ge(sV.h, tok["bvbc"])
            if t > 0:
                e.wait_ge(qVW.h, tok[("vdma", t - 1)])
            for hf in range(2):
                instr = e.tensor_tensor(out=AB[:, EV0 + hf * 512: EV0 + (hf + 1) * 512],
                                        in0=pPV[hf][:, :],
                                        in1=AF[:, BCB + hf * 512: BCB + (hf + 1) * 512],
                                        op=OP.add)
            inc(instr, b.sV)
            tok[("vev", t)] = b.sV.c

        @block.gpsimd
        def _(e, t=t):
            e.wait_ge(sV.h, tok[("vev", t)])
            tok[("vdma", t)] = dma(e, qVW, st["agv_in"].ap()[t * 128:(t + 1) * 128, :],
                                   AB[:, EV0:EV0 + 1024])

    @block.gpsimd
    def _(e):
        e.wait_ge(qPJ.h, tok[("pjdma", "k", 7)])
        e.wait_ge(qVW.h, tok[("vdma", 15)])
        i1 = e.collective_compute("AllGather", OP.bypass,
                                  replica_groups=[[0, 1, 2, 3], [4, 5, 6, 7]],
                                  ins=[st["agk_in"].ap().opt()], outs=[st["agk_out"].ap().opt()])
        inc(i1, b.cc)
        i2 = e.collective_compute("AllGather", OP.bypass,
                                  replica_groups=[[0, 1, 2, 3], [4, 5, 6, 7]],
                                  ins=[st["agv_in"].ap().opt()], outs=[st["agv_out"].ap().opt()])
        inc(i2, b.cc)
        e.wait_ge(b.cc.h, b.cc.c)
        e.sem_inc(b.sG.h, 1)
        b.sG.c += 1
        tok["ag"] = b.sG.c

    pj_w_load("q", "wqbT", ("wcast", "v", 1))
    pj_qk("q", 128)


    agko = st["agk_out"].ap()
    agvo = st["agv_out"].ap()
    for c in range(8):
        sT0 = b.sT.c
        sA0 = b.sA.c
        for j in range(64):
            tok[("Sj", c, j)] = sT0 + j + 1
            tok[("sig", c, j)] = sA0 + j + 1
        tok[("Slast", c)] = sT0 + 64
        for vi in range(128):
            tok[("pvj", c, vi)] = sT0 + 64 + vi + 1
        tok[("pvdone", c)] = sT0 + 192

        @block.sync
        def _(e, c=c):
            e.wait_ge(b.sG.h, tok["ag"])
            if c == 0:
                e.wait_ge(qPJ.h, tok[("pjdma", "q", 7)])
            else:
                e.wait_ge(sT.h, tok[("Slast", c - 1)])
            qv = st["qbD"].rearrange("(ic p) n -> p ic n", p=128)
            tok[("qtld", c)] = dma(
                e, qQT, AB[:, QT0:QT0 + 2048].rearrange("p (ic n) -> p ic n", n=256),
                qv[:, :, c * 256:(c + 1) * 256])
            for j in range(64):
                if j >= 2:
                    e.wait_ge(sT.h, tok[("Sj", c, j - 2)])
                elif c > 0:
                    e.wait_ge(sT.h, tok[("Slast", c - 1)])
                r, lc = j // 16, (j % 16) * 128
                tok[("kld", c, j)] = dma(
                    e, qKr[j % 2],
                    AB[:, KR0 + (j % 2) * 1024: KR0 + (j % 2 + 1) * 1024].rearrange("p (x n) -> p x n", n=128),
                    agko[r * 1024:(r + 1) * 1024, lc:lc + 128].rearrange("(ic p) n -> p ic n", p=128))

        @block.tensor
        def _(e, c=c, sT0=sT0):
            e.wait_ge(qQT.h, tok[("qtld", c)])
            if c > 0:
                e.wait_ge(sT.h, tok[("pvdone", c - 1)])
            for j in range(64):
                e.wait_ge(qKr[j % 2].h, tok[("kld", c, j)])
                if j >= 2:
                    e.wait_ge(sA.h, tok[("sig", c, j - 2)])
                instr = None
                for ic in range(8):
                    instr = e.matmul(pS[j % 2][:, 0:256],
                                     AB[:, KR0 + (j % 2) * 1024 + ic * 128: KR0 + (j % 2) * 1024 + (ic + 1) * 128],
                                     AB[:, QT0 + ic * 256: QT0 + (ic + 1) * 256],
                                     start=(ic == 0), stop=(ic == 7))
                got = inc(instr, b.sT)
                assert got == tok[("Sj", c, j)], (got, tok[("Sj", c, j)])

        @block.scalar
        def _(e, c=c):
            if c > 0:
                e.wait_ge(sT.h, tok[("pvdone", c - 1)])
            for j in range(64):
                e.wait_ge(sT.h, tok[("Sj", c, j)])
                instr = e.activation(out=AB[:, WS0 + j * 256: WS0 + (j + 1) * 256],
                                     in_=pS[j % 2][:, 0:256], func=AFT.Sigmoid, scale=SCALE_B)
                got = inc(instr, b.sA)
                assert got == tok[("sig", c, j)], (got, tok[("sig", c, j)])

        for dh in range(2):
            @block.gpsimd
            def _(e, c=c, dh=dh):
                if dh == 0:
                    e.wait_ge(sA.h, tok[("sig", c, 63)])
                for j in range(64):
                    vi = dh * 64 + j
                    if vi >= 4:
                        e.wait_ge(sT.h, tok[("pvj", c, vi - 4)])
                    tok[("vld", c, vi)] = dma(
                        e, qVr[vi % 4], AB[:, VR0 + (vi % 4) * 512: VR0 + (vi % 4 + 1) * 512],
                        agvo[j * 128:(j + 1) * 128, dh * 512:(dh + 1) * 512])

            @block.tensor
            def _(e, c=c, dh=dh):
                if dh == 0:
                    e.wait_ge(sA.h, tok[("sig", c, 63)])
                if (c, dh) != (0, 0):
                    pc, pdh = (c, 0) if dh == 1 else (c - 1, 1)
                    e.wait_ge(sV.h, tok[("uadd", pc, pdh)])
                for j in range(64):
                    vi = dh * 64 + j
                    e.wait_ge(qVr[vi % 4].h, tok[("vld", c, vi)])
                    instr = None
                    for sub in range(2):
                        instr = e.matmul(pPV[sub][:, :],
                                         AB[:, WS0 + j * 256 + sub * 128: WS0 + j * 256 + (sub + 1) * 128],
                                         AB[:, VR0 + (vi % 4) * 512: VR0 + (vi % 4 + 1) * 512],
                                         start=(j == 0), stop=(j == 63))
                    got = inc(instr, b.sT)
                    assert got == tok[("pvj", c, vi)], (got, tok[("pvj", c, vi)])

            @block.vector
            def _(e, c=c, dh=dh):
                e.wait_ge(sT.h, tok[("pvj", c, dh * 64 + 63)])
                instr = None
                for sub in range(2):
                    t = c * 2 + sub
                    instr = e.tensor_tensor(
                        out=resid[:, t * 1024 + dh * 512: t * 1024 + (dh + 1) * 512],
                        in0=resid[:, t * 1024 + dh * 512: t * 1024 + (dh + 1) * 512],
                        in1=pPV[sub][:, :], op=OP.add)
                inc(instr, b.sV)
                tok[("uadd", c, dh)] = b.sV.c

    barrier(b, block)
    es.close()


def phase_out(b, st):
    block, resid = st["block"], st["resid"]

    qOut = b.chan("out")

    if st.get("upto", 6) < 2:
        @block.vector
        def _(e):
            e.memset(resid[:, :], 0.0)
            e.sem_inc(b.sV.h, 1)
            b.sV.c += 1
        barrier(b, block)

    @block.scalar
    def _(e):
        for t in range(16):
            dma(e, qOut, st["out_own"][t * 128:(t + 1) * 128, :],
                resid[:, t * 1024:(t + 1) * 1024])
        e.wait_ge(qOut.h, qOut.c)



def _stage_inputs(inputs):
    ins = {k: np.ascontiguousarray(np.asarray(v, dtype=np.float32)) for k, v in inputs.items()}
    x = ins["L0_attn_a_x"]

    import ml_dtypes
    identity = np.eye(128, dtype=ml_dtypes.bfloat16)
    masks = np.full((4, 128, 512), MASKVAL, np.float32)
    jj = np.arange(128)[:, None]
    ii = np.arange(512)[None, :]
    for d_idx in range(4):
        masks[d_idx][ii >= jj + 128 * d_idx] = 0.0
    masks = np.ascontiguousarray(masks.reshape(4 * 128, 512))

    wqT = np.ascontiguousarray(ins["L0_attn_a_Wq"].T)
    wkT = np.ascontiguousarray(ins["L0_attn_a_Wk"].T)
    wvT = np.ascontiguousarray(ins["L0_attn_a_Wv"].T)
    wfT = np.ascontiguousarray(ins["L0_attn_a_Wf"].T)
    woT = np.ascontiguousarray(ins["L0_attn_a_Wo"].T)
    wgT = np.ascontiguousarray(ins["L0_ffn_w_gate"].T)
    wuT = np.ascontiguousarray(ins["L0_ffn_w_up"].T)
    wdT = np.ascontiguousarray(ins["L0_ffn_w_down"].T)

    in_maps = []
    for c in range(8):
        bb, g = c // 4, c % 4
        hc = slice(g * HPC * DH, (g + 1) * HPC * DH)
        m = {
            "x_full": np.ascontiguousarray(x[bb]),
            "x_own": np.ascontiguousarray(x[bb, g * TOK:(g + 1) * TOK]),
            "wqaT": np.ascontiguousarray(wqT[:, hc]),
            "wkaT": np.ascontiguousarray(wkT[:, hc]),
            "wvaT": np.ascontiguousarray(wvT[:, hc]),
            "wfaT": np.ascontiguousarray(wfT[:, g * HPC:(g + 1) * HPC]),
            "bfa": np.ascontiguousarray(ins["L0_attn_a_bf"][g * HPC:(g + 1) * HPC, None]),
            "wogT": np.ascontiguousarray(woT[g * HPC * DH:(g + 1) * HPC * DH, :]),
            "wqbT": ins["L0_attn_b_Wq"], "wkbT": ins["L0_attn_b_Wk"], "wvbT": ins["L0_attn_b_Wv"],
            "bqb": np.ascontiguousarray(ins["L0_attn_b_bq"][:, None]),
            "bkb": np.ascontiguousarray(ins["L0_attn_b_bk"][:, None]),
            "bvb": np.ascontiguousarray(ins["L0_attn_b_bv"][None, :]),
            "wgT": wgT, "wuT": wuT, "wdT": wdT,
            "masks": masks, "ident": identity,
        }
        in_maps.append(m)
    return in_maps


def kernel(**inputs):
    nc = build_nc()
    in_maps = _stage_inputs(inputs)
    res = run_bass_kernel_spmd(nc, in_maps, core_ids=list(range(8)))
    out = np.empty((B, S, D), np.float32)
    for c in range(8):
        bb, g = c // 4, c % 4
        out[bb, g * TOK:(g + 1) * TOK] = res.results[c]["out_own"]
    return out


# version 09
# speedup vs baseline: 1.0706x; 1.0706x over previous
import contextlib
import numpy as np

import concourse.bass as bass
import concourse.mybir as mybir
from concourse.bass_utils import run_bass_kernel_spmd
from concourse.alu_op_type import AluOpType as OP

F32 = mybir.dt.float32
BF16 = mybir.dt.bfloat16
I32 = mybir.dt.int32
AFT = mybir.ActivationFunctionType
AX = mybir.AxisListType.X

B, S, D = 2, 8192, 1024
H, DH, HPC = 16, 64, 4
TOK = 2048
INNER = 4096
EPS = 1e-5
SCALE_A = 0.125
SCALE_B = 0.03125
MASKVAL = -1.0e7

UPTO = 6
TAPS = 0
P1SUB = 9
P1PARTS = 9
P1GRPS = 16


class Sem:
    def __init__(self, h):
        self.h = h
        self.c = 0


def inc(instr, s, n=1):
    instr.then_inc(s.h, n)
    s.c += n
    return s.c


def dma(e, q, out, in_):
    e.dma_start(out=out, in_=in_).then_inc(q.h, 16)
    q.c += 16
    return q.c


class Bld:
    def __init__(self, nc, es):
        self.nc = nc
        self.es = es
        self._sems = []
        self._chans = {}
        self.sT = self._mk("sT")
        self.sV = self._mk("sV")
        self.sA = self._mk("sA")
        self.sG = self._mk("sG")
        self.cc = self._mk("cc")

    def _mk(self, name):
        s = Sem(self.es.enter_context(self.nc.semaphore(name)))
        self._sems.append(s)
        return s

    def chan(self, name):
        if name not in self._chans:
            self._chans[name] = self._mk("q_" + name)
        return self._chans[name]

    def all_sems(self):
        return list(self._sems)


def vgap(e, smalls, n=2):
    for _ in range(n):
        e.tensor_copy(out=smalls[:, 254:255], in_=smalls[:, 253:254])


def barrier(b, block):
    snap = [(s, s.c) for s in b.all_sems() if s.c > 0]
    for en in ("sync", "vector", "scalar", "tensor", "gpsimd"):
        @getattr(block, en)
        def _(e, snap=snap):
            for s, v in snap:
                e.wait_ge(s.h, v)



def emit_ln_group(b, block, srcs, dsts, smalls, sq8, pre_wait_v=(), pre_wait_a=()):
    n = len(srcs)
    stats = {nm: smalls[:, i * 4:i * 4 + 4] for i, nm in
             enumerate(["s1", "s2", "mean", "vpe", "rstd", "nbias"])}
    col = lambda t, k: t[:, k:k + 1]

    @block.vector
    def _(e):
        for s, v in pre_wait_v:
            e.wait_ge(s.h, v)
        for k in range(n):
            e.reduce_sum(out=col(stats["s1"], k), in_=srcs[k], axis=AX)
        instr = None
        for k in range(n):
            instr = e.tensor_scalar(out=col(stats["mean"], k), in0=col(stats["s1"], k),
                                    scalar1=1.0 / D, scalar2=None, op0=OP.mult)
        inc(instr, b.sV)
    t_v1 = b.sV.c
    t_v2, t_v3 = t_v1 + 1, t_v1 + 2

    @block.scalar
    def _(e):
        for s, v in pre_wait_a:
            e.wait_ge(s.h, v)
        instr = None
        for k in range(n):
            instr = e.activation(out=sq8[:, k * 1024:(k + 1) * 1024], in_=srcs[k],
                                 func=AFT.Square, accum_out=col(stats["s2"], k))
        inc(instr, b.sA)
        t_s2 = b.sA.c
        e.wait_ge(b.sV.h, t_v2)
        instr = None
        for k in range(n):
            instr = e.activation(out=col(stats["s2"], k), in_=col(stats["vpe"], k), func=AFT.Sqrt)
        inc(instr, b.sA)
        e.wait_ge(b.sV.h, t_v3)
        instr = None
        for k in range(n):
            instr = e.activation(out=dsts[k], in_=srcs[k], func=AFT.Identity,
                                 scale=col(stats["rstd"], k), bias=col(stats["nbias"], k))
        inc(instr, b.sA)
    t_s2 = b.sA.c - 2
    t_sq = b.sA.c - 1

    @block.vector
    def _(e):
        e.wait_ge(b.sA.h, t_s2)
        for k in range(n):
            e.tensor_scalar(out=col(stats["vpe"], k), in0=col(stats["s2"], k),
                            scalar1=1.0 / D, scalar2=None, op0=OP.mult)
        for k in range(n):
            e.tensor_scalar(out=col(stats["s1"], k), in0=col(stats["mean"], k),
                            scalar1=col(stats["mean"], k), scalar2=None, op0=OP.mult)
        instr = None
        for k in range(n):
            instr = e.tensor_scalar(out=col(stats["vpe"], k), in0=col(stats["vpe"], k),
                                    scalar1=col(stats["s1"], k), scalar2=EPS,
                                    op0=OP.subtract, op1=OP.add)
        inc(instr, b.sV)
    assert b.sV.c == t_v2

    @block.vector
    def _(e):
        e.wait_ge(b.sA.h, t_sq)
        for k in range(n):
            e.reciprocal(out=col(stats["rstd"], k), in_=col(stats["s2"], k))
        instr = None
        for k in range(n):
            instr = e.tensor_scalar(out=col(stats["nbias"], k), in0=col(stats["mean"], k),
                                    scalar1=col(stats["rstd"], k), scalar2=-1.0,
                                    op0=OP.mult, op1=OP.mult)
        inc(instr, b.sV)
    assert b.sV.c == t_v3
    return b.sA.c



def build_nc(upto=None, taps=None):
    upto = UPTO if upto is None else upto
    taps = TAPS if taps is None else taps
    nc = bass.Bass("TRN2", target_bir_lowering=False, detect_race_conditions=False)

    dT = {}
    def din(name, shape, dt=F32):
        dT[name] = nc.dram_tensor(name, shape, dt, kind="ExternalInput").ap()
    din("x_full", [S, D]); din("x_own", [TOK, D])
    din("wqaT", [D, HPC * DH]); din("wkaT", [D, HPC * DH]); din("wvaT", [D, HPC * DH])
    din("wfaT", [D, HPC]); din("bfa", [HPC, 1]); din("wogT", [HPC * DH, D])
    din("wqbT", [D, D]); din("wkbT", [D, D]); din("wvbT", [D, D])
    din("bqb", [D, 1]); din("bkb", [D, 1]); din("bvb", [1, D])
    din("wgT", [D, INNER]); din("wuT", [D, INNER]); din("wdT", [INNER, D])
    din("masks", [4 * 128, 512]); din("ident", [128, 128], BF16)

    out_own = nc.dram_tensor("out_own", [TOK, D], F32, kind="ExternalOutput").ap()

    qa = nc.dram_tensor("qa_dram", [HPC * 72, S], BF16).ap()
    ka = nc.dram_tensor("ka_dram", [HPC * 72, S], BF16).ap()
    fD = nc.dram_tensor("f_dram", [HPC, S], F32).ap()
    rs_in = nc.dram_tensor("rs_in", [S, D], BF16)
    rs_out = nc.dram_tensor("rs_out", [TOK, D], BF16)
    wg_q = nc.dram_tensor("wg_q", [D, INNER], BF16).ap()
    wu_q = nc.dram_tensor("wu_q", [D, INNER], BF16).ap()
    wd_q = nc.dram_tensor("wd_q", [INNER, D], BF16).ap()
    hD = nc.dram_tensor("h_dram", [TOK, INNER], F32).ap()
    qbD = nc.dram_tensor("qb_dram", [D, TOK], BF16).ap()
    agk_in = nc.dram_tensor("agk_in", [D, TOK], BF16)
    agk_out = nc.dram_tensor("agk_out", [4 * D, TOK], BF16)
    agv_in = nc.dram_tensor("agv_in", [TOK, D], BF16)
    agv_out = nc.dram_tensor("agv_out", [S, D], BF16)

    tap = {}
    if taps:
        for p in (2, 3, 4):
            tap[p] = nc.dram_tensor(f"tap_resid{p}", [TOK, D], F32, kind="ExternalOutput").ap()
    if taps >= 9:
        tap["smalls"] = nc.dram_tensor("tap_smalls", [128, 256], F32, kind="ExternalOutput").ap()
        tap["h"] = nc.dram_tensor("tap_h", [TOK, INNER], F32, kind="ExternalOutput").ap()
        tap["wgq"] = nc.dram_tensor("tap_wgq", [INNER, D], BF16, kind="ExternalOutput").ap()
        tap["xqt"] = nc.dram_tensor("tap_xqt", [128, 4096], BF16, kind="ExternalOutput").ap()
        tap["rsin"] = nc.dram_tensor("tap_rsin", [S, D], F32, kind="ExternalOutput").ap()
        tap["f"] = nc.dram_tensor("tap_f", [HPC, S], F32, kind="ExternalOutput").ap()
        tap["qa"] = nc.dram_tensor("tap_qa", [HPC * 72, S], BF16, kind="ExternalOutput").ap()
        tap["ka"] = nc.dram_tensor("tap_ka", [HPC * 72, S], BF16, kind="ExternalOutput").ap()

    es = contextlib.ExitStack()
    with es:
        b = Bld(nc, es)
        sb = lambda n, sh, dt: es.enter_context(nc.sbuf_tensor(n, sh, dt))
        resid = sb("resid", [128, 16 * 1024], F32)
        AB = sb("AB", [128, 45312], BF16)
        AF = sb("AF", [128, 10240], F32)
        AI = sb("AI", [128, 1024], I32)
        sq8 = sb("sq8", [128, 4096], mybir.dt.int8)
        ident = sb("ident_s", [128, 128], BF16)
        smalls = sb("smalls", [128, 256], F32)

        st = dict(dT=dT, out_own=out_own, qa=qa, ka=ka, fD=fD,
                  rs_in=rs_in, rs_out=rs_out, wg_q=wg_q, wu_q=wu_q,
                  wd_q=wd_q, hD=hD, qbD=qbD, agk_in=agk_in, agk_out=agk_out,
                  agv_in=agv_in, agv_out=agv_out, resid=resid, AB=AB, AF=AF,
                  AI=AI, sq8=sq8, ident=ident, smalls=smalls, tap=tap, nc=nc, es=es)

        block = es.enter_context(nc.Block())
        st["block"] = block

        st["upto"] = upto
        phase1(b, st)
        if upto >= 2:
            phase2a(b, st)
        if upto >= 3:
            quant_weights(b, st)
        if "rsin" in tap:
            qT9 = b.chan("t9")

            @block.gpsimd
            def _(e):
                dma(e, qT9, tap["rsin"], rs_in.ap()[:, :])
                dma(e, qT9, tap["f"], fD[:, :])
                dma(e, qT9, tap["qa"], qa[:, :])
                dma(e, qT9, tap["ka"], ka[:, :])
                e.wait_ge(qT9.h, qT9.c)
            barrier(b, st["block"])
        if upto >= 2:
            phase2(b, st)
            emit_tap(b, st, 2)
        if upto >= 3:
            phase_ffn(b, st, first=True)
            if "smalls" in tap:
                qT8 = b.chan("t8")

                @block.gpsimd
                def _(e):
                    dma(e, qT8, tap["smalls"], smalls[:, :])
                    dma(e, qT8, tap["h"], st["hD"][:, :])
                    dma(e, qT8, tap["wgq"], st["wd_q"][:, :])
                    dma(e, qT8, tap["xqt"], AB[:, 32768:45056 + 0][:, 8192:8192 + 4096] if False else AB[:, 40960:45056])
                    e.wait_ge(qT8.h, qT8.c)
                barrier(b, st["block"])
            emit_tap(b, st, 3)
        if upto >= 4:
            phase4(b, st)
            emit_tap(b, st, 4)
        if upto >= 5:
            phase_ffn(b, st, first=False)
        phase_out(b, st)

    return nc


def emit_tap(b, st, p):
    if p not in st["tap"]:
        return
    block, resid = st["block"], st["resid"]
    barrier(b, st["block"])

    qTap = b.chan("tap")

    @block.scalar
    def _(e):
        for t in range(16):
            dma(e, qTap, st["tap"][p][t * 128:(t + 1) * 128, :],
                resid[:, t * 1024:(t + 1) * 1024])
    barrier(b, st["block"])



def phase1(b, st):
    nc, block = st["nc"], st["block"]
    es = contextlib.ExitStack()
    st["es"] = es

    dT, AB, AF, AI = st["dT"], st["AB"], st["AF"], st["AI"]
    smalls, ident = st["smalls"], st["ident"]
    qa, ka, fD = st["qa"], st["ka"], st["fD"]

    WQ0, WK0, WV0, WF0 = 0, 2048, 4096, 6144
    XNT0 = 6176
    XNB0 = XNT0 + 4096
    EV0 = XNB0 + 4096
    ONE0 = EV0 + 1024
    SPL0 = ONE0 + 8192
    VA0 = 28672
    VEND = VA0 + HPC * 64 * 65
    assert VEND <= 45312, VEND
    KB0 = 0
    QR0 = 8192
    PR0 = 9216
    UN0 = 11264
    UGT0 = 12288

    XS0 = 0
    FEV0 = 4096
    BC0 = 4608
    MK0 = 8192

    sT, sV, sA = b.sT, b.sV, b.sA
    qC = b.chan("c1"); qWa = b.chan("wa"); qX = b.chan("x")
    qKd = [b.chan("qkd0"), b.chan("qkd1")]; qFd = b.chan("fd")
    qOn = b.chan("ones"); qSp = b.chan("split"); qKb = b.chan("kb")
    qQs = [b.chan("qs0"), b.chan("qs1")]; qL = b.chan("lrow")
    qU = [b.chan("ugt0"), b.chan("ugt1")]
    qY = [b.chan("y0"), b.chan("y1")]

    pU = [es.enter_context(nc.psum_tensor(f"pU{i}", [65, 512], F32)) for i in range(2)]
    es_a = contextlib.ExitStack()
    pQK = [es_a.enter_context(nc.psum_tensor(f"pQK{i}", [128, 512], F32)) for i in range(2)]
    pV = es_a.enter_context(nc.psum_tensor("pV", [128, 512], F32))
    pF = es_a.enter_context(nc.psum_tensor("pF", [4, 512], F32))
    psb = es_a.enter_context(nc.psum_tensor("psb", [128, 2048], BF16))

    tok = {}

    @block.sync
    def _(e):
        dma(e, qC, ident[:, :], dT["ident"][:, :])
        dma(e, qC, AF[:, MK0:MK0 + 2048].rearrange("p (m n) -> p m n", n=512),
            dT["masks"].rearrange("(m p) n -> p m n", p=128))
        tok["consts"] = dma(e, qC, smalls[0:HPC, 96:97], dT["bfa"][:, :])

    for name, dst0, ncol in (("wqaT", WQ0, 256), ("wkaT", WK0, 256),
                             ("wvaT", WV0, 256), ("wfaT", WF0, 4)):
        @block.sync
        def _(e, name=name, ncol=ncol):
            prev = tok.get("wcast_prev")
            if prev:
                e.wait_ge(sV.h, prev)
            tok[name] = dma(e, qWa, AF[:, XS0:XS0 + 8 * ncol].rearrange("p (c n) -> p c n", n=ncol),
                            dT[name].rearrange("(c p) n -> p c n", p=128))

        @block.vector
        def _(e, name=name, dst0=dst0, ncol=ncol):
            e.wait_ge(qWa.h, tok[name])
            instr = e.tensor_copy(out=AB[:, dst0:dst0 + 8 * ncol], in_=AF[:, XS0:XS0 + 8 * ncol])
            inc(instr, b.sV)
            tok["wcast_prev"] = b.sV.c

    @block.vector
    def _(e):
        e.memset(smalls[:, 253:254], 0.0)
        e.memset(smalls[:, 254:255], 0.0)
        e.memset(AF[0:64, BC0:BC0 + 512], 0.0)
        e.memset(AB[0:1, ONE0:ONE0 + 8192], 1.0)
        vones = AB[:, VA0:VEND].rearrange("p (t c) -> p t c", c=65)
        instr = e.memset(vones[:, :, 64:65], 1.0)
        inc(instr, b.sV)
        tok["wcast"] = b.sV.c

    @block.gpsimd
    def _(e):
        e.wait_ge(sV.h, tok["wcast"])
        ka3 = ka.rearrange("(h rr) n -> h rr n", rr=72)
        qa3 = qa.rearrange("(h rr) n -> h rr n", rr=72)
        for r in range(3):
            for h in range(HPC):
                dma(e, qOn, ka3[h:h + 1, 64 + r, :], AB[0:1, ONE0:ONE0 + 8192])
                dma(e, qOn, qa3[h:h + 1, 67 + r, :], AB[0:1, ONE0:ONE0 + 8192])
        tok["onesrows"] = qOn.c

    NG = P1GRPS
    for grp in range(NG):
        t0 = grp * 512
        xs = [AF[:, XS0 + k * 1024: XS0 + (k + 1) * 1024] for k in range(4)]
        xnb = [AB[:, XNB0 + k * 1024: XNB0 + (k + 1) * 1024] for k in range(4)]
        xnt = lambda ic: AB[:, XNT0 + ic * 512: XNT0 + (ic + 1) * 512]

        @block.sync
        def _(e, grp=grp, t0=t0, xs=xs):
            if grp > 0:
                e.wait_ge(sA.h, tok[("xn", grp - 1)])
            else:
                e.wait_ge(sV.h, tok["wcast"])
            for k in range(4):
                dma(e, qX, xs[k], dT["x_full"][t0 + k * 128: t0 + (k + 1) * 128, :])
            tok[("x", grp)] = qX.c

        pwv = [(qX, tok[("x", grp)])]
        pwa = [(qX, tok[("x", grp)]), (qC, tok["consts"])]
        if grp == 0:
            pwa.append((sV, tok["wcast"]))
            pwv.append((sV, tok["wcast"]))
        elif P1PARTS >= 2:
            pwa.append((sT, tok[("tp", grp - 1)]))
        tok[("xn", grp)] = emit_ln_group(b, block, xs, xnb, smalls, st["sq8"],
                                         pre_wait_v=pwv, pre_wait_a=pwa)

        for k in range(4 if P1PARTS >= 2 else 0):
            for ic in range(8):
                idx = k * 8 + ic

                @block.tensor
                def _(e, grp=grp, k=k, ic=ic, idx=idx, xnb=xnb):
                    if idx == 0:
                        e.wait_ge(sA.h, tok[("xn", grp)])
                        if grp > 0:
                            e.wait_ge(sV.h, tok[("tpev", grp - 1, 31)])
                    if idx >= 2:
                        e.wait_ge(sV.h, tok[("tpev", grp, idx - 2)])
                    instr = e.transpose(psb[:, (idx % 2) * 1024:(idx % 2) * 1024 + 128],
                                        xnb[k][:, ic * 128:(ic + 1) * 128], ident[:, :])
                    inc(instr, b.sT)
                    tok[("tp1", grp, idx)] = b.sT.c
                    if idx == 31:
                        tok[("tp", grp)] = b.sT.c

                @block.vector
                def _(e, grp=grp, k=k, ic=ic, idx=idx):
                    e.wait_ge(sT.h, tok[("tp1", grp, idx)])
                    instr = e.tensor_copy(out=xnt(ic)[:, k * 128:(k + 1) * 128],
                                          in_=psb[:, (idx % 2) * 1024:(idx % 2) * 1024 + 128])
                    inc(instr, b.sV)
                    tok[("tpev", grp, idx)] = b.sV.c

        for proj, w0, scl, dram in ((("q", WQ0, SCALE_A, qa), ("k", WK0, 1.0, ka)) if P1PARTS >= 3 else ()):
            @block.tensor
            def _(e, grp=grp, proj=proj, w0=w0):
                if proj == "q":
                    e.wait_ge(sV.h, tok[("tpev", grp, 31)])
                for oc in range(2):
                    prev = tok.get(("qkev", grp, "q", oc)) if proj == "k" else \
                           (tok.get(("qkev", grp - 1, "k", oc)) if grp > 0 else None)
                    if prev:
                        e.wait_ge(sA.h, prev)
                    instr = None
                    for ic in range(8):
                        instr = e.matmul(pQK[oc][:, :],
                                         AB[:, w0 + ic * 256 + oc * 128: w0 + ic * 256 + (oc + 1) * 128],
                                         xnt(ic), start=(ic == 0), stop=(ic == 7))
                    inc(instr, b.sT)
                    tok[("qk", grp, proj, oc)] = b.sT.c

            @block.scalar
            def _(e, grp=grp, proj=proj, scl=scl):
                for oc in range(2):
                    e.wait_ge(sT.h, tok[("qk", grp, proj, oc)])
                    prev = tok.get(("qkdma", grp, "q", oc)) if proj == "k" else \
                           (tok.get(("qkdma", grp - 1, "k", oc)) if grp > 0 else None)
                    if prev:
                        e.wait_ge(qKd[oc].h, prev)
                    instr = e.activation(out=AB[:, EV0 + oc * 512: EV0 + (oc + 1) * 512],
                                         in_=pQK[oc][:, :], func=AFT.Identity, scale=scl)
                    inc(instr, b.sA)
                    tok[("qkev", grp, proj, oc)] = b.sA.c

            @block.gpsimd
            def _(e, grp=grp, proj=proj, dram=dram, t0=t0):
                for oc in range(2):
                    e.wait_ge(sA.h, tok[("qkev", grp, proj, oc)])
                    for hh in range(2):
                        h = oc * 2 + hh
                        dma(e, qKd[oc], dram[h * 72: h * 72 + 64, t0:t0 + 512],
                            AB[hh * 64:(hh + 1) * 64, EV0 + oc * 512: EV0 + (oc + 1) * 512])
                    tok[("qkdma", grp, proj, oc)] = qKd[oc].c

        for k in range(4 if P1PARTS >= 4 else 0):
            @block.tensor
            def _(e, grp=grp, k=k):
                prev = tok.get(("vev", grp, k - 1)) if k > 0 else \
                       (tok.get(("vev", grp - 1, 3)) if grp > 0 else None)
                if prev:
                    e.wait_ge(sV.h, prev)
                instr = None
                for ic in range(8):
                    instr = e.matmul(pV[:, 0:256], xnt(ic)[:, k * 128:(k + 1) * 128],
                                     AB[:, WV0 + ic * 256: WV0 + (ic + 1) * 256],
                                     start=(ic == 0), stop=(ic == 7))
                inc(instr, b.sT)
                tok[("v", grp, k)] = b.sT.c

            @block.vector
            def _(e, grp=grp, k=k):
                e.wait_ge(sT.h, tok[("v", grp, k)])
                kt = grp * 4 + k
                instr = None
                for h in range(HPC):
                    c0 = VA0 + (h * 64 + kt) * 65
                    instr = e.tensor_copy(out=AB[:, c0:c0 + 64], in_=pV[:, h * 64:(h + 1) * 64])
                inc(instr, b.sV)
                tok[("vev", grp, k)] = b.sV.c

        if P1PARTS < 5:
            continue

        @block.tensor
        def _(e, grp=grp):
            if grp > 0:
                e.wait_ge(sA.h, tok[("fev", grp - 1)])
            instr = None
            for ic in range(8):
                instr = e.matmul(pF[:, :], AB[:, WF0 + ic * 4: WF0 + (ic + 1) * 4],
                                 xnt(ic), start=(ic == 0), stop=(ic == 7))
            inc(instr, b.sT)
            tok[("f", grp)] = b.sT.c

        @block.scalar
        def _(e, grp=grp):
            e.wait_ge(sT.h, tok[("f", grp)])
            if grp > 0:
                e.wait_ge(qFd.h, tok[("fdma", grp - 1)])
            instr = e.activation(out=AF[0:4, FEV0:FEV0 + 512], in_=pF[:, :],
                                 func=AFT.Sigmoid, bias=smalls[0:4, 96:97], scale=1.0)
            inc(instr, b.sA)
            tok[("fev", grp)] = b.sA.c

        @block.gpsimd
        def _(e, grp=grp, t0=t0):
            e.wait_ge(sA.h, tok[("fev", grp)])
            tok[("fdma", grp)] = dma(e, qFd, fD[:, t0:t0 + 512], AF[0:4, FEV0:FEV0 + 512])

    barrier(b, block)
    if P1SUB < 2:
        es.close()
        return

    fr = AF[0:4, 0:2048]
    Cw = [AF[0:4, 2048:4096], AF[0:4, 4096:6144]]
    mw = AF[0:4, 6144:8192]
    hi = AB[0:4, 0:2048]
    mid = AB[0:4, 2048:4096]
    lo = AB[0:4, 4096:6144]

    for qtr in range(4):
        c0 = qtr * 2048
        Cr = Cw[qtr % 2]

        @block.sync
        def _(e, qtr=qtr, c0=c0):
            if qtr > 0:
                e.wait_ge(sV.h, tok[("fp2", qtr - 1)])
            tok[("fload", qtr)] = dma(e, qX, fr, fD[:, c0:c0 + 2048])

        @block.vector
        def _(e, qtr=qtr):
            e.wait_ge(qX.h, tok[("fload", qtr)])
            instr = e.tensor_scalar(out=fr, in0=fr, scalar1=1e-6, scalar2=None, op0=OP.add)
            inc(instr, b.sV)
            tok[("fp", qtr)] = b.sV.c

        @block.scalar
        def _(e, qtr=qtr):
            e.wait_ge(sV.h, tok[("fp", qtr)])
            instr = e.activation(out=fr, in_=fr, func=AFT.Ln)
            inc(instr, b.sA)
            tok[("lnf", qtr)] = b.sA.c

        @block.vector
        def _(e, qtr=qtr, Cr=Cr):
            e.wait_ge(sA.h, tok[("lnf", qtr)])
            if qtr == 0:
                e.memset(smalls[0:4, 100:101], 0.0)
                e.memset(smalls[0:4, 101:102], 0.0)
                init = smalls[0:4, 100:101]
            else:
                init = Cw[(qtr - 1) % 2][:, 2047:2048]
            if qtr >= 1:
                e.wait_ge(qSp.h, tok[("kadma", qtr - 1)])
            e.tensor_tensor_scan(out=Cr, data0=fr, data1=fr, initial=init,
                                 op0=OP.add, op1=OP.bypass)
            vgap(e, smalls)
            e.tensor_copy(out=hi, in_=Cr)
            vgap(e, smalls)
            e.tensor_copy(out=mw, in_=hi)
            vgap(e, smalls)
            e.tensor_tensor(out=fr, in0=Cr, in1=mw, op=OP.subtract)
            vgap(e, smalls)
            e.tensor_copy(out=mid, in_=fr)
            vgap(e, smalls)
            e.tensor_copy(out=mw, in_=mid)
            vgap(e, smalls)
            e.tensor_tensor(out=fr, in0=fr, in1=mw, op=OP.subtract)
            vgap(e, smalls)
            instr = e.tensor_copy(out=lo, in_=fr)
            inc(instr, b.sV)
            tok[("csplit", qtr)] = b.sV.c
            tok[("fp2", qtr)] = b.sV.c

        @block.gpsimd
        def _(e, qtr=qtr, c0=c0):
            e.wait_ge(sV.h, tok[("csplit", qtr)])
            qa3 = qa.rearrange("(h rr) n -> h rr n", rr=72)
            for r, t in ((0, hi), (1, mid), (2, lo)):
                dma(e, qSp, qa3[:, 64 + r, c0:c0 + 2048], t)
            tok[("qadma", qtr)] = qSp.c

        @block.vector
        def _(e, qtr=qtr):
            e.wait_ge(qSp.h, tok[("qadma", qtr)])
            instr = None
            for t in (hi, mid, lo):
                instr = e.tensor_scalar(out=t, in0=t, scalar1=-1.0, scalar2=None, op0=OP.mult)
            inc(instr, b.sV)
            tok[("nsplit", qtr)] = b.sV.c

        @block.gpsimd
        def _(e, qtr=qtr, c0=c0):
            e.wait_ge(sV.h, tok[("nsplit", qtr)])
            ka3 = ka.rearrange("(h rr) n -> h rr n", rr=72)
            for r, t in ((0, hi), (1, mid), (2, lo)):
                dma(e, qSp, ka3[:, 67 + r, c0:c0 + 2048], t)
            tok[("kadma", qtr)] = qSp.c

    barrier(b, block)
    es_a.close()
    pS2 = [es.enter_context(nc.psum_tensor(f"pS2{i}", [128, 1024], F32)) for i in range(2)]
    pY = [es.enter_context(nc.psum_tensor(f"pY{i}", [128, 512], F32)) for i in range(2)]
    if P1SUB < 3:
        es.close()
        return

    kslot = lambda kt: AB[:, KB0 + kt * 128: KB0 + (kt + 1) * 128]
    qslot = lambda it: AB[:, QR0 + (it % 2) * 512: QR0 + (it % 2 + 1) * 512]
    pslot = lambda gi: AB[:, PR0 + (gi % 4) * 512: PR0 + (gi % 4 + 1) * 512]
    unslot = lambda it: AB[0:64, UN0 + (it % 2) * 512: UN0 + (it % 2 + 1) * 512]

    seq = [(h, qc) for h in range(HPC) for qc in range(16)]
    sA0 = b.sA.c
    ptot = sum((4 * qc + 4) // 2 for h, qc in seq)
    for gp in range(ptot):
        tok[("expp", gp)] = sA0 + gp + 1
    pslot2 = lambda gp: AB[:, PR0 + (gp % 2) * 1024: PR0 + (gp % 2 + 1) * 1024]

    gbase = 0
    for it, (h, qc) in enumerate(seq):
        kmax = 4 * qc + 4
        pbase = gbase // 2

        @block.sync
        def _(e, it=it, h=h, qc=qc):
            if qc == 0:
                if it > 0:
                    e.wait_ge(sT.h, tok[("pvlast", it - 1)])
                tok[("kld", h)] = dma(e, qKb, AB[0:70, KB0:KB0 + 8192], ka[h * 72: h * 72 + 70, :])
            if it >= 2:
                e.wait_ge(sT.h, tok[("qfree", it - 2)])
            tok[("qld", it)] = dma(e, qQs[it % 2], qslot(it)[0:70, :],
                                   qa[h * 72: h * 72 + 70, qc * 512:(qc + 1) * 512])

        @block.tensor
        def _(e, it=it, h=h, qc=qc, kmax=kmax, gbase=gbase, pbase=pbase):
            e.wait_ge(qQs[it % 2].h, tok[("qld", it)])
            if qc == 0:
                e.wait_ge(qKb.h, tok[("kld", h)])
            for kp in range(kmax // 2):
                gp = pbase + kp
                if gp >= 2:
                    e.wait_ge(sA.h, tok[("expp", gp - 2)])
                instr = None
                for half in range(2):
                    kt = 2 * kp + half
                    instr = e.matmul(pS2[gp % 2][:, half * 512:(half + 1) * 512],
                                     kslot(kt)[0:70, :], qslot(it)[0:70, :],
                                     start=True, stop=True)
                inc(instr, b.sT)
                tok[("Sp", gp)] = b.sT.c
                if kp == kmax // 2 - 1:
                    tok[("qfree", it)] = b.sT.c
                if kp >= 1:
                    _pv2(e, b, tok, it, h, kp - 1, gbase, pbase, kmax, pU, AB, VA0, pslot2)
            _pv2(e, b, tok, it, h, kmax // 2 - 1, gbase, pbase, kmax, pU, AB, VA0, pslot2)
            tok[("pvlast", it)] = b.sT.c

        @block.vector
        def _(e, it=it, qc=qc, kmax=kmax, gbase=gbase, pbase=pbase):
            for kt in range(kmax - 4, kmax):
                gp = pbase + kt // 2
                d_idx = kt - (kmax - 4)
                half = kt % 2
                e.wait_ge(sT.h, tok[("Sp", gp)])
                instr = e.tensor_tensor(out=pS2[gp % 2][:, half * 512:(half + 1) * 512],
                                        in0=pS2[gp % 2][:, half * 512:(half + 1) * 512],
                                        in1=AF[:, MK0 + d_idx * 512: MK0 + (d_idx + 1) * 512],
                                        op=OP.add)
                inc(instr, b.sV)
                tok[("mask", gp, half)] = b.sV.c

        @block.scalar
        def _(e, it=it, qc=qc, kmax=kmax, gbase=gbase, pbase=pbase):
            for kp in range(kmax // 2):
                gp = pbase + kp
                diag = kp >= kmax // 2 - 2
                if diag:
                    e.wait_ge(sV.h, tok[("mask", gp, 1)])
                else:
                    e.wait_ge(sT.h, tok[("Sp", gp)])
                if gp >= 2:
                    e.wait_ge(sT.h, tok[("pv", 2 * (gp - 2) + 1)])
                instr = e.activation(out=pslot2(gp), in_=pS2[gp % 2][:, :], func=AFT.Exp)
                got = inc(instr, b.sA)
                assert got == tok[("expp", gp)], (got, tok[("expp", gp)])

        @block.vector
        def _(e, it=it):
            e.wait_ge(sT.h, tok[("pvlast", it)])
            if it >= 2:
                e.wait_ge(sV.h, tok[("unorm", it - 2)])
            instr = e.tensor_copy(out=AF[64:65, BC0:BC0 + 512], in_=pU[it % 2][64:65, :])
            inc(instr, b.sV)
            tok[("lcp", it)] = b.sV.c

        @block.gpsimd
        def _(e, it=it):
            e.wait_ge(sV.h, tok[("lcp", it)])
            dma(e, qL, AF[0:1, BC0:BC0 + 512], AF[64:65, BC0:BC0 + 512])
            dma(e, qL, AF[32:33, BC0:BC0 + 512], AF[64:65, BC0:BC0 + 512])
            tok[("lrow", it)] = qL.c

        @block.vector
        def _(e, it=it):
            e.wait_ge(qL.h, tok[("lrow", it)])
            e.reciprocal(out=AF[0:1, BC0:BC0 + 512], in_=AF[0:1, BC0:BC0 + 512])
            e.reciprocal(out=AF[32:33, BC0:BC0 + 512], in_=AF[32:33, BC0:BC0 + 512])
            vgap(e, smalls)
            e.stream_shuffle(out=AF[0:64, BC0:BC0 + 512], in_=AF[0:64, BC0:BC0 + 512],
                             mask=[0] * 32)
            if it >= 2:
                e.wait_ge(qU[it % 2].h, tok[("undma", it - 2)])
            vgap(e, smalls)
            instr = e.tensor_tensor(out=unslot(it), in0=pU[it % 2][0:64, :],
                                    in1=AF[0:64, BC0:BC0 + 512], op=OP.mult)
            inc(instr, b.sV)
            tok[("unorm", it)] = b.sV.c

        @block.gpsimd
        def _(e, it=it, h=h, qc=qc):
            e.wait_ge(sV.h, tok[("unorm", it)])
            c0 = UGT0 + (h // 2) * 8192 + qc * 512
            r0 = (h % 2) * 64
            tok[("undma", it)] = dma(e, qU[it % 2], AB[r0:r0 + 64, c0:c0 + 512], unslot(it))

        gbase += kmax

    barrier(b, block)
    if P1SUB < 4:
        es.close()
        return

    @block.sync
    def _(e):
        tok["wogld"] = dma(e, qWa, AF[:, XS0:XS0 + 2048].rearrange("p (c n) -> p c n", n=1024),
                           dT["wogT"].rearrange("(c p) n -> p c n", p=128))

    @block.vector
    def _(e):
        e.wait_ge(qWa.h, tok["wogld"])
        instr = e.tensor_copy(out=AB[:, QR0:QR0 + 2048], in_=AF[:, XS0:XS0 + 2048])
        inc(instr, b.sV)
        tok["wogcast"] = b.sV.c

    if P1SUB == 4:
        es.close()
        return
    for t in range(64 if P1SUB >= 6 else 1):
        @block.tensor
        def _(e, t=t):
            if t == 0:
                e.wait_ge(sV.h, tok["wogcast"])
            if t > 0:
                e.wait_ge(sV.h, tok[("yev", t - 1)])
            instr = None
            for half in range(2):
                for c2 in range(2):
                    lhs = AB[:, UGT0 + c2 * 8192 + t * 128: UGT0 + c2 * 8192 + (t + 1) * 128]
                    instr = e.matmul(pY[half][:, :], lhs,
                                     AB[:, QR0 + c2 * 1024 + half * 512: QR0 + c2 * 1024 + half * 512 + 512],
                                     start=(c2 == 0), stop=(c2 == 1))
            inc(instr, b.sT)
            tok[("yp", t)] = b.sT.c

        @block.vector
        def _(e, t=t):
            e.wait_ge(sT.h, tok[("yp", t)])
            if t >= 2:
                e.wait_ge(qY[t % 2].h, tok[("ydma", t - 2)])
            yb = AB[:, VA0 + (t % 2) * 1024: VA0 + (t % 2 + 1) * 1024]
            e.tensor_copy(out=yb[:, 0:512], in_=pY[0][:, :])
            instr = e.tensor_copy(out=yb[:, 512:1024], in_=pY[1][:, :])
            inc(instr, b.sV)
            tok[("yev", t)] = b.sV.c

        @block.gpsimd
        def _(e, t=t):
            e.wait_ge(sV.h, tok[("yev", t)])
            yb = AB[:, VA0 + (t % 2) * 1024: VA0 + (t % 2 + 1) * 1024]
            tok[("ydma", t)] = dma(e, qY[t % 2], st["rs_in"].ap()[t * 128:(t + 1) * 128, :], yb)

    barrier(b, block)
    es.close()


def _pv2(e, b, tok, it, h, kp, gbase, pbase, kmax, pU, AB, VA0, pslot2):
    gp = pbase + kp
    e.wait_ge(b.sA.h, tok[("expp", gp)])
    if kp == 0 and it >= 2:
        e.wait_ge(b.sV.h, tok[("unorm", it - 2)])
    for half in range(2):
        kt = 2 * kp + half
        c0 = VA0 + (h * 64 + kt) * 65
        instr = e.matmul(pU[it % 2][:, :], AB[:, c0:c0 + 65],
                         pslot2(gp)[:, half * 512:(half + 1) * 512],
                         start=(kt == 0), stop=(kt == kmax - 1))
        inc(instr, b.sT)
        tok[("pv", 2 * gp + half)] = b.sT.c



def phase2a(b, st):
    block = st["block"]

    @block.gpsimd
    def _(e):
        instr = e.collective_compute(
            "ReduceScatter", OP.add, replica_groups=[[0, 1, 2, 3], [4, 5, 6, 7]],
            ins=[st["rs_in"].ap().opt()], outs=[st["rs_out"].ap().opt()])
        inc(instr, b.cc)
        e.wait_ge(b.cc.h, b.cc.c)
        e.sem_inc(b.sG.h, 1)
        b.sG.c += 1
        st["tok_rs"] = b.sG.c


def phase2(b, st):
    nc, block = st["nc"], st["block"]
    dT, AF = st["dT"], st["AF"]
    resid, smalls = st["resid"], st["smalls"]
    sV = b.sV
    qP2 = [b.chan("p2a"), b.chan("p2b")]
    tok = {"rs": st["tok_rs"]}

    AB = st["AB"]
    for t in range(16):
        @block.sync
        def _(e, t=t):
            if t == 0:
                e.wait_ge(b.sG.h, tok["rs"])
            if t >= 2:
                e.wait_ge(sV.h, tok[("radd", t - 2)])
            s = (t % 2) * 2048
            dma(e, qP2[t % 2], AB[:, s:s + 1024], st["rs_out"].ap()[t * 128:(t + 1) * 128, :])
            dma(e, qP2[t % 2], AF[:, s:s + 1024], dT["x_own"][t * 128:(t + 1) * 128, :])
            tok[("ld", t)] = qP2[t % 2].c

        @block.vector
        def _(e, t=t):
            e.wait_ge(qP2[t % 2].h, tok[("ld", t)])
            s = (t % 2) * 2048
            e.tensor_copy(out=resid[:, t * 1024:(t + 1) * 1024], in_=AB[:, s:s + 1024])
            vgap(e, smalls)
            instr = e.tensor_tensor(out=resid[:, t * 1024:(t + 1) * 1024],
                                    in0=resid[:, t * 1024:(t + 1) * 1024],
                                    in1=AF[:, s:s + 1024], op=OP.add)
            inc(instr, b.sV)
            tok[("radd", t)] = b.sV.c

    barrier(b, block)




def quant_weights(b, st):
    nc, block, es0 = st["nc"], st["block"], None
    import contextlib as _ct
    es = _ct.ExitStack()
    dT, AB, AF, AI = st["dT"], st["AB"], st["AF"], st["AI"]
    smalls = st["smalls"]
    sT, sV, sA = b.sT, b.sV, b.sA
    qWq = [b.chan("wq0"), b.chan("wq1")]; qWst = b.chan("wst"); qPb = b.chan("pb")
    tok = {}
    GS0 = 32768
    OSC = {"g": 104, "u": 105, "d": 106}
    SW = {"g": 107, "u": 108, "d": 109}
    pPar = es.enter_context(nc.psum_tensor("pPar", [1, 8], F32))

    for wname, wdst, key in (("wgT", st["wg_q"], "g"), ("wuT", st["wu_q"], "u"),
                             ("wdT", st["wd_q"], "d")):
        wn = dT[wname].shape[1]
        wview = dT[wname].rearrange("(c p) n -> p c n", p=128)
        nview = wdst.rearrange("(c p) n -> p c n", p=128)
        npiece = 8
        cpp = wview.shape[1] // npiece
        pw = cpp * wn
        slot = lambda ch: AF[:, (ch % 2) * 4096:(ch % 2) * 4096 + 4096]

        for ch in range(npiece):
            @block.sync
            def _(e, ch=ch, wview=wview, pw=pw, cpp=cpp, wn=wn, key=key):
                if ch >= 2:
                    e.wait_ge(sA.h, tok[(key, "abs", ch - 2)])
                elif ch == 0 and key != "g":
                    e.wait_ge(sV.h, tok[({"u": "g", "d": "u"}[key], "clip", 7)])
                elif ch == 1 and key != "g":
                    e.wait_ge(sV.h, tok[({"u": "g", "d": "u"}[key], "clip", 7)])
                tok[(key, "wld", ch)] = dma(
                    e, qWq[ch % 2], slot(ch)[:, 0:pw].rearrange("p (c n) -> p c n", n=wn),
                    wview[:, ch * cpp:(ch + 1) * cpp, :])

            @block.scalar
            def _(e, ch=ch, pw=pw, key=key):
                e.wait_ge(qWq[ch % 2].h, tok[(key, "wld", ch)])
                if ch == 0 and key != "g":
                    e.wait_ge(sV.h, tok[({"u": "g", "d": "u"}[key], "rsum")])
                instr = None
                for cq in range(4):
                    instr = e.activation(out=st["sq8"][:, cq * 1024:(cq + 1) * 1024],
                                         in_=slot(ch)[:, cq * 1024:(cq + 1) * 1024],
                                         func=AFT.Abs,
                                         accum_out=smalls[:, 160 + ch * 4 + cq: 161 + ch * 4 + cq])
                inc(instr, b.sA)
                tok[(key, "abs", ch)] = b.sA.c

        @block.vector
        def _(e, key=key):
            e.wait_ge(sA.h, tok[(key, "abs", 7)])
            instr = e.reduce_sum(out=smalls[:, 126:127], in_=smalls[:, 160:192], axis=AX)
            inc(instr, b.sV)
            tok[(key, "rsum")] = b.sV.c

        @block.vector
        def _(e, key=key):
            if key == "g":
                e.memset(smalls[:, 252:253], 1.0)
            e.sem_inc(sV.h, 1)
            b.sV.c += 1
            tok[(key, "ones")] = b.sV.c

        @block.tensor
        def _(e, key=key):
            e.wait_ge(sV.h, tok[(key, "rsum")])
            e.wait_ge(sV.h, tok[(key, "ones")])
            if key != "g":
                e.wait_ge(sV.h, tok[({"u": "g", "d": "u"}[key], "scal")])
            instr = e.matmul(pPar[0:1, 0:1], smalls[:, 252:253], smalls[:, 126:127],
                             start=True, stop=True)
            inc(instr, b.sT)
            tok[(key, "par")] = b.sT.c

        @block.vector
        def _(e, key=key):
            e.wait_ge(sT.h, tok[(key, "par")])
            instr = e.tensor_copy(out=smalls[0:1, 127:128], in_=pPar[0:1, 0:1])
            inc(instr, b.sV)
            tok[(key, "parc")] = b.sV.c

        @block.sync
        def _(e, key=key):
            e.wait_ge(sV.h, tok[(key, "parc")])
            for p in (32, 64, 96):
                dma(e, qPb, smalls[p:p + 1, 127:128], smalls[0:1, 127:128])
            tok[(key, "parb")] = qPb.c

        @block.vector
        def _(e, key=key):
            e.wait_ge(qPb.h, tok[(key, "parb")])
            vgap(e, smalls)
            e.stream_shuffle(out=smalls[:, 127:128], in_=smalls[:, 127:128], mask=[0] * 32)
            vgap(e, smalls)
            e.tensor_scalar(out=smalls[:, OSC[key]:OSC[key] + 1], in0=smalls[:, 127:128],
                            scalar1=1.0 / (INNER * D), scalar2=1e-5, op0=OP.mult, op1=OP.max)
            vgap(e, smalls)
            instr = e.reciprocal(out=smalls[:, SW[key]:SW[key] + 1],
                                 in_=smalls[:, OSC[key]:OSC[key] + 1])
            inc(instr, b.sV)
            tok[(key, "scal")] = b.sV.c

        for ch in range(npiece):
            @block.sync
            def _(e, ch=ch, wview=wview, pw=pw, cpp=cpp, wn=wn, key=key):
                e.wait_ge(sV.h, tok[(key, "scal")])
                if ch >= 2:
                    e.wait_ge(sV.h, tok[(key, "clip", ch - 2)])
                tok[(key, "wld2", ch)] = dma(
                    e, qWq[ch % 2], slot(ch)[:, 0:pw].rearrange("p (c n) -> p c n", n=wn),
                    wview[:, ch * cpp:(ch + 1) * cpp, :])

            @block.vector
            def _(e, ch=ch, pw=pw, key=key):
                e.wait_ge(qWq[ch % 2].h, tok[(key, "wld2", ch)])
                if ch > 0:
                    e.wait_ge(qWst.h, tok[(key, "wst", ch - 1)])
                e.tensor_scalar(out=slot(ch)[:, 0:pw], in0=slot(ch)[:, 0:pw],
                                scalar1=smalls[:, SW[key]:SW[key] + 1], scalar2=None,
                                op0=OP.mult)
                instr = None
                vgap(e, smalls)
                for cq in range(4):
                    e.tensor_copy(out=AI[:, 0:1024], in_=slot(ch)[:, cq * 1024:(cq + 1) * 1024])
                    vgap(e, smalls)
                    instr = e.tensor_scalar(out=AB[0:128, GS0 + cq * 1024: GS0 + (cq + 1) * 1024],
                                            in0=AI[:, 0:1024], scalar1=-1.0, scalar2=1.0,
                                            op0=OP.max, op1=OP.min)
                    vgap(e, smalls)
                inc(instr, b.sV)
                tok[(key, "clip", ch)] = b.sV.c

            @block.sync
            def _(e, ch=ch, nview=nview, pw=pw, cpp=cpp, wn=wn, key=key):
                e.wait_ge(sV.h, tok[(key, "clip", ch)])
                tok[(key, "wst", ch)] = dma(
                    e, qWst, nview[:, ch * cpp:(ch + 1) * cpp, :],
                    AB[:, GS0:GS0 + pw].rearrange("p (c n) -> p c n", n=wn))

    barrier(b, block)
    es.close()




def phase_ffn(b, st, first):
    nc, block = st["nc"], st["block"]
    es = contextlib.ExitStack()
    dT, AB, AF, AI = st["dT"], st["AB"], st["AF"], st["AI"]
    resid, smalls = st["resid"], st["smalls"]
    sT, sV, sA = b.sT, b.sV, b.sA
    qWq = b.chan("wq"); qWst = b.chan("wst"); qGU = b.chan("gu")
    qH = [b.chan("h0"), b.chan("h1")]; qHL = b.chan("hl")
    tag = "f1" if first else "f2"
    tok = {}

    XQT0 = 0
    WG0 = 16384
    WU0 = 24576
    GS0 = 32768
    US0 = GS0 + 1024
    XQ0 = US0 + 1024
    WD0 = 0
    HQT0 = 32768
    HQ0 = 40960

    XN0 = 4096
    HB0 = 0

    AMX, S127, RS = 48, 64, 80
    OSC = {"g": 104, "u": 105, "d": 106}
    SW = {"g": 107, "u": 108, "d": 109}
    AMXH, S127H, RSH = 144, 160, 176

    psA = [es.enter_context(nc.psum_tensor(f"psA{tag}{i}", [128, 512], F32)) for i in range(4)]
    psb2 = es.enter_context(nc.psum_tensor(f"psb{tag}", [128, 2048], BF16))



    for t4 in range(4):
        srcs = [resid[:, (t4 * 4 + k) * 1024: (t4 * 4 + k + 1) * 1024] for k in range(4)]
        dsts = [AF[:, XN0 + k * 1024: XN0 + (k + 1) * 1024] for k in range(4)]
        pwv = []
        if t4 > 0:
            pwv.append((sT, tok[("xqtp", t4 - 1, 3)]))
        tok[("xn", t4)] = emit_ln_group(b, block, srcs, dsts, smalls, st["sq8"],
                                        pre_wait_v=pwv, pre_wait_a=pwv)

        @block.vector
        def _(e, t4=t4, dsts=dsts):
            e.wait_ge(sA.h, tok[("xn", t4)])
            for k in range(4):
                e.reduce_max(out=smalls[:, AMX + t4 * 4 + k: AMX + t4 * 4 + k + 1],
                             in_=dsts[k], axis=AX, apply_absolute_value=True)
            for k in range(4):
                c = t4 * 4 + k
                e.tensor_scalar(out=smalls[:, S127 + c: S127 + c + 1],
                                in0=smalls[:, AMX + c: AMX + c + 1],
                                scalar1=1e-5, scalar2=None, op0=OP.max)
            for k in range(4):
                c = t4 * 4 + k
                e.tensor_scalar(out=smalls[:, RS + c: RS + c + 1],
                                in0=smalls[:, S127 + c: S127 + c + 1],
                                scalar1=1.0 / 127.0, scalar2=None, op0=OP.mult)
            for k in range(4):
                c = t4 * 4 + k
                e.reciprocal(out=smalls[:, S127 + c: S127 + c + 1],
                             in_=smalls[:, S127 + c: S127 + c + 1])
            instr = None
            for k in range(4):
                c = t4 * 4 + k
                instr = e.tensor_scalar(out=smalls[:, S127 + c: S127 + c + 1],
                                        in0=smalls[:, S127 + c: S127 + c + 1],
                                        scalar1=127.0, scalar2=None, op0=OP.mult)
            inc(instr, b.sV)
            tok[("scx", t4)] = b.sV.c

        for k in range(4):
            t = t4 * 4 + k

            @block.vector
            def _(e, t4=t4, k=k, t=t, dsts=dsts):
                e.wait_ge(sV.h, tok[("scx", t4)])
                e.tensor_scalar(out=dsts[k], in0=dsts[k],
                                scalar1=smalls[:, S127 + t: S127 + t + 1], scalar2=None,
                                op0=OP.mult)
                if t > 0:
                    e.wait_ge(sA.h, tok[("xqa", t - 1)])
                vgap(e, smalls)
                instr = e.tensor_copy(out=AI[:, 0:1024], in_=dsts[k])
                inc(instr, b.sV)
                tok[("xqi", t)] = b.sV.c

            @block.scalar
            def _(e, t=t):
                e.wait_ge(sV.h, tok[("xqi", t)])
                if t >= 2:
                    e.wait_ge(sT.h, tok[("xqtpd", t - 2)])
                instr = e.activation(out=AB[:, XQ0 + (t % 2) * 1024: XQ0 + (t % 2 + 1) * 1024],
                                     in_=AI[:, 0:1024], func=AFT.Identity,
                                     scale=smalls[:, RS + t: RS + t + 1])
                inc(instr, b.sA)
                tok[("xqa", t)] = b.sA.c

            for ic in range(8):
                @block.tensor
                def _(e, t=t, ic=ic, t4=t4, k=k):
                    if ic == 0:
                        e.wait_ge(sA.h, tok[("xqa", t)])
                    idx = (t * 8 + ic)
                    if idx >= 2:
                        e.wait_ge(sV.h, tok[("xqev", idx - 2)])
                    instr = e.transpose(psb2[:, (idx % 2) * 1024:(idx % 2) * 1024 + 128],
                                        AB[:, XQ0 + (t % 2) * 1024 + ic * 128: XQ0 + (t % 2) * 1024 + (ic + 1) * 128],
                                        st["ident"][:, :])
                    inc(instr, b.sT)
                    tok[("xqtp1", idx)] = b.sT.c
                    if ic == 7:
                        tok[("xqtpd", t)] = b.sT.c
                        tok[("xqtp", t4, k)] = b.sT.c

                @block.vector
                def _(e, t=t, ic=ic):
                    idx = (t * 8 + ic)
                    e.wait_ge(sT.h, tok[("xqtp1", idx)])
                    instr = e.tensor_copy(
                        out=AB[:, XQT0 + ic * 2048 + t * 128: XQT0 + ic * 2048 + (t + 1) * 128],
                        in_=psb2[:, (idx % 2) * 1024:(idx % 2) * 1024 + 128])
                    inc(instr, b.sV)
                    tok[("xqev", idx)] = b.sV.c

    barrier(b, block)

    @block.vector
    def _(e):
        instr = e.memset(smalls[:, AMXH:AMXH + 16], 0.0)
        inc(instr, b.sV)
        tok["amxh0"] = b.sV.c

    gview = st["wg_q"].rearrange("(c p) n -> p c n", p=128)
    uview = st["wu_q"].rearrange("(c p) n -> p c n", p=128)
    for q4 in range(4):
        sT0, sA0, sV0 = b.sT.c, b.sA.c, b.sV.c
        qhb = [qH[0].c, qH[1].c]
        for t in range(16):
            tok[("gu", q4, t)] = sT0 + t + 1
            tok[("guev", q4, t)] = sA0 + t + 1
            tok[("hmul", q4, t)] = sV0 + t + 1
            tok[("hst", q4, t)] = qhb[t % 2] + 16 * (t // 2 + 1)

        @block.sync
        def _(e, q4=q4):
            if q4 > 0:
                e.wait_ge(sT.h, tok[("gu", q4 - 1, 15)])
            abg = AB[:, WG0:WG0 + 8192].rearrange("p (c n) -> p c n", n=1024)
            abu = AB[:, WU0:WU0 + 8192].rearrange("p (c n) -> p c n", n=1024)
            dma(e, qGU, abg, gview[:, :, q4 * 1024:(q4 + 1) * 1024])
            dma(e, qGU, abu, uview[:, :, q4 * 1024:(q4 + 1) * 1024])
            tok[("guld", q4)] = qGU.c

        @block.tensor
        def _(e, q4=q4):
            e.wait_ge(qGU.h, tok[("guld", q4)])
            if q4 == 0:
                e.wait_ge(sV.h, tok["amxh0"])
            for t in range(16):
                prev = tok[("guev", q4, t - 1)] if t > 0 else \
                       (tok.get(("guev", q4 - 1, 15)) if q4 > 0 else None)
                if prev:
                    e.wait_ge(sA.h, prev)
                instr = None
                for w0, p0 in ((WG0, 0), (WU0, 2)):
                    for ic in range(8):
                        lhs = AB[:, XQT0 + ic * 2048 + t * 128: XQT0 + ic * 2048 + (t + 1) * 128]
                        w = AB[:, w0 + ic * 1024: w0 + (ic + 1) * 1024]
                        st_, sp = (ic == 0), (ic == 7)
                        e.matmul(psA[p0][:, :], lhs, w[:, 0:512], start=st_, stop=sp)
                        instr = e.matmul(psA[p0 + 1][:, :], lhs, w[:, 512:1024], start=st_, stop=sp)
                got = inc(instr, b.sT)
                assert got == tok[("gu", q4, t)]

        @block.scalar
        def _(e, q4=q4):
            for t in range(16):
                e.wait_ge(sT.h, tok[("gu", q4, t)])
                prev = tok[("hmul", q4, t - 1)] if t > 0 else \
                       (tok.get(("hmul", q4 - 1, 15)) if q4 > 0 else None)
                if prev:
                    e.wait_ge(sV.h, prev)
                instr = None
                for hf in range(2):
                    e.activation(out=AB[:, GS0 + hf * 512: GS0 + (hf + 1) * 512],
                                 in_=psA[hf][:, :], func=AFT.Silu,
                                 scale=smalls[:, OSC["g"]:OSC["g"] + 1])
                    instr = e.activation(out=AB[:, US0 + hf * 512: US0 + (hf + 1) * 512],
                                         in_=psA[2 + hf][:, :], func=AFT.Identity,
                                         scale=smalls[:, OSC["u"]:OSC["u"] + 1])
                got = inc(instr, b.sA)
                assert got == tok[("guev", q4, t)]

        @block.vector
        def _(e, q4=q4):
            for t in range(16):
                e.wait_ge(sA.h, tok[("guev", q4, t)])
                if t >= 2:
                    e.wait_ge(qH[t % 2].h, tok[("hst", q4, t - 2)])
                hb = AF[:, HB0 + (t % 2) * 1024: HB0 + (t % 2 + 1) * 1024]
                e.tensor_tensor(out=hb, in0=AB[:, GS0:GS0 + 1024],
                                in1=AB[:, US0:US0 + 1024], op=OP.mult)
                vgap(e, smalls)
                e.reduce_max(out=smalls[:, 192:193], in_=hb, axis=AX,
                             apply_absolute_value=True)
                vgap(e, smalls)
                instr = e.tensor_scalar(out=smalls[:, AMXH + t: AMXH + t + 1],
                                        in0=smalls[:, AMXH + t: AMXH + t + 1],
                                        scalar1=smalls[:, 192:193], scalar2=None, op0=OP.max)
                got = inc(instr, b.sV)
                assert got == tok[("hmul", q4, t)]

        @block.gpsimd
        def _(e, q4=q4):
            for t in range(16):
                e.wait_ge(sV.h, tok[("hmul", q4, t)])
                hb = AF[:, HB0 + (t % 2) * 1024: HB0 + (t % 2 + 1) * 1024]
                got = dma(e, qH[t % 2], st["hD"][t * 128:(t + 1) * 128,
                                                 q4 * 1024:(q4 + 1) * 1024], hb)
                assert got == tok[("hst", q4, t)]

    barrier(b, block)

    @block.vector
    def _(e):
        e.tensor_scalar(out=smalls[:, 196:212], in0=smalls[:, AMXH:AMXH + 16],
                        scalar1=1e-5, scalar2=None, op0=OP.max)
        e.memset(smalls[:, 194:195], 0.0)
        e.memset(smalls[:, 195:196], 0.0)
        e.tensor_scalar(out=smalls[:, RSH:RSH + 16], in0=smalls[:, 196:212],
                        scalar1=1.0 / 127.0, scalar2=None, op0=OP.mult)
        e.reciprocal(out=smalls[:, S127H:S127H + 16], in_=smalls[:, 196:212])
        e.memset(smalls[:, 194:195], 0.0)
        e.memset(smalls[:, 195:196], 0.0)
        instr = e.tensor_scalar(out=smalls[:, S127H:S127H + 16], in0=smalls[:, S127H:S127H + 16],
                                scalar1=127.0, scalar2=None, op0=OP.mult)
        inc(instr, b.sV)
        tok["sch"] = b.sV.c

    dview = st["wd_q"].rearrange("(c p) n -> p c n", p=128)

    @block.sync
    def _(e):
        abd = AB[:, WD0:WD0 + 32768].rearrange("p (c n) -> p c n", n=1024)
        dma(e, qGU, abd, dview[:, :, :])
        tok["wdld"] = qGU.c

    for t in range(16):
        @block.sync
        def _(e, t=t):
            if t > 0:
                e.wait_ge(sV.h, tok[("hqi", t - 1, 3)])
            tok[("hld", t)] = dma(e, qHL, AF[:, 2048:2048 + 4096], st["hD"][t * 128:(t + 1) * 128, :])

        sAb = b.sA.c
        for cq in range(4):
            tok[("hqa", t, cq)] = sAb + cq + 1

        @block.vector
        def _(e, t=t):
            e.wait_ge(qHL.h, tok[("hld", t)])
            if t == 0:
                e.wait_ge(sV.h, tok["sch"])
            hl = AF[:, 2048:2048 + 4096]
            e.tensor_scalar(out=hl, in0=hl, scalar1=smalls[:, S127H + t: S127H + t + 1],
                            scalar2=None, op0=OP.mult)
            if t > 0:
                e.wait_ge(sA.h, tok[("hqa", t - 1, 3)])
                e.wait_ge(sT.h, tok[("hqtp", (t - 1) * 32 + 31)])
            vgap(e, smalls)
            for cq in range(4):
                if cq > 0:
                    e.wait_ge(sA.h, tok[("hqa", t, cq - 1)])
                instr = e.tensor_copy(out=AI[:, 0:1024], in_=hl[:, cq * 1024:(cq + 1) * 1024])
                inc(instr, b.sV)
                tok[("hqi", t, cq)] = b.sV.c
            tok[("hq_y", t)] = b.sV.c

        @block.scalar
        def _(e, t=t):
            for cq in range(4):
                e.wait_ge(sV.h, tok[("hqi", t, cq)])
                instr = e.activation(out=AB[:, HQ0 + cq * 1024: HQ0 + (cq + 1) * 1024],
                                     in_=AI[:, 0:1024], func=AFT.Identity,
                                     scale=smalls[:, RSH + t: RSH + t + 1])
                got = inc(instr, b.sA)
                assert got == tok[("hqa", t, cq)]
        for i2 in range(32):
            @block.tensor
            def _(e, t=t, i2=i2):
                if i2 % 8 == 0:
                    e.wait_ge(sA.h, tok[("hqa", t, i2 // 8)])
                idx = t * 32 + i2
                if idx >= 2:
                    e.wait_ge(sV.h, tok[("hqev", idx - 2)])
                instr = e.transpose(psb2[:, (idx % 2) * 1024:(idx % 2) * 1024 + 128],
                                    AB[:, HQ0 + i2 * 128: HQ0 + (i2 + 1) * 128],
                                    st["ident"][:, :])
                inc(instr, b.sT)
                tok[("hqtp", idx)] = b.sT.c

            @block.vector
            def _(e, t=t, i2=i2):
                idx = t * 32 + i2
                e.wait_ge(sT.h, tok[("hqtp", idx)])
                if t >= 2 and i2 == 0:
                    e.wait_ge(sT.h, tok[("down", t - 2)])
                instr = e.tensor_copy(out=AB[:, HQT0 + (t % 2) * 4096 + i2 * 128:
                                              HQT0 + (t % 2) * 4096 + (i2 + 1) * 128],
                                      in_=psb2[:, (idx % 2) * 1024:(idx % 2) * 1024 + 128])
                inc(instr, b.sV)
                tok[("hqev", idx)] = b.sV.c

        @block.tensor
        def _(e, t=t):
            e.wait_ge(sV.h, tok[("hqev", t * 32 + 31)])
            if t == 0:
                e.wait_ge(qGU.h, tok["wdld"])
            if t >= 2:
                e.wait_ge(sV.h, tok[("dadd", t - 2)])
            instr = None
            for i2 in range(32):
                lhs = AB[:, HQT0 + (t % 2) * 4096 + i2 * 128: HQT0 + (t % 2) * 4096 + (i2 + 1) * 128]
                wd = AB[:, WD0 + i2 * 1024: WD0 + (i2 + 1) * 1024]
                e.matmul(psA[2 * (t % 2)][:, :], lhs, wd[:, 0:512],
                         start=(i2 == 0), stop=(i2 == 31))
                instr = e.matmul(psA[2 * (t % 2) + 1][:, :], lhs, wd[:, 512:1024],
                                 start=(i2 == 0), stop=(i2 == 31))
            inc(instr, b.sT)
            tok[("down", t)] = b.sT.c

        @block.vector
        def _(e, t=t):
            e.wait_ge(sT.h, tok[("down", t)])
            instr = None
            for hf in range(2):
                instr = e.scalar_tensor_tensor(
                    out=resid[:, t * 1024 + hf * 512: t * 1024 + (hf + 1) * 512],
                    in0=psA[2 * (t % 2) + hf][:, :],
                    scalar=smalls[:, OSC["d"]:OSC["d"] + 1],
                    in1=resid[:, t * 1024 + hf * 512: t * 1024 + (hf + 1) * 512],
                    op0=OP.mult, op1=OP.add)
            inc(instr, b.sV)
            tok[("dadd", t)] = b.sV.c

    barrier(b, block)
    es.close()



def phase4(b, st):
    nc, block = st["nc"], st["block"]
    es = contextlib.ExitStack()
    dT, AB, AF, AI = st["dT"], st["AB"], st["AF"], st["AI"]
    resid, smalls, ident = st["resid"], st["smalls"], st["ident"]
    sT, sV, sA = b.sT, b.sV, b.sA
    qC4 = b.chan("c4"); qWb = b.chan("wb"); qPJ = b.chan("pj")
    qVW = b.chan("vw"); qQT = b.chan("qt")
    qKr = [b.chan("k40"), b.chan("k41")]
    qVr = [b.chan(f"v4{i}") for i in range(4)]
    tok = {}

    XNT0 = 0
    WB0 = 16384
    EV0 = 24576
    XNB0 = 26624
    WS0 = 0
    QT0 = 32768
    KR0 = 36864
    VR0 = 38912
    BCB = 6144
    XN0 = 4096

    pS = [es.enter_context(nc.psum_tensor(f"p4s{i}", [128, 512], F32)) for i in range(2)]
    pPV = [es.enter_context(nc.psum_tensor(f"p4v{i}", [128, 512], F32)) for i in range(4)]
    psb4 = es.enter_context(nc.psum_tensor("psb4", [128, 2048], BF16))

    for t4 in range(4):
        srcs = [resid[:, (t4 * 4 + k) * 1024: (t4 * 4 + k + 1) * 1024] for k in range(4)]
        dsts = [AB[:, XNB0 + k * 1024: XNB0 + (k + 1) * 1024] for k in range(4)]
        pwv = []
        if t4 > 0:
            pwv.append((sT, tok[("tp", t4 - 1, 31)]))
        tok[("xn", t4)] = emit_ln_group(b, block, srcs, dsts, smalls, st["sq8"],
                                        pre_wait_v=pwv, pre_wait_a=pwv)

        for k in range(4):
            for ic in range(8):
                idx = k * 8 + ic
                t = t4 * 4 + k

                @block.tensor
                def _(e, t4=t4, k=k, ic=ic, idx=idx, t=t):
                    if idx == 0:
                        e.wait_ge(sA.h, tok[("xn", t4)])
                    gidx = t4 * 32 + idx
                    if gidx >= 2:
                        e.wait_ge(sV.h, tok[("tpev", gidx - 2)])
                    instr = e.transpose(psb4[:, (gidx % 2) * 1024:(gidx % 2) * 1024 + 128],
                                        AB[:, XNB0 + k * 1024 + ic * 128: XNB0 + k * 1024 + (ic + 1) * 128],
                                        ident[:, :])
                    inc(instr, b.sT)
                    tok[("tp1", gidx)] = b.sT.c
                    if idx == 31:
                        tok[("tp", t4, 31)] = b.sT.c

                @block.vector
                def _(e, t4=t4, k=k, ic=ic, idx=idx, t=t):
                    gidx = t4 * 32 + idx
                    e.wait_ge(sT.h, tok[("tp1", gidx)])
                    instr = e.tensor_copy(
                        out=AB[:, XNT0 + ic * 2048 + t * 128: XNT0 + ic * 2048 + (t + 1) * 128],
                        in_=psb4[:, (gidx % 2) * 1024:(gidx % 2) * 1024 + 128])
                    inc(instr, b.sV)
                    tok[("tpev", gidx)] = b.sV.c

    @block.sync
    def _(e):
        for bias_name, col in (("bqb", 128), ("bkb", 136)):
            e.dma_start(out=smalls[:, col:col + 8].rearrange("p (c o) -> p c o", o=1),
                        in_=dT[bias_name].rearrange("(c p) one -> p c one", p=128),
                        allow_slow_non_contiguous=True).then_inc(qC4.h, 16)
            qC4.c += 16
        tok["bld"] = dma(e, qC4, AF[0:1, BCB:BCB + 1024], dT["bvb"][:, :])

    @block.gpsimd
    def _(e):
        e.wait_ge(qC4.h, tok["bld"])
        for p in (32, 64, 96):
            dma(e, qC4, AF[p:p + 1, BCB:BCB + 1024], AF[0:1, BCB:BCB + 1024])
        tok["bvcp"] = qC4.c

    @block.vector
    def _(e):
        e.wait_ge(qC4.h, tok["bvcp"])
        instr = e.stream_shuffle(out=AF[:, BCB:BCB + 1024], in_=AF[:, BCB:BCB + 1024],
                                 mask=[0] * 32)
        inc(instr, b.sV)
        tok["bvbc"] = b.sV.c

    def pj_w_load(pj, wname, first_wait):
        for hf in range(2):
            @block.sync
            def _(e, pj=pj, wname=wname, hf=hf, first_wait=first_wait):
                if hf == 1:
                    e.wait_ge(sV.h, tok[("wcast", pj, 0)])
                elif first_wait is not None:
                    e.wait_ge(sV.h, tok[first_wait])
                tok[("wld", pj, hf)] = dma(
                    e, qWb, AF[:, 0:4096].rearrange("p (c n) -> p c n", n=1024),
                    dT[wname].rearrange("(c p) n -> p c n", p=128)[:, hf * 4:(hf + 1) * 4, :])

            @block.vector
            def _(e, pj=pj, hf=hf):
                e.wait_ge(qWb.h, tok[("wld", pj, hf)])
                if hf == 0 and pj == "v":
                    e.wait_ge(sT.h, tok[("pj", "k", 7)])
                if hf == 0 and pj == "q":
                    e.wait_ge(sT.h, tok[("vpj", 15)])
                instr = e.tensor_copy(out=AB[:, WB0 + hf * 4096: WB0 + (hf + 1) * 4096],
                                      in_=AF[:, 0:4096])
                inc(instr, b.sV)
                tok[("wcast", pj, hf)] = b.sV.c

    def pj_qk(pj, bcol):
        for oc in range(8):
            @block.tensor
            def _(e, pj=pj, oc=oc):
                if oc == 0:
                    e.wait_ge(sV.h, tok[("wcast", pj, 1)])
                    if pj == "k":
                        e.wait_ge(sV.h, tok[("tpev", 127)])
                    if pj == "q":
                        e.wait_ge(sV.h, tok[("vev", 15)])
                prev = tok.get(("pjev", pj, oc - 1)) if oc > 0 else None
                if prev:
                    e.wait_ge(sA.h, prev)
                instr = None
                for tp in range(2):
                    for ic in range(8):
                        lhs = AB[:, WB0 + ic * 1024 + oc * 128: WB0 + ic * 1024 + (oc + 1) * 128]
                        for tc in (2 * tp, 2 * tp + 1):
                            instr = e.matmul(pPV[tc][:, :], lhs,
                                             AB[:, XNT0 + ic * 2048 + tc * 512: XNT0 + ic * 2048 + (tc + 1) * 512],
                                             start=(ic == 0), stop=(ic == 7))
                inc(instr, b.sT)
                tok[("pj", pj, oc)] = b.sT.c

            @block.scalar
            def _(e, pj=pj, oc=oc, bcol=bcol):
                e.wait_ge(sT.h, tok[("pj", pj, oc)])
                prev = tok.get(("pjdma", pj, oc - 1)) if oc > 0 else \
                       (tok.get(("vdma", 15)) if pj == "q" else None)
                if prev:
                    e.wait_ge(qPJ.h if pj != "q" or oc > 0 else qVW.h, prev)
                instr = None
                for tc in range(4):
                    instr = e.activation(out=AB[:, EV0 + tc * 512: EV0 + (tc + 1) * 512],
                                         in_=pPV[tc][:, :], func=AFT.Identity,
                                         bias=smalls[:, bcol + oc: bcol + oc + 1])
                inc(instr, b.sA)
                tok[("pjev", pj, oc)] = b.sA.c

            @block.sync
            def _(e, pj=pj, oc=oc):
                e.wait_ge(sA.h, tok[("pjev", pj, oc)])
                dst = st["qbD"] if pj == "q" else st["agk_in"].ap()
                tok[("pjdma", pj, oc)] = dma(e, qPJ, dst[oc * 128:(oc + 1) * 128, :],
                                             AB[:, EV0:EV0 + 2048])

    pj_w_load("k", "wkbT", None)
    pj_qk("k", 136)
    pj_w_load("v", "wvbT", None)
    for t in range(16):
        @block.tensor
        def _(e, t=t):
            if t == 0:
                e.wait_ge(sV.h, tok[("wcast", "v", 1)])
            prev = tok.get(("vev", t - 1)) if t > 0 else tok.get(("pjev", "k", 7))
            if prev:
                e.wait_ge(sV.h if t > 0 else sA.h, prev)
            instr = None
            for ic in range(8):
                lhs = AB[:, XNT0 + ic * 2048 + t * 128: XNT0 + ic * 2048 + (t + 1) * 128]
                e.matmul(pPV[0][:, :], lhs, AB[:, WB0 + ic * 1024: WB0 + ic * 1024 + 512],
                         start=(ic == 0), stop=(ic == 7))
                instr = e.matmul(pPV[1][:, :], lhs, AB[:, WB0 + ic * 1024 + 512: WB0 + (ic + 1) * 1024],
                                 start=(ic == 0), stop=(ic == 7))
            inc(instr, b.sT)
            tok[("vpj", t)] = b.sT.c

        @block.vector
        def _(e, t=t):
            e.wait_ge(sT.h, tok[("vpj", t)])
            e.wait_ge(sV.h, tok["bvbc"])
            if t > 0:
                e.wait_ge(qVW.h, tok[("vdma", t - 1)])
            for hf in range(2):
                instr = e.tensor_tensor(out=AB[:, EV0 + hf * 512: EV0 + (hf + 1) * 512],
                                        in0=pPV[hf][:, :],
                                        in1=AF[:, BCB + hf * 512: BCB + (hf + 1) * 512],
                                        op=OP.add)
            inc(instr, b.sV)
            tok[("vev", t)] = b.sV.c

        @block.gpsimd
        def _(e, t=t):
            e.wait_ge(sV.h, tok[("vev", t)])
            tok[("vdma", t)] = dma(e, qVW, st["agv_in"].ap()[t * 128:(t + 1) * 128, :],
                                   AB[:, EV0:EV0 + 1024])

    @block.gpsimd
    def _(e):
        e.wait_ge(qPJ.h, tok[("pjdma", "k", 7)])
        e.wait_ge(qVW.h, tok[("vdma", 15)])
        i1 = e.collective_compute("AllGather", OP.bypass,
                                  replica_groups=[[0, 1, 2, 3], [4, 5, 6, 7]],
                                  ins=[st["agk_in"].ap().opt()], outs=[st["agk_out"].ap().opt()])
        inc(i1, b.cc)
        i2 = e.collective_compute("AllGather", OP.bypass,
                                  replica_groups=[[0, 1, 2, 3], [4, 5, 6, 7]],
                                  ins=[st["agv_in"].ap().opt()], outs=[st["agv_out"].ap().opt()])
        inc(i2, b.cc)
        e.wait_ge(b.cc.h, b.cc.c)
        e.sem_inc(b.sG.h, 1)
        b.sG.c += 1
        tok["ag"] = b.sG.c

    pj_w_load("q", "wqbT", ("wcast", "v", 1))
    pj_qk("q", 128)


    agko = st["agk_out"].ap()
    agvo = st["agv_out"].ap()
    for c in range(4):
        sT0 = b.sT.c
        sA0 = b.sA.c
        for j in range(64):
            tok[("Sj", c, j)] = sT0 + j + 1
            tok[("sig", c, j)] = sA0 + j + 1
        tok[("Slast", c)] = sT0 + 64
        for vi in range(256):
            tok[("pvj", c, vi)] = sT0 + 64 + vi + 1
        tok[("pvdone", c)] = sT0 + 320

        @block.sync
        def _(e, c=c):
            e.wait_ge(b.sG.h, tok["ag"])
            if c == 0:
                e.wait_ge(qPJ.h, tok[("pjdma", "q", 7)])
            else:
                e.wait_ge(sT.h, tok[("Slast", c - 1)])
            qv = st["qbD"].rearrange("(ic p) n -> p ic n", p=128)
            tok[("qtld", c)] = dma(
                e, qQT, AB[:, QT0:QT0 + 4096].rearrange("p (ic n) -> p ic n", n=512),
                qv[:, :, c * 512:(c + 1) * 512])
            for j in range(64):
                if j >= 2:
                    e.wait_ge(sT.h, tok[("Sj", c, j - 2)])
                elif c > 0:
                    e.wait_ge(sT.h, tok[("Slast", c - 1)])
                r, lc = j // 16, (j % 16) * 128
                tok[("kld", c, j)] = dma(
                    e, qKr[j % 2],
                    AB[:, KR0 + (j % 2) * 1024: KR0 + (j % 2 + 1) * 1024].rearrange("p (x n) -> p x n", n=128),
                    agko[r * 1024:(r + 1) * 1024, lc:lc + 128].rearrange("(ic p) n -> p ic n", p=128))

        @block.tensor
        def _(e, c=c):
            e.wait_ge(qQT.h, tok[("qtld", c)])
            if c > 0:
                e.wait_ge(sT.h, tok[("pvdone", c - 1)])
            for j in range(64):
                e.wait_ge(qKr[j % 2].h, tok[("kld", c, j)])
                if j >= 2:
                    e.wait_ge(sA.h, tok[("sig", c, j - 2)])
                instr = None
                for ic in range(8):
                    instr = e.matmul(pS[j % 2][:, :],
                                     AB[:, KR0 + (j % 2) * 1024 + ic * 128: KR0 + (j % 2) * 1024 + (ic + 1) * 128],
                                     AB[:, QT0 + ic * 512: QT0 + (ic + 1) * 512],
                                     start=(ic == 0), stop=(ic == 7))
                got = inc(instr, b.sT)
                assert got == tok[("Sj", c, j)], (got, tok[("Sj", c, j)])

        @block.scalar
        def _(e, c=c):
            if c > 0:
                e.wait_ge(sT.h, tok[("pvdone", c - 1)])
            for j in range(64):
                e.wait_ge(sT.h, tok[("Sj", c, j)])
                instr = e.activation(out=AB[:, WS0 + j * 512: WS0 + (j + 1) * 512],
                                     in_=pS[j % 2][:, :], func=AFT.Sigmoid, scale=SCALE_B)
                got = inc(instr, b.sA)
                assert got == tok[("sig", c, j)], (got, tok[("sig", c, j)])

        for dh in range(2):
            for sp in range(2):
                @block.gpsimd
                def _(e, c=c, dh=dh, sp=sp):
                    if dh == 0 and sp == 0:
                        e.wait_ge(sA.h, tok[("sig", c, 63)])
                    for j in range(64):
                        vi = (dh * 2 + sp) * 64 + j
                        if vi >= 4:
                            e.wait_ge(sT.h, tok[("pvj", c, vi - 4)])
                        tok[("vld", c, vi)] = dma(
                            e, qVr[vi % 4], AB[:, VR0 + (vi % 4) * 512: VR0 + (vi % 4 + 1) * 512],
                            agvo[j * 128:(j + 1) * 128, dh * 512:(dh + 1) * 512])

                @block.tensor
                def _(e, c=c, dh=dh, sp=sp):
                    if dh == 0 and sp == 0:
                        e.wait_ge(sA.h, tok[("sig", c, 63)])
                    if (c, dh, sp) != (0, 0, 0):
                        pc, pdh, psp = (c, dh, sp - 1) if sp == 1 else \
                                       ((c, 0, 1) if dh == 1 else (c - 1, 1, 1))
                        e.wait_ge(sV.h, tok[("uadd", pc, pdh, psp)])
                    for j in range(64):
                        vi = (dh * 2 + sp) * 64 + j
                        e.wait_ge(qVr[vi % 4].h, tok[("vld", c, vi)])
                        instr = None
                        for s in range(2):
                            sub = 2 * sp + s
                            instr = e.matmul(pPV[s][:, :],
                                             AB[:, WS0 + j * 512 + sub * 128: WS0 + j * 512 + (sub + 1) * 128],
                                             AB[:, VR0 + (vi % 4) * 512: VR0 + (vi % 4 + 1) * 512],
                                             start=(j == 0), stop=(j == 63))
                        got = inc(instr, b.sT)
                        assert got == tok[("pvj", c, vi)], (got, tok[("pvj", c, vi)])

                @block.vector
                def _(e, c=c, dh=dh, sp=sp):
                    e.wait_ge(sT.h, tok[("pvj", c, (dh * 2 + sp) * 64 + 63)])
                    instr = None
                    for s in range(2):
                        t = c * 4 + 2 * sp + s
                        instr = e.tensor_tensor(
                            out=resid[:, t * 1024 + dh * 512: t * 1024 + (dh + 1) * 512],
                            in0=resid[:, t * 1024 + dh * 512: t * 1024 + (dh + 1) * 512],
                            in1=pPV[s][:, :], op=OP.add)
                    inc(instr, b.sV)
                    tok[("uadd", c, dh, sp)] = b.sV.c

    barrier(b, block)
    es.close()


def phase_out(b, st):
    block, resid = st["block"], st["resid"]

    qOut = b.chan("out")

    if st.get("upto", 6) < 2:
        @block.vector
        def _(e):
            e.memset(resid[:, :], 0.0)
            e.sem_inc(b.sV.h, 1)
            b.sV.c += 1
        barrier(b, block)

    @block.scalar
    def _(e):
        for t in range(16):
            dma(e, qOut, st["out_own"][t * 128:(t + 1) * 128, :],
                resid[:, t * 1024:(t + 1) * 1024])
        e.wait_ge(qOut.h, qOut.c)



def _stage_inputs(inputs):
    ins = {k: np.ascontiguousarray(np.asarray(v, dtype=np.float32)) for k, v in inputs.items()}
    x = ins["L0_attn_a_x"]

    import ml_dtypes
    identity = np.eye(128, dtype=ml_dtypes.bfloat16)
    masks = np.full((4, 128, 512), MASKVAL, np.float32)
    jj = np.arange(128)[:, None]
    ii = np.arange(512)[None, :]
    for d_idx in range(4):
        masks[d_idx][ii >= jj + 128 * d_idx] = 0.0
    masks = np.ascontiguousarray(masks.reshape(4 * 128, 512))

    wqT = np.ascontiguousarray(ins["L0_attn_a_Wq"].T)
    wkT = np.ascontiguousarray(ins["L0_attn_a_Wk"].T)
    wvT = np.ascontiguousarray(ins["L0_attn_a_Wv"].T)
    wfT = np.ascontiguousarray(ins["L0_attn_a_Wf"].T)
    woT = np.ascontiguousarray(ins["L0_attn_a_Wo"].T)
    wgT = np.ascontiguousarray(ins["L0_ffn_w_gate"].T)
    wuT = np.ascontiguousarray(ins["L0_ffn_w_up"].T)
    wdT = np.ascontiguousarray(ins["L0_ffn_w_down"].T)

    in_maps = []
    for c in range(8):
        bb, g = c // 4, c % 4
        hc = slice(g * HPC * DH, (g + 1) * HPC * DH)
        m = {
            "x_full": np.ascontiguousarray(x[bb]),
            "x_own": np.ascontiguousarray(x[bb, g * TOK:(g + 1) * TOK]),
            "wqaT": np.ascontiguousarray(wqT[:, hc]),
            "wkaT": np.ascontiguousarray(wkT[:, hc]),
            "wvaT": np.ascontiguousarray(wvT[:, hc]),
            "wfaT": np.ascontiguousarray(wfT[:, g * HPC:(g + 1) * HPC]),
            "bfa": np.ascontiguousarray(ins["L0_attn_a_bf"][g * HPC:(g + 1) * HPC, None]),
            "wogT": np.ascontiguousarray(woT[g * HPC * DH:(g + 1) * HPC * DH, :]),
            "wqbT": ins["L0_attn_b_Wq"], "wkbT": ins["L0_attn_b_Wk"], "wvbT": ins["L0_attn_b_Wv"],
            "bqb": np.ascontiguousarray(ins["L0_attn_b_bq"][:, None]),
            "bkb": np.ascontiguousarray(ins["L0_attn_b_bk"][:, None]),
            "bvb": np.ascontiguousarray(ins["L0_attn_b_bv"][None, :]),
            "wgT": wgT, "wuT": wuT, "wdT": wdT,
            "masks": masks, "ident": identity,
        }
        in_maps.append(m)
    return in_maps


def kernel(**inputs):
    nc = build_nc()
    in_maps = _stage_inputs(inputs)
    res = run_bass_kernel_spmd(nc, in_maps, core_ids=list(range(8)))
    out = np.empty((B, S, D), np.float32)
    for c in range(8):
        bb, g = c // 4, c % 4
        out[bb, g * TOK:(g + 1) * TOK] = res.results[c]["out_own"]
    return out


# version 10 (final)
# speedup vs baseline: 1.0814x; 1.0101x over previous
import contextlib
import numpy as np

import concourse.bass as bass
import concourse.mybir as mybir
from concourse.bass_utils import run_bass_kernel_spmd
from concourse.alu_op_type import AluOpType as OP

F32 = mybir.dt.float32
BF16 = mybir.dt.bfloat16
I32 = mybir.dt.int32
AFT = mybir.ActivationFunctionType
AX = mybir.AxisListType.X

B, S, D = 2, 8192, 1024
H, DH, HPC = 16, 64, 4
TOK = 2048
INNER = 4096
EPS = 1e-5
SCALE_A = 0.125
SCALE_B = 0.03125
MASKVAL = -1.0e7

UPTO = 6
TAPS = 0
P1SUB = 9
P1PARTS = 9
P1GRPS = 16


class Sem:
    def __init__(self, h):
        self.h = h
        self.c = 0


def inc(instr, s, n=1):
    instr.then_inc(s.h, n)
    s.c += n
    return s.c


def dma(e, q, out, in_):
    e.dma_start(out=out, in_=in_).then_inc(q.h, 16)
    q.c += 16
    return q.c


class Bld:
    def __init__(self, nc, es):
        self.nc = nc
        self.es = es
        self._sems = []
        self._chans = {}
        self.sT = self._mk("sT")
        self.sV = self._mk("sV")
        self.sA = self._mk("sA")
        self.sG = self._mk("sG")
        self.cc = self._mk("cc")

    def _mk(self, name):
        s = Sem(self.es.enter_context(self.nc.semaphore(name)))
        self._sems.append(s)
        return s

    def chan(self, name):
        if name not in self._chans:
            self._chans[name] = self._mk("q_" + name)
        return self._chans[name]

    def all_sems(self):
        return list(self._sems)


def vgap(e, smalls, n=2):
    for _ in range(n):
        e.tensor_copy(out=smalls[:, 254:255], in_=smalls[:, 253:254])


def barrier(b, block):
    snap = [(s, s.c) for s in b.all_sems() if s.c > 0]
    for en in ("sync", "vector", "scalar", "tensor", "gpsimd"):
        @getattr(block, en)
        def _(e, snap=snap):
            for s, v in snap:
                e.wait_ge(s.h, v)



def emit_ln_group(b, block, srcs, dsts, smalls, sq8, pre_wait_v=(), pre_wait_a=()):
    n = len(srcs)
    stats = {nm: smalls[:, i * 4:i * 4 + 4] for i, nm in
             enumerate(["s1", "s2", "mean", "vpe", "rstd", "nbias"])}
    col = lambda t, k: t[:, k:k + 1]

    @block.vector
    def _(e):
        for s, v in pre_wait_v:
            e.wait_ge(s.h, v)
        for k in range(n):
            e.reduce_sum(out=col(stats["s1"], k), in_=srcs[k], axis=AX)
        instr = None
        for k in range(n):
            instr = e.tensor_scalar(out=col(stats["mean"], k), in0=col(stats["s1"], k),
                                    scalar1=1.0 / D, scalar2=None, op0=OP.mult)
        inc(instr, b.sV)
    t_v1 = b.sV.c
    t_v2, t_v3 = t_v1 + 1, t_v1 + 2

    @block.scalar
    def _(e):
        for s, v in pre_wait_a:
            e.wait_ge(s.h, v)
        instr = None
        for k in range(n):
            instr = e.activation(out=sq8[:, k * 1024:(k + 1) * 1024], in_=srcs[k],
                                 func=AFT.Square, accum_out=col(stats["s2"], k))
        inc(instr, b.sA)
        t_s2 = b.sA.c
        e.wait_ge(b.sV.h, t_v2)
        instr = None
        for k in range(n):
            instr = e.activation(out=col(stats["s2"], k), in_=col(stats["vpe"], k), func=AFT.Sqrt)
        inc(instr, b.sA)
        e.wait_ge(b.sV.h, t_v3)
        instr = None
        for k in range(n):
            instr = e.activation(out=dsts[k], in_=srcs[k], func=AFT.Identity,
                                 scale=col(stats["rstd"], k), bias=col(stats["nbias"], k))
        inc(instr, b.sA)
    t_s2 = b.sA.c - 2
    t_sq = b.sA.c - 1

    @block.vector
    def _(e):
        e.wait_ge(b.sA.h, t_s2)
        for k in range(n):
            e.tensor_scalar(out=col(stats["vpe"], k), in0=col(stats["s2"], k),
                            scalar1=1.0 / D, scalar2=None, op0=OP.mult)
        for k in range(n):
            e.tensor_scalar(out=col(stats["s1"], k), in0=col(stats["mean"], k),
                            scalar1=col(stats["mean"], k), scalar2=None, op0=OP.mult)
        instr = None
        for k in range(n):
            instr = e.tensor_scalar(out=col(stats["vpe"], k), in0=col(stats["vpe"], k),
                                    scalar1=col(stats["s1"], k), scalar2=EPS,
                                    op0=OP.subtract, op1=OP.add)
        inc(instr, b.sV)
    assert b.sV.c == t_v2

    @block.vector
    def _(e):
        e.wait_ge(b.sA.h, t_sq)
        for k in range(n):
            e.reciprocal(out=col(stats["rstd"], k), in_=col(stats["s2"], k))
        instr = None
        for k in range(n):
            instr = e.tensor_scalar(out=col(stats["nbias"], k), in0=col(stats["mean"], k),
                                    scalar1=col(stats["rstd"], k), scalar2=-1.0,
                                    op0=OP.mult, op1=OP.mult)
        inc(instr, b.sV)
    assert b.sV.c == t_v3
    return b.sA.c



def build_nc(upto=None, taps=None):
    upto = UPTO if upto is None else upto
    taps = TAPS if taps is None else taps
    nc = bass.Bass("TRN2", target_bir_lowering=False, detect_race_conditions=False)

    dT = {}
    def din(name, shape, dt=F32):
        dT[name] = nc.dram_tensor(name, shape, dt, kind="ExternalInput").ap()
    din("x_full", [S, D]); din("x_own", [TOK, D])
    din("wqaT", [D, HPC * DH]); din("wkaT", [D, HPC * DH]); din("wvaT", [D, HPC * DH])
    din("wfaT", [D, HPC]); din("bfa", [HPC, 1]); din("wogT", [HPC * DH, D])
    din("wqbT", [D, D]); din("wkbT", [D, D]); din("wvbT", [D, D])
    din("bqb", [D, 1]); din("bkb", [D, 1]); din("bvb", [1, D])
    din("wgT", [D, INNER]); din("wuT", [D, INNER]); din("wdT", [INNER, D])
    din("masks", [4 * 128, 512]); din("ident", [128, 128], BF16)

    out_own = nc.dram_tensor("out_own", [TOK, D], F32, kind="ExternalOutput").ap()

    qa = nc.dram_tensor("qa_dram", [HPC * 72, S], BF16).ap()
    ka = nc.dram_tensor("ka_dram", [HPC * 72, S], BF16).ap()
    fD = nc.dram_tensor("f_dram", [HPC, S], F32).ap()
    rs_in = nc.dram_tensor("rs_in", [S, D], BF16)
    rs_out = nc.dram_tensor("rs_out", [TOK, D], BF16)
    wg_q = nc.dram_tensor("wg_q", [D, INNER], BF16).ap()
    wu_q = nc.dram_tensor("wu_q", [D, INNER], BF16).ap()
    wd_q = nc.dram_tensor("wd_q", [INNER, D], BF16).ap()
    hD = nc.dram_tensor("h_dram", [TOK, INNER], F32).ap()
    qbD = nc.dram_tensor("qb_dram", [D, TOK], BF16).ap()
    agk_in = nc.dram_tensor("agk_in", [D, TOK], BF16)
    agk_out = nc.dram_tensor("agk_out", [4 * D, TOK], BF16)
    agv_in = nc.dram_tensor("agv_in", [TOK, D], BF16)
    agv_out = nc.dram_tensor("agv_out", [S, D], BF16)

    tap = {}
    if taps:
        for p in (2, 3, 4):
            tap[p] = nc.dram_tensor(f"tap_resid{p}", [TOK, D], F32, kind="ExternalOutput").ap()
    if taps >= 9:
        tap["smalls"] = nc.dram_tensor("tap_smalls", [128, 256], F32, kind="ExternalOutput").ap()
        tap["h"] = nc.dram_tensor("tap_h", [TOK, INNER], F32, kind="ExternalOutput").ap()
        tap["wgq"] = nc.dram_tensor("tap_wgq", [INNER, D], BF16, kind="ExternalOutput").ap()
        tap["xqt"] = nc.dram_tensor("tap_xqt", [128, 4096], BF16, kind="ExternalOutput").ap()
        tap["rsin"] = nc.dram_tensor("tap_rsin", [S, D], F32, kind="ExternalOutput").ap()
        tap["f"] = nc.dram_tensor("tap_f", [HPC, S], F32, kind="ExternalOutput").ap()
        tap["qa"] = nc.dram_tensor("tap_qa", [HPC * 72, S], BF16, kind="ExternalOutput").ap()
        tap["ka"] = nc.dram_tensor("tap_ka", [HPC * 72, S], BF16, kind="ExternalOutput").ap()

    es = contextlib.ExitStack()
    with es:
        b = Bld(nc, es)
        sb = lambda n, sh, dt: es.enter_context(nc.sbuf_tensor(n, sh, dt))
        resid = sb("resid", [128, 16 * 1024], F32)
        AB = sb("AB", [128, 45312], BF16)
        AF = sb("AF", [128, 10240], F32)
        AI = sb("AI", [128, 1024], I32)
        sq8 = sb("sq8", [128, 4096], mybir.dt.int8)
        ident = sb("ident_s", [128, 128], BF16)
        smalls = sb("smalls", [128, 256], F32)

        st = dict(dT=dT, out_own=out_own, qa=qa, ka=ka, fD=fD,
                  rs_in=rs_in, rs_out=rs_out, wg_q=wg_q, wu_q=wu_q,
                  wd_q=wd_q, hD=hD, qbD=qbD, agk_in=agk_in, agk_out=agk_out,
                  agv_in=agv_in, agv_out=agv_out, resid=resid, AB=AB, AF=AF,
                  AI=AI, sq8=sq8, ident=ident, smalls=smalls, tap=tap, nc=nc, es=es)

        block = es.enter_context(nc.Block())
        st["block"] = block

        st["upto"] = upto
        phase1(b, st)
        if upto >= 2:
            phase2a(b, st)
        if upto >= 3:
            quant_weights(b, st)
        if "rsin" in tap:
            qT9 = b.chan("t9")

            @block.gpsimd
            def _(e):
                dma(e, qT9, tap["rsin"], rs_in.ap()[:, :])
                dma(e, qT9, tap["f"], fD[:, :])
                dma(e, qT9, tap["qa"], qa[:, :])
                dma(e, qT9, tap["ka"], ka[:, :])
                e.wait_ge(qT9.h, qT9.c)
            barrier(b, st["block"])
        if upto >= 2:
            phase2(b, st)
            emit_tap(b, st, 2)
        if upto >= 3:
            phase_ffn(b, st, first=True)
            if "smalls" in tap:
                qT8 = b.chan("t8")

                @block.gpsimd
                def _(e):
                    dma(e, qT8, tap["smalls"], smalls[:, :])
                    dma(e, qT8, tap["h"], st["hD"][:, :])
                    dma(e, qT8, tap["wgq"], st["wd_q"][:, :])
                    dma(e, qT8, tap["xqt"], AB[:, 32768:45056 + 0][:, 8192:8192 + 4096] if False else AB[:, 40960:45056])
                    e.wait_ge(qT8.h, qT8.c)
                barrier(b, st["block"])
            emit_tap(b, st, 3)
        if upto >= 4:
            phase4(b, st)
            emit_tap(b, st, 4)
        if upto >= 5:
            phase_ffn(b, st, first=False)
        phase_out(b, st)

    return nc


def emit_tap(b, st, p):
    if p not in st["tap"]:
        return
    block, resid = st["block"], st["resid"]
    barrier(b, st["block"])

    qTap = b.chan("tap")

    @block.scalar
    def _(e):
        for t in range(16):
            dma(e, qTap, st["tap"][p][t * 128:(t + 1) * 128, :],
                resid[:, t * 1024:(t + 1) * 1024])
    barrier(b, st["block"])



def phase1(b, st):
    nc, block = st["nc"], st["block"]
    es = contextlib.ExitStack()
    st["es"] = es

    dT, AB, AF, AI = st["dT"], st["AB"], st["AF"], st["AI"]
    smalls, ident = st["smalls"], st["ident"]
    qa, ka, fD = st["qa"], st["ka"], st["fD"]

    WQ0, WK0, WV0, WF0 = 0, 2048, 4096, 6144
    XNT0 = 6176
    XNB0 = XNT0 + 4096
    EV0 = XNB0 + 4096
    ONE0 = EV0 + 1024
    SPL0 = ONE0 + 8192
    VA0 = 28672
    VEND = VA0 + HPC * 64 * 65
    assert VEND <= 45312, VEND
    KB0 = 0
    QR0 = 8192
    PR0 = 9216
    UN0 = 11264
    UGT0 = 12288

    XS0 = 0
    FEV0 = 4096
    BC0 = 4608
    MK0 = 8192

    sT, sV, sA = b.sT, b.sV, b.sA
    qC = b.chan("c1"); qWa = b.chan("wa"); qX = b.chan("x")
    qKd = [b.chan("qkd0"), b.chan("qkd1")]; qFd = b.chan("fd")
    qOn = b.chan("ones"); qSp = b.chan("split"); qKb = b.chan("kb")
    qQs = [b.chan("qs0"), b.chan("qs1")]; qL = b.chan("lrow")
    qU = [b.chan("ugt0"), b.chan("ugt1")]
    qY = [b.chan("y0"), b.chan("y1")]

    pU = [es.enter_context(nc.psum_tensor(f"pU{i}", [65, 512], F32)) for i in range(2)]
    es_a = contextlib.ExitStack()
    pQK = [es_a.enter_context(nc.psum_tensor(f"pQK{i}", [128, 512], F32)) for i in range(2)]
    pV = es_a.enter_context(nc.psum_tensor("pV", [128, 512], F32))
    pF = es_a.enter_context(nc.psum_tensor("pF", [4, 512], F32))
    psb = es_a.enter_context(nc.psum_tensor("psb", [128, 2048], BF16))

    tok = {}

    @block.sync
    def _(e):
        dma(e, qC, ident[:, :], dT["ident"][:, :])
        dma(e, qC, AF[:, MK0:MK0 + 2048].rearrange("p (m n) -> p m n", n=512),
            dT["masks"].rearrange("(m p) n -> p m n", p=128))
        tok["consts"] = dma(e, qC, smalls[0:HPC, 96:97], dT["bfa"][:, :])

    for name, dst0, ncol in (("wqaT", WQ0, 256), ("wkaT", WK0, 256),
                             ("wvaT", WV0, 256), ("wfaT", WF0, 4)):
        @block.sync
        def _(e, name=name, ncol=ncol):
            prev = tok.get("wcast_prev")
            if prev:
                e.wait_ge(sV.h, prev)
            tok[name] = dma(e, qWa, AF[:, XS0:XS0 + 8 * ncol].rearrange("p (c n) -> p c n", n=ncol),
                            dT[name].rearrange("(c p) n -> p c n", p=128))

        @block.vector
        def _(e, name=name, dst0=dst0, ncol=ncol):
            e.wait_ge(qWa.h, tok[name])
            instr = e.tensor_copy(out=AB[:, dst0:dst0 + 8 * ncol], in_=AF[:, XS0:XS0 + 8 * ncol])
            inc(instr, b.sV)
            tok["wcast_prev"] = b.sV.c

    @block.vector
    def _(e):
        e.memset(smalls[:, 253:254], 0.0)
        e.memset(smalls[:, 254:255], 0.0)
        e.memset(AF[0:64, BC0:BC0 + 512], 0.0)
        e.memset(AB[0:1, ONE0:ONE0 + 8192], 1.0)
        vones = AB[:, VA0:VEND].rearrange("p (t c) -> p t c", c=65)
        instr = e.memset(vones[:, :, 64:65], 1.0)
        inc(instr, b.sV)
        tok["wcast"] = b.sV.c

    @block.gpsimd
    def _(e):
        e.wait_ge(sV.h, tok["wcast"])
        ka3 = ka.rearrange("(h rr) n -> h rr n", rr=72)
        qa3 = qa.rearrange("(h rr) n -> h rr n", rr=72)
        for r in range(3):
            for h in range(HPC):
                dma(e, qOn, ka3[h:h + 1, 64 + r, :], AB[0:1, ONE0:ONE0 + 8192])
                dma(e, qOn, qa3[h:h + 1, 67 + r, :], AB[0:1, ONE0:ONE0 + 8192])
        tok["onesrows"] = qOn.c

    NG = P1GRPS
    for grp in range(NG):
        t0 = grp * 512
        xs = [AF[:, XS0 + k * 1024: XS0 + (k + 1) * 1024] for k in range(4)]
        xnb = [AB[:, XNB0 + k * 1024: XNB0 + (k + 1) * 1024] for k in range(4)]
        xnt = lambda ic: AB[:, XNT0 + ic * 512: XNT0 + (ic + 1) * 512]

        @block.sync
        def _(e, grp=grp, t0=t0, xs=xs):
            if grp > 0:
                e.wait_ge(sA.h, tok[("xn", grp - 1)])
            else:
                e.wait_ge(sV.h, tok["wcast"])
            for k in range(4):
                dma(e, qX, xs[k], dT["x_full"][t0 + k * 128: t0 + (k + 1) * 128, :])
            tok[("x", grp)] = qX.c

        pwv = [(qX, tok[("x", grp)])]
        pwa = [(qX, tok[("x", grp)]), (qC, tok["consts"])]
        if grp == 0:
            pwa.append((sV, tok["wcast"]))
            pwv.append((sV, tok["wcast"]))
        elif P1PARTS >= 2:
            pwa.append((sT, tok[("tp", grp - 1)]))
        tok[("xn", grp)] = emit_ln_group(b, block, xs, xnb, smalls, st["sq8"],
                                         pre_wait_v=pwv, pre_wait_a=pwa)

        for k in range(4 if P1PARTS >= 2 else 0):
            for ic in range(8):
                idx = k * 8 + ic

                @block.tensor
                def _(e, grp=grp, k=k, ic=ic, idx=idx, xnb=xnb):
                    if idx == 0:
                        e.wait_ge(sA.h, tok[("xn", grp)])
                        if grp > 0:
                            e.wait_ge(sV.h, tok[("tpev", grp - 1, 31)])
                    if idx >= 2:
                        e.wait_ge(sV.h, tok[("tpev", grp, idx - 2)])
                    instr = e.transpose(psb[:, (idx % 2) * 1024:(idx % 2) * 1024 + 128],
                                        xnb[k][:, ic * 128:(ic + 1) * 128], ident[:, :])
                    inc(instr, b.sT)
                    tok[("tp1", grp, idx)] = b.sT.c
                    if idx == 31:
                        tok[("tp", grp)] = b.sT.c

                @block.vector
                def _(e, grp=grp, k=k, ic=ic, idx=idx):
                    e.wait_ge(sT.h, tok[("tp1", grp, idx)])
                    instr = e.tensor_copy(out=xnt(ic)[:, k * 128:(k + 1) * 128],
                                          in_=psb[:, (idx % 2) * 1024:(idx % 2) * 1024 + 128])
                    inc(instr, b.sV)
                    tok[("tpev", grp, idx)] = b.sV.c

        for proj, w0, scl, dram in ((("q", WQ0, SCALE_A, qa), ("k", WK0, 1.0, ka)) if P1PARTS >= 3 else ()):
            @block.tensor
            def _(e, grp=grp, proj=proj, w0=w0):
                if proj == "q":
                    e.wait_ge(sV.h, tok[("tpev", grp, 31)])
                for oc in range(2):
                    prev = tok.get(("qkev", grp, "q", oc)) if proj == "k" else \
                           (tok.get(("qkev", grp - 1, "k", oc)) if grp > 0 else None)
                    if prev:
                        e.wait_ge(sA.h, prev)
                    instr = None
                    for ic in range(8):
                        instr = e.matmul(pQK[oc][:, :],
                                         AB[:, w0 + ic * 256 + oc * 128: w0 + ic * 256 + (oc + 1) * 128],
                                         xnt(ic), start=(ic == 0), stop=(ic == 7))
                    inc(instr, b.sT)
                    tok[("qk", grp, proj, oc)] = b.sT.c

            @block.scalar
            def _(e, grp=grp, proj=proj, scl=scl):
                for oc in range(2):
                    e.wait_ge(sT.h, tok[("qk", grp, proj, oc)])
                    prev = tok.get(("qkdma", grp, "q", oc)) if proj == "k" else \
                           (tok.get(("qkdma", grp - 1, "k", oc)) if grp > 0 else None)
                    if prev:
                        e.wait_ge(qKd[oc].h, prev)
                    instr = e.activation(out=AB[:, EV0 + oc * 512: EV0 + (oc + 1) * 512],
                                         in_=pQK[oc][:, :], func=AFT.Identity, scale=scl)
                    inc(instr, b.sA)
                    tok[("qkev", grp, proj, oc)] = b.sA.c

            @block.gpsimd
            def _(e, grp=grp, proj=proj, dram=dram, t0=t0):
                for oc in range(2):
                    e.wait_ge(sA.h, tok[("qkev", grp, proj, oc)])
                    for hh in range(2):
                        h = oc * 2 + hh
                        dma(e, qKd[oc], dram[h * 72: h * 72 + 64, t0:t0 + 512],
                            AB[hh * 64:(hh + 1) * 64, EV0 + oc * 512: EV0 + (oc + 1) * 512])
                    tok[("qkdma", grp, proj, oc)] = qKd[oc].c

        for k in range(4 if P1PARTS >= 4 else 0):
            @block.tensor
            def _(e, grp=grp, k=k):
                prev = tok.get(("vev", grp, k - 1)) if k > 0 else \
                       (tok.get(("vev", grp - 1, 3)) if grp > 0 else None)
                if prev:
                    e.wait_ge(sV.h, prev)
                instr = None
                for ic in range(8):
                    instr = e.matmul(pV[:, 0:256], xnt(ic)[:, k * 128:(k + 1) * 128],
                                     AB[:, WV0 + ic * 256: WV0 + (ic + 1) * 256],
                                     start=(ic == 0), stop=(ic == 7))
                inc(instr, b.sT)
                tok[("v", grp, k)] = b.sT.c

            @block.vector
            def _(e, grp=grp, k=k):
                e.wait_ge(sT.h, tok[("v", grp, k)])
                kt = grp * 4 + k
                instr = None
                for h in range(HPC):
                    c0 = VA0 + (h * 64 + kt) * 65
                    instr = e.tensor_copy(out=AB[:, c0:c0 + 64], in_=pV[:, h * 64:(h + 1) * 64])
                inc(instr, b.sV)
                tok[("vev", grp, k)] = b.sV.c

        if P1PARTS < 5:
            continue

        @block.tensor
        def _(e, grp=grp):
            if grp > 0:
                e.wait_ge(sA.h, tok[("fev", grp - 1)])
            instr = None
            for ic in range(8):
                instr = e.matmul(pF[:, :], AB[:, WF0 + ic * 4: WF0 + (ic + 1) * 4],
                                 xnt(ic), start=(ic == 0), stop=(ic == 7))
            inc(instr, b.sT)
            tok[("f", grp)] = b.sT.c

        @block.scalar
        def _(e, grp=grp):
            e.wait_ge(sT.h, tok[("f", grp)])
            if grp > 0:
                e.wait_ge(qFd.h, tok[("fdma", grp - 1)])
            instr = e.activation(out=AF[0:4, FEV0:FEV0 + 512], in_=pF[:, :],
                                 func=AFT.Sigmoid, bias=smalls[0:4, 96:97], scale=1.0)
            inc(instr, b.sA)
            tok[("fev", grp)] = b.sA.c

        @block.gpsimd
        def _(e, grp=grp, t0=t0):
            e.wait_ge(sA.h, tok[("fev", grp)])
            tok[("fdma", grp)] = dma(e, qFd, fD[:, t0:t0 + 512], AF[0:4, FEV0:FEV0 + 512])

    barrier(b, block)
    if P1SUB < 2:
        es.close()
        return

    fr = AF[0:4, 0:2048]
    Cw = [AF[0:4, 2048:4096], AF[0:4, 4096:6144]]
    mw = AF[0:4, 6144:8192]
    hi = AB[0:4, 0:2048]
    mid = AB[0:4, 2048:4096]
    lo = AB[0:4, 4096:6144]

    for qtr in range(4):
        c0 = qtr * 2048
        Cr = Cw[qtr % 2]

        @block.sync
        def _(e, qtr=qtr, c0=c0):
            if qtr > 0:
                e.wait_ge(sV.h, tok[("fp2", qtr - 1)])
            tok[("fload", qtr)] = dma(e, qX, fr, fD[:, c0:c0 + 2048])

        @block.vector
        def _(e, qtr=qtr):
            e.wait_ge(qX.h, tok[("fload", qtr)])
            instr = e.tensor_scalar(out=fr, in0=fr, scalar1=1e-6, scalar2=None, op0=OP.add)
            inc(instr, b.sV)
            tok[("fp", qtr)] = b.sV.c

        @block.scalar
        def _(e, qtr=qtr):
            e.wait_ge(sV.h, tok[("fp", qtr)])
            instr = e.activation(out=fr, in_=fr, func=AFT.Ln)
            inc(instr, b.sA)
            tok[("lnf", qtr)] = b.sA.c

        @block.vector
        def _(e, qtr=qtr, Cr=Cr):
            e.wait_ge(sA.h, tok[("lnf", qtr)])
            if qtr == 0:
                e.memset(smalls[0:4, 100:101], 0.0)
                e.memset(smalls[0:4, 101:102], 0.0)
                init = smalls[0:4, 100:101]
            else:
                init = Cw[(qtr - 1) % 2][:, 2047:2048]
            if qtr >= 1:
                e.wait_ge(qSp.h, tok[("kadma", qtr - 1)])
            e.tensor_tensor_scan(out=Cr, data0=fr, data1=fr, initial=init,
                                 op0=OP.add, op1=OP.bypass)
            vgap(e, smalls)
            e.tensor_copy(out=hi, in_=Cr)
            vgap(e, smalls)
            e.tensor_copy(out=mw, in_=hi)
            vgap(e, smalls)
            e.tensor_tensor(out=fr, in0=Cr, in1=mw, op=OP.subtract)
            vgap(e, smalls)
            e.tensor_copy(out=mid, in_=fr)
            vgap(e, smalls)
            e.tensor_copy(out=mw, in_=mid)
            vgap(e, smalls)
            e.tensor_tensor(out=fr, in0=fr, in1=mw, op=OP.subtract)
            vgap(e, smalls)
            instr = e.tensor_copy(out=lo, in_=fr)
            inc(instr, b.sV)
            tok[("csplit", qtr)] = b.sV.c
            tok[("fp2", qtr)] = b.sV.c

        @block.gpsimd
        def _(e, qtr=qtr, c0=c0):
            e.wait_ge(sV.h, tok[("csplit", qtr)])
            qa3 = qa.rearrange("(h rr) n -> h rr n", rr=72)
            for r, t in ((0, hi), (1, mid), (2, lo)):
                dma(e, qSp, qa3[:, 64 + r, c0:c0 + 2048], t)
            tok[("qadma", qtr)] = qSp.c

        @block.vector
        def _(e, qtr=qtr):
            e.wait_ge(qSp.h, tok[("qadma", qtr)])
            instr = None
            for t in (hi, mid, lo):
                instr = e.tensor_scalar(out=t, in0=t, scalar1=-1.0, scalar2=None, op0=OP.mult)
            inc(instr, b.sV)
            tok[("nsplit", qtr)] = b.sV.c

        @block.gpsimd
        def _(e, qtr=qtr, c0=c0):
            e.wait_ge(sV.h, tok[("nsplit", qtr)])
            ka3 = ka.rearrange("(h rr) n -> h rr n", rr=72)
            for r, t in ((0, hi), (1, mid), (2, lo)):
                dma(e, qSp, ka3[:, 67 + r, c0:c0 + 2048], t)
            tok[("kadma", qtr)] = qSp.c

    barrier(b, block)
    es_a.close()
    pS2 = [es.enter_context(nc.psum_tensor(f"pS2{i}", [128, 1024], F32)) for i in range(2)]
    pY = [es.enter_context(nc.psum_tensor(f"pY{i}", [128, 512], F32)) for i in range(2)]
    if P1SUB < 3:
        es.close()
        return

    kslot = lambda kt: AB[:, KB0 + kt * 128: KB0 + (kt + 1) * 128]
    qslot = lambda it: AB[:, QR0 + (it % 2) * 512: QR0 + (it % 2 + 1) * 512]
    pslot = lambda gi: AB[:, PR0 + (gi % 4) * 512: PR0 + (gi % 4 + 1) * 512]
    unslot = lambda it: AB[0:64, UN0 + (it % 2) * 512: UN0 + (it % 2 + 1) * 512]

    seq = [(h, qc) for h in range(HPC) for qc in range(16)]
    sA0 = b.sA.c
    ptot = sum((4 * qc + 4) // 2 for h, qc in seq)
    for gp in range(ptot):
        tok[("expp", gp)] = sA0 + gp + 1
    pslot2 = lambda gp: AB[:, PR0 + (gp % 2) * 1024: PR0 + (gp % 2 + 1) * 1024]

    gbase = 0
    for it, (h, qc) in enumerate(seq):
        kmax = 4 * qc + 4
        pbase = gbase // 2

        @block.sync
        def _(e, it=it, h=h, qc=qc):
            if qc == 0:
                if it > 0:
                    e.wait_ge(sT.h, tok[("pvlast", it - 1)])
                tok[("kld", h)] = dma(e, qKb, AB[0:70, KB0:KB0 + 8192], ka[h * 72: h * 72 + 70, :])
            if it >= 2:
                e.wait_ge(sT.h, tok[("qfree", it - 2)])
            tok[("qld", it)] = dma(e, qQs[it % 2], qslot(it)[0:70, :],
                                   qa[h * 72: h * 72 + 70, qc * 512:(qc + 1) * 512])

        @block.tensor
        def _(e, it=it, h=h, qc=qc, kmax=kmax, gbase=gbase, pbase=pbase):
            e.wait_ge(qQs[it % 2].h, tok[("qld", it)])
            if qc == 0:
                e.wait_ge(qKb.h, tok[("kld", h)])
            for kp in range(kmax // 2):
                gp = pbase + kp
                if gp >= 2:
                    e.wait_ge(sA.h, tok[("expp", gp - 2)])
                instr = None
                for half in range(2):
                    kt = 2 * kp + half
                    instr = e.matmul(pS2[gp % 2][:, half * 512:(half + 1) * 512],
                                     kslot(kt)[0:70, :], qslot(it)[0:70, :],
                                     start=True, stop=True)
                inc(instr, b.sT)
                tok[("Sp", gp)] = b.sT.c
                if kp == kmax // 2 - 1:
                    tok[("qfree", it)] = b.sT.c
                if kp >= 1:
                    _pv2(e, b, tok, it, h, kp - 1, gbase, pbase, kmax, pU, AB, VA0, pslot2)
            _pv2(e, b, tok, it, h, kmax // 2 - 1, gbase, pbase, kmax, pU, AB, VA0, pslot2)
            tok[("pvlast", it)] = b.sT.c

        @block.vector
        def _(e, it=it, qc=qc, kmax=kmax, gbase=gbase, pbase=pbase):
            for kt in range(kmax - 4, kmax):
                gp = pbase + kt // 2
                d_idx = kt - (kmax - 4)
                half = kt % 2
                e.wait_ge(sT.h, tok[("Sp", gp)])
                instr = e.tensor_tensor(out=pS2[gp % 2][:, half * 512:(half + 1) * 512],
                                        in0=pS2[gp % 2][:, half * 512:(half + 1) * 512],
                                        in1=AF[:, MK0 + d_idx * 512: MK0 + (d_idx + 1) * 512],
                                        op=OP.add)
                inc(instr, b.sV)
                tok[("mask", gp, half)] = b.sV.c

        @block.scalar
        def _(e, it=it, qc=qc, kmax=kmax, gbase=gbase, pbase=pbase):
            for kp in range(kmax // 2):
                gp = pbase + kp
                diag = kp >= kmax // 2 - 2
                if diag:
                    e.wait_ge(sV.h, tok[("mask", gp, 1)])
                else:
                    e.wait_ge(sT.h, tok[("Sp", gp)])
                if gp >= 2:
                    e.wait_ge(sT.h, tok[("pv", 2 * (gp - 2) + 1)])
                instr = e.activation(out=pslot2(gp), in_=pS2[gp % 2][:, :], func=AFT.Exp)
                got = inc(instr, b.sA)
                assert got == tok[("expp", gp)], (got, tok[("expp", gp)])

        @block.vector
        def _(e, it=it):
            e.wait_ge(sT.h, tok[("pvlast", it)])
            if it >= 2:
                e.wait_ge(sV.h, tok[("unorm", it - 2)])
            instr = e.tensor_copy(out=AF[64:65, BC0:BC0 + 512], in_=pU[it % 2][64:65, :])
            inc(instr, b.sV)
            tok[("lcp", it)] = b.sV.c

        @block.gpsimd
        def _(e, it=it):
            e.wait_ge(sV.h, tok[("lcp", it)])
            dma(e, qL, AF[0:1, BC0:BC0 + 512], AF[64:65, BC0:BC0 + 512])
            dma(e, qL, AF[32:33, BC0:BC0 + 512], AF[64:65, BC0:BC0 + 512])
            tok[("lrow", it)] = qL.c

        @block.vector
        def _(e, it=it):
            e.wait_ge(qL.h, tok[("lrow", it)])
            e.reciprocal(out=AF[0:1, BC0:BC0 + 512], in_=AF[0:1, BC0:BC0 + 512])
            e.reciprocal(out=AF[32:33, BC0:BC0 + 512], in_=AF[32:33, BC0:BC0 + 512])
            vgap(e, smalls)
            e.stream_shuffle(out=AF[0:64, BC0:BC0 + 512], in_=AF[0:64, BC0:BC0 + 512],
                             mask=[0] * 32)
            if it >= 2:
                e.wait_ge(qU[it % 2].h, tok[("undma", it - 2)])
            vgap(e, smalls)
            instr = e.tensor_tensor(out=unslot(it), in0=pU[it % 2][0:64, :],
                                    in1=AF[0:64, BC0:BC0 + 512], op=OP.mult)
            inc(instr, b.sV)
            tok[("unorm", it)] = b.sV.c

        @block.gpsimd
        def _(e, it=it, h=h, qc=qc):
            e.wait_ge(sV.h, tok[("unorm", it)])
            c0 = UGT0 + (h // 2) * 8192 + qc * 512
            r0 = (h % 2) * 64
            tok[("undma", it)] = dma(e, qU[it % 2], AB[r0:r0 + 64, c0:c0 + 512], unslot(it))

        gbase += kmax

    barrier(b, block)
    if P1SUB < 4:
        es.close()
        return

    @block.sync
    def _(e):
        tok["wogld"] = dma(e, qWa, AF[:, XS0:XS0 + 2048].rearrange("p (c n) -> p c n", n=1024),
                           dT["wogT"].rearrange("(c p) n -> p c n", p=128))

    @block.vector
    def _(e):
        e.wait_ge(qWa.h, tok["wogld"])
        instr = e.tensor_copy(out=AB[:, QR0:QR0 + 2048], in_=AF[:, XS0:XS0 + 2048])
        inc(instr, b.sV)
        tok["wogcast"] = b.sV.c

    if P1SUB == 4:
        es.close()
        return
    for t in range(64 if P1SUB >= 6 else 1):
        @block.tensor
        def _(e, t=t):
            if t == 0:
                e.wait_ge(sV.h, tok["wogcast"])
            if t > 0:
                e.wait_ge(sV.h, tok[("yev", t - 1)])
            instr = None
            for half in range(2):
                for c2 in range(2):
                    lhs = AB[:, UGT0 + c2 * 8192 + t * 128: UGT0 + c2 * 8192 + (t + 1) * 128]
                    instr = e.matmul(pY[half][:, :], lhs,
                                     AB[:, QR0 + c2 * 1024 + half * 512: QR0 + c2 * 1024 + half * 512 + 512],
                                     start=(c2 == 0), stop=(c2 == 1))
            inc(instr, b.sT)
            tok[("yp", t)] = b.sT.c

        @block.vector
        def _(e, t=t):
            e.wait_ge(sT.h, tok[("yp", t)])
            if t >= 2:
                e.wait_ge(qY[t % 2].h, tok[("ydma", t - 2)])
            yb = AB[:, VA0 + (t % 2) * 1024: VA0 + (t % 2 + 1) * 1024]
            e.tensor_copy(out=yb[:, 0:512], in_=pY[0][:, :])
            instr = e.tensor_copy(out=yb[:, 512:1024], in_=pY[1][:, :])
            inc(instr, b.sV)
            tok[("yev", t)] = b.sV.c

        @block.gpsimd
        def _(e, t=t):
            e.wait_ge(sV.h, tok[("yev", t)])
            yb = AB[:, VA0 + (t % 2) * 1024: VA0 + (t % 2 + 1) * 1024]
            tok[("ydma", t)] = dma(e, qY[t % 2], st["rs_in"].ap()[t * 128:(t + 1) * 128, :], yb)

    barrier(b, block)
    es.close()


def _pv2(e, b, tok, it, h, kp, gbase, pbase, kmax, pU, AB, VA0, pslot2):
    gp = pbase + kp
    e.wait_ge(b.sA.h, tok[("expp", gp)])
    if kp == 0 and it >= 2:
        e.wait_ge(b.sV.h, tok[("unorm", it - 2)])
    for half in range(2):
        kt = 2 * kp + half
        c0 = VA0 + (h * 64 + kt) * 65
        instr = e.matmul(pU[it % 2][:, :], AB[:, c0:c0 + 65],
                         pslot2(gp)[:, half * 512:(half + 1) * 512],
                         start=(kt == 0), stop=(kt == kmax - 1))
        inc(instr, b.sT)
        tok[("pv", 2 * gp + half)] = b.sT.c



def phase2a(b, st):
    block = st["block"]

    @block.gpsimd
    def _(e):
        instr = e.collective_compute(
            "ReduceScatter", OP.add, replica_groups=[[0, 1, 2, 3], [4, 5, 6, 7]],
            ins=[st["rs_in"].ap().opt()], outs=[st["rs_out"].ap().opt()])
        inc(instr, b.cc)
        e.wait_ge(b.cc.h, b.cc.c)
        e.sem_inc(b.sG.h, 1)
        b.sG.c += 1
        st["tok_rs"] = b.sG.c


def phase2(b, st):
    nc, block = st["nc"], st["block"]
    dT, AF = st["dT"], st["AF"]
    resid, smalls = st["resid"], st["smalls"]
    sV = b.sV
    qP2 = [b.chan("p2a"), b.chan("p2b")]
    tok = {"rs": st["tok_rs"]}

    AB = st["AB"]
    for t in range(16):
        @block.sync
        def _(e, t=t):
            if t == 0:
                e.wait_ge(b.sG.h, tok["rs"])
            if t >= 2:
                e.wait_ge(sV.h, tok[("radd", t - 2)])
            s = (t % 2) * 2048
            dma(e, qP2[t % 2], AB[:, s:s + 1024], st["rs_out"].ap()[t * 128:(t + 1) * 128, :])
            dma(e, qP2[t % 2], AF[:, s:s + 1024], dT["x_own"][t * 128:(t + 1) * 128, :])
            tok[("ld", t)] = qP2[t % 2].c

        @block.vector
        def _(e, t=t):
            e.wait_ge(qP2[t % 2].h, tok[("ld", t)])
            s = (t % 2) * 2048
            e.tensor_copy(out=resid[:, t * 1024:(t + 1) * 1024], in_=AB[:, s:s + 1024])
            vgap(e, smalls)
            instr = e.tensor_tensor(out=resid[:, t * 1024:(t + 1) * 1024],
                                    in0=resid[:, t * 1024:(t + 1) * 1024],
                                    in1=AF[:, s:s + 1024], op=OP.add)
            inc(instr, b.sV)
            tok[("radd", t)] = b.sV.c

    barrier(b, block)




def quant_weights(b, st):
    nc, block, es0 = st["nc"], st["block"], None
    import contextlib as _ct
    es = _ct.ExitStack()
    dT, AB, AF, AI = st["dT"], st["AB"], st["AF"], st["AI"]
    smalls = st["smalls"]
    sT, sV, sA = b.sT, b.sV, b.sA
    qWq = [b.chan("wq0"), b.chan("wq1")]; qWst = b.chan("wst"); qPb = b.chan("pb")
    tok = {}
    GS0 = 32768
    OSC = {"g": 104, "u": 105, "d": 106}
    SW = {"g": 107, "u": 108, "d": 109}
    pPar = es.enter_context(nc.psum_tensor("pPar", [1, 8], F32))

    for wname, wdst, key in (("wgT", st["wg_q"], "g"), ("wuT", st["wu_q"], "u"),
                             ("wdT", st["wd_q"], "d")):
        wn = dT[wname].shape[1]
        wview = dT[wname].rearrange("(c p) n -> p c n", p=128)
        nview = wdst.rearrange("(c p) n -> p c n", p=128)
        npiece = 8
        cpp = wview.shape[1] // npiece
        pw = cpp * wn
        slot = lambda ch: AF[:, (ch % 2) * 4096:(ch % 2) * 4096 + 4096]

        for ch in range(npiece):
            @block.sync
            def _(e, ch=ch, wview=wview, pw=pw, cpp=cpp, wn=wn, key=key):
                if ch >= 2:
                    e.wait_ge(sA.h, tok[(key, "abs", ch - 2)])
                elif ch == 0 and key != "g":
                    e.wait_ge(sV.h, tok[({"u": "g", "d": "u"}[key], "clip", 7)])
                elif ch == 1 and key != "g":
                    e.wait_ge(sV.h, tok[({"u": "g", "d": "u"}[key], "clip", 7)])
                tok[(key, "wld", ch)] = dma(
                    e, qWq[ch % 2], slot(ch)[:, 0:pw].rearrange("p (c n) -> p c n", n=wn),
                    wview[:, ch * cpp:(ch + 1) * cpp, :])

            @block.scalar
            def _(e, ch=ch, pw=pw, key=key):
                e.wait_ge(qWq[ch % 2].h, tok[(key, "wld", ch)])
                if ch == 0 and key != "g":
                    e.wait_ge(sV.h, tok[({"u": "g", "d": "u"}[key], "rsum")])
                instr = None
                for cq in range(4):
                    instr = e.activation(out=st["sq8"][:, cq * 1024:(cq + 1) * 1024],
                                         in_=slot(ch)[:, cq * 1024:(cq + 1) * 1024],
                                         func=AFT.Abs,
                                         accum_out=smalls[:, 160 + ch * 4 + cq: 161 + ch * 4 + cq])
                inc(instr, b.sA)
                tok[(key, "abs", ch)] = b.sA.c

        @block.vector
        def _(e, key=key):
            e.wait_ge(sA.h, tok[(key, "abs", 7)])
            instr = e.reduce_sum(out=smalls[:, 126:127], in_=smalls[:, 160:192], axis=AX)
            inc(instr, b.sV)
            tok[(key, "rsum")] = b.sV.c

        @block.vector
        def _(e, key=key):
            if key == "g":
                e.memset(smalls[:, 252:253], 1.0)
            e.sem_inc(sV.h, 1)
            b.sV.c += 1
            tok[(key, "ones")] = b.sV.c

        @block.tensor
        def _(e, key=key):
            e.wait_ge(sV.h, tok[(key, "rsum")])
            e.wait_ge(sV.h, tok[(key, "ones")])
            if key != "g":
                e.wait_ge(sV.h, tok[({"u": "g", "d": "u"}[key], "scal")])
            instr = e.matmul(pPar[0:1, 0:1], smalls[:, 252:253], smalls[:, 126:127],
                             start=True, stop=True)
            inc(instr, b.sT)
            tok[(key, "par")] = b.sT.c

        @block.vector
        def _(e, key=key):
            e.wait_ge(sT.h, tok[(key, "par")])
            instr = e.tensor_copy(out=smalls[0:1, 127:128], in_=pPar[0:1, 0:1])
            inc(instr, b.sV)
            tok[(key, "parc")] = b.sV.c

        @block.sync
        def _(e, key=key):
            e.wait_ge(sV.h, tok[(key, "parc")])
            for p in (32, 64, 96):
                dma(e, qPb, smalls[p:p + 1, 127:128], smalls[0:1, 127:128])
            tok[(key, "parb")] = qPb.c

        @block.vector
        def _(e, key=key):
            e.wait_ge(qPb.h, tok[(key, "parb")])
            vgap(e, smalls)
            e.stream_shuffle(out=smalls[:, 127:128], in_=smalls[:, 127:128], mask=[0] * 32)
            vgap(e, smalls)
            e.tensor_scalar(out=smalls[:, OSC[key]:OSC[key] + 1], in0=smalls[:, 127:128],
                            scalar1=1.0 / (INNER * D), scalar2=1e-5, op0=OP.mult, op1=OP.max)
            vgap(e, smalls)
            instr = e.reciprocal(out=smalls[:, SW[key]:SW[key] + 1],
                                 in_=smalls[:, OSC[key]:OSC[key] + 1])
            inc(instr, b.sV)
            tok[(key, "scal")] = b.sV.c

        for ch in range(npiece):
            @block.sync
            def _(e, ch=ch, wview=wview, pw=pw, cpp=cpp, wn=wn, key=key):
                e.wait_ge(sV.h, tok[(key, "scal")])
                if ch >= 2:
                    e.wait_ge(sV.h, tok[(key, "clip", ch - 2)])
                tok[(key, "wld2", ch)] = dma(
                    e, qWq[ch % 2], slot(ch)[:, 0:pw].rearrange("p (c n) -> p c n", n=wn),
                    wview[:, ch * cpp:(ch + 1) * cpp, :])

            @block.vector
            def _(e, ch=ch, pw=pw, key=key):
                e.wait_ge(qWq[ch % 2].h, tok[(key, "wld2", ch)])
                if ch > 0:
                    e.wait_ge(qWst.h, tok[(key, "wst", ch - 1)])
                e.tensor_scalar(out=slot(ch)[:, 0:pw], in0=slot(ch)[:, 0:pw],
                                scalar1=smalls[:, SW[key]:SW[key] + 1], scalar2=None,
                                op0=OP.mult)
                instr = None
                vgap(e, smalls)
                for cq in range(4):
                    e.tensor_copy(out=AI[:, 0:1024], in_=slot(ch)[:, cq * 1024:(cq + 1) * 1024])
                    vgap(e, smalls)
                    instr = e.tensor_scalar(out=AB[0:128, GS0 + cq * 1024: GS0 + (cq + 1) * 1024],
                                            in0=AI[:, 0:1024], scalar1=-1.0, scalar2=1.0,
                                            op0=OP.max, op1=OP.min)
                    vgap(e, smalls)
                inc(instr, b.sV)
                tok[(key, "clip", ch)] = b.sV.c

            @block.sync
            def _(e, ch=ch, nview=nview, pw=pw, cpp=cpp, wn=wn, key=key):
                e.wait_ge(sV.h, tok[(key, "clip", ch)])
                tok[(key, "wst", ch)] = dma(
                    e, qWst, nview[:, ch * cpp:(ch + 1) * cpp, :],
                    AB[:, GS0:GS0 + pw].rearrange("p (c n) -> p c n", n=wn))

    barrier(b, block)
    es.close()




def phase_ffn(b, st, first):
    nc, block = st["nc"], st["block"]
    es = contextlib.ExitStack()
    dT, AB, AF, AI = st["dT"], st["AB"], st["AF"], st["AI"]
    resid, smalls = st["resid"], st["smalls"]
    sT, sV, sA = b.sT, b.sV, b.sA
    qWq = b.chan("wq"); qWst = b.chan("wst"); qGU = b.chan("gu")
    qH = [b.chan("h0"), b.chan("h1")]; qHL = b.chan("hl")
    tag = "f1" if first else "f2"
    tok = {}

    XQT0 = 0
    WG0 = 16384
    WU0 = 24576
    GS0 = 32768
    US0 = GS0 + 1024
    XQ0 = US0 + 1024
    WD0 = 0
    HQT0 = 32768
    HQ0 = 40960

    XN0 = 4096
    HB0 = 0

    AMX, S127, RS = 48, 64, 80
    OSC = {"g": 104, "u": 105, "d": 106}
    SW = {"g": 107, "u": 108, "d": 109}
    AMXH, S127H, RSH = 144, 160, 176

    psA = [es.enter_context(nc.psum_tensor(f"psA{tag}{i}", [128, 512], F32)) for i in range(4)]
    psb2 = es.enter_context(nc.psum_tensor(f"psb{tag}", [128, 2048], BF16))



    for t4 in range(4):
        srcs = [resid[:, (t4 * 4 + k) * 1024: (t4 * 4 + k + 1) * 1024] for k in range(4)]
        dsts = [AF[:, XN0 + k * 1024: XN0 + (k + 1) * 1024] for k in range(4)]
        pwv = []
        if t4 > 0:
            pwv.append((sT, tok[("xqtp", t4 - 1, 3)]))
        tok[("xn", t4)] = emit_ln_group(b, block, srcs, dsts, smalls, st["sq8"],
                                        pre_wait_v=pwv, pre_wait_a=pwv)

        @block.vector
        def _(e, t4=t4, dsts=dsts):
            e.wait_ge(sA.h, tok[("xn", t4)])
            for k in range(4):
                e.reduce_max(out=smalls[:, AMX + t4 * 4 + k: AMX + t4 * 4 + k + 1],
                             in_=dsts[k], axis=AX, apply_absolute_value=True)
            for k in range(4):
                c = t4 * 4 + k
                e.tensor_scalar(out=smalls[:, S127 + c: S127 + c + 1],
                                in0=smalls[:, AMX + c: AMX + c + 1],
                                scalar1=1e-5, scalar2=None, op0=OP.max)
            for k in range(4):
                c = t4 * 4 + k
                e.tensor_scalar(out=smalls[:, RS + c: RS + c + 1],
                                in0=smalls[:, S127 + c: S127 + c + 1],
                                scalar1=1.0 / 127.0, scalar2=None, op0=OP.mult)
            for k in range(4):
                c = t4 * 4 + k
                e.reciprocal(out=smalls[:, S127 + c: S127 + c + 1],
                             in_=smalls[:, S127 + c: S127 + c + 1])
            instr = None
            for k in range(4):
                c = t4 * 4 + k
                instr = e.tensor_scalar(out=smalls[:, S127 + c: S127 + c + 1],
                                        in0=smalls[:, S127 + c: S127 + c + 1],
                                        scalar1=127.0, scalar2=None, op0=OP.mult)
            inc(instr, b.sV)
            tok[("scx", t4)] = b.sV.c

        for k in range(4):
            t = t4 * 4 + k

            @block.vector
            def _(e, t4=t4, k=k, t=t, dsts=dsts):
                e.wait_ge(sV.h, tok[("scx", t4)])
                e.tensor_scalar(out=dsts[k], in0=dsts[k],
                                scalar1=smalls[:, S127 + t: S127 + t + 1], scalar2=None,
                                op0=OP.mult)
                if t > 0:
                    e.wait_ge(sA.h, tok[("xqa", t - 1)])
                vgap(e, smalls)
                instr = e.tensor_copy(out=AI[:, 0:1024], in_=dsts[k])
                inc(instr, b.sV)
                tok[("xqi", t)] = b.sV.c

            @block.scalar
            def _(e, t=t):
                e.wait_ge(sV.h, tok[("xqi", t)])
                if t >= 2:
                    e.wait_ge(sT.h, tok[("xqtpd", t - 2)])
                instr = e.activation(out=AB[:, XQ0 + (t % 2) * 1024: XQ0 + (t % 2 + 1) * 1024],
                                     in_=AI[:, 0:1024], func=AFT.Identity,
                                     scale=smalls[:, RS + t: RS + t + 1])
                inc(instr, b.sA)
                tok[("xqa", t)] = b.sA.c

            for ic in range(8):
                @block.tensor
                def _(e, t=t, ic=ic, t4=t4, k=k):
                    if ic == 0:
                        e.wait_ge(sA.h, tok[("xqa", t)])
                    idx = (t * 8 + ic)
                    if idx >= 2:
                        e.wait_ge(sV.h, tok[("xqev", idx - 2)])
                    instr = e.transpose(psb2[:, (idx % 2) * 1024:(idx % 2) * 1024 + 128],
                                        AB[:, XQ0 + (t % 2) * 1024 + ic * 128: XQ0 + (t % 2) * 1024 + (ic + 1) * 128],
                                        st["ident"][:, :])
                    inc(instr, b.sT)
                    tok[("xqtp1", idx)] = b.sT.c
                    if ic == 7:
                        tok[("xqtpd", t)] = b.sT.c
                        tok[("xqtp", t4, k)] = b.sT.c

                @block.vector
                def _(e, t=t, ic=ic):
                    idx = (t * 8 + ic)
                    e.wait_ge(sT.h, tok[("xqtp1", idx)])
                    instr = e.tensor_copy(
                        out=AB[:, XQT0 + ic * 2048 + t * 128: XQT0 + ic * 2048 + (t + 1) * 128],
                        in_=psb2[:, (idx % 2) * 1024:(idx % 2) * 1024 + 128])
                    inc(instr, b.sV)
                    tok[("xqev", idx)] = b.sV.c

    barrier(b, block)

    @block.vector
    def _(e):
        instr = e.memset(smalls[:, AMXH:AMXH + 16], 0.0)
        inc(instr, b.sV)
        tok["amxh0"] = b.sV.c

    gview = st["wg_q"].rearrange("(c p) n -> p c n", p=128)
    uview = st["wu_q"].rearrange("(c p) n -> p c n", p=128)
    for q4 in range(4):
        sT0, sA0, sV0 = b.sT.c, b.sA.c, b.sV.c
        qhb = [qH[0].c, qH[1].c]
        for t in range(16):
            tok[("gu", q4, t)] = sT0 + t + 1
            tok[("guev", q4, t)] = sA0 + t + 1
            tok[("hmul", q4, t)] = sV0 + t + 1
            tok[("hst", q4, t)] = qhb[t % 2] + 16 * (t // 2 + 1)

        @block.sync
        def _(e, q4=q4):
            if q4 > 0:
                e.wait_ge(sT.h, tok[("gu", q4 - 1, 15)])
            abg = AB[:, WG0:WG0 + 8192].rearrange("p (c n) -> p c n", n=1024)
            abu = AB[:, WU0:WU0 + 8192].rearrange("p (c n) -> p c n", n=1024)
            dma(e, qGU, abg, gview[:, :, q4 * 1024:(q4 + 1) * 1024])
            dma(e, qGU, abu, uview[:, :, q4 * 1024:(q4 + 1) * 1024])
            tok[("guld", q4)] = qGU.c

        @block.tensor
        def _(e, q4=q4):
            e.wait_ge(qGU.h, tok[("guld", q4)])
            if q4 == 0:
                e.wait_ge(sV.h, tok["amxh0"])
            for t in range(16):
                prev = tok[("guev", q4, t - 1)] if t > 0 else \
                       (tok.get(("guev", q4 - 1, 15)) if q4 > 0 else None)
                if prev:
                    e.wait_ge(sA.h, prev)
                instr = None
                for w0, p0 in ((WG0, 0), (WU0, 2)):
                    for ic in range(8):
                        lhs = AB[:, XQT0 + ic * 2048 + t * 128: XQT0 + ic * 2048 + (t + 1) * 128]
                        w = AB[:, w0 + ic * 1024: w0 + (ic + 1) * 1024]
                        st_, sp = (ic == 0), (ic == 7)
                        e.matmul(psA[p0][:, :], lhs, w[:, 0:512], start=st_, stop=sp)
                        instr = e.matmul(psA[p0 + 1][:, :], lhs, w[:, 512:1024], start=st_, stop=sp)
                got = inc(instr, b.sT)
                assert got == tok[("gu", q4, t)]

        @block.scalar
        def _(e, q4=q4):
            for t in range(16):
                e.wait_ge(sT.h, tok[("gu", q4, t)])
                prev = tok[("hmul", q4, t - 1)] if t > 0 else \
                       (tok.get(("hmul", q4 - 1, 15)) if q4 > 0 else None)
                if prev:
                    e.wait_ge(sV.h, prev)
                instr = None
                for hf in range(2):
                    e.activation(out=AB[:, GS0 + hf * 512: GS0 + (hf + 1) * 512],
                                 in_=psA[hf][:, :], func=AFT.Silu,
                                 scale=smalls[:, OSC["g"]:OSC["g"] + 1])
                    instr = e.activation(out=AB[:, US0 + hf * 512: US0 + (hf + 1) * 512],
                                         in_=psA[2 + hf][:, :], func=AFT.Identity,
                                         scale=smalls[:, OSC["u"]:OSC["u"] + 1])
                got = inc(instr, b.sA)
                assert got == tok[("guev", q4, t)]

        @block.vector
        def _(e, q4=q4):
            for t in range(16):
                e.wait_ge(sA.h, tok[("guev", q4, t)])
                if t >= 2:
                    e.wait_ge(qH[t % 2].h, tok[("hst", q4, t - 2)])
                hb = AF[:, HB0 + (t % 2) * 1024: HB0 + (t % 2 + 1) * 1024]
                e.tensor_tensor(out=hb, in0=AB[:, GS0:GS0 + 1024],
                                in1=AB[:, US0:US0 + 1024], op=OP.mult)
                vgap(e, smalls)
                e.reduce_max(out=smalls[:, 192:193], in_=hb, axis=AX,
                             apply_absolute_value=True)
                vgap(e, smalls)
                instr = e.tensor_scalar(out=smalls[:, AMXH + t: AMXH + t + 1],
                                        in0=smalls[:, AMXH + t: AMXH + t + 1],
                                        scalar1=smalls[:, 192:193], scalar2=None, op0=OP.max)
                got = inc(instr, b.sV)
                assert got == tok[("hmul", q4, t)]

        @block.gpsimd
        def _(e, q4=q4):
            for t in range(16):
                e.wait_ge(sV.h, tok[("hmul", q4, t)])
                hb = AF[:, HB0 + (t % 2) * 1024: HB0 + (t % 2 + 1) * 1024]
                got = dma(e, qH[t % 2], st["hD"][t * 128:(t + 1) * 128,
                                                 q4 * 1024:(q4 + 1) * 1024], hb)
                assert got == tok[("hst", q4, t)]

    barrier(b, block)

    @block.vector
    def _(e):
        e.tensor_scalar(out=smalls[:, 196:212], in0=smalls[:, AMXH:AMXH + 16],
                        scalar1=1e-5, scalar2=None, op0=OP.max)
        e.memset(smalls[:, 194:195], 0.0)
        e.memset(smalls[:, 195:196], 0.0)
        e.tensor_scalar(out=smalls[:, RSH:RSH + 16], in0=smalls[:, 196:212],
                        scalar1=1.0 / 127.0, scalar2=None, op0=OP.mult)
        e.reciprocal(out=smalls[:, S127H:S127H + 16], in_=smalls[:, 196:212])
        e.memset(smalls[:, 194:195], 0.0)
        e.memset(smalls[:, 195:196], 0.0)
        instr = e.tensor_scalar(out=smalls[:, S127H:S127H + 16], in0=smalls[:, S127H:S127H + 16],
                                scalar1=127.0, scalar2=None, op0=OP.mult)
        inc(instr, b.sV)
        tok["sch"] = b.sV.c

    dview = st["wd_q"].rearrange("(c p) n -> p c n", p=128)

    @block.sync
    def _(e):
        abd = AB[:, WD0:WD0 + 32768].rearrange("p (c n) -> p c n", n=1024)
        dma(e, qGU, abd, dview[:, :, :])
        tok["wdld"] = qGU.c

    for t in range(16):
        @block.sync
        def _(e, t=t):
            if t > 0:
                e.wait_ge(sV.h, tok[("hqi", t - 1, 3)])
            tok[("hld", t)] = dma(e, qHL, AF[:, 2048:2048 + 4096], st["hD"][t * 128:(t + 1) * 128, :])

        sAb = b.sA.c
        for cq in range(4):
            tok[("hqa", t, cq)] = sAb + cq + 1

        @block.vector
        def _(e, t=t):
            e.wait_ge(qHL.h, tok[("hld", t)])
            if t == 0:
                e.wait_ge(sV.h, tok["sch"])
            hl = AF[:, 2048:2048 + 4096]
            e.tensor_scalar(out=hl, in0=hl, scalar1=smalls[:, S127H + t: S127H + t + 1],
                            scalar2=None, op0=OP.mult)
            if t > 0:
                e.wait_ge(sA.h, tok[("hqa", t - 1, 3)])
                e.wait_ge(sT.h, tok[("hqtp", (t - 1) * 32 + 31)])
            vgap(e, smalls)
            for cq in range(4):
                if cq > 0:
                    e.wait_ge(sA.h, tok[("hqa", t, cq - 1)])
                instr = e.tensor_copy(out=AI[:, 0:1024], in_=hl[:, cq * 1024:(cq + 1) * 1024])
                inc(instr, b.sV)
                tok[("hqi", t, cq)] = b.sV.c
            tok[("hq_y", t)] = b.sV.c

        @block.scalar
        def _(e, t=t):
            for cq in range(4):
                e.wait_ge(sV.h, tok[("hqi", t, cq)])
                instr = e.activation(out=AB[:, HQ0 + cq * 1024: HQ0 + (cq + 1) * 1024],
                                     in_=AI[:, 0:1024], func=AFT.Identity,
                                     scale=smalls[:, RSH + t: RSH + t + 1])
                got = inc(instr, b.sA)
                assert got == tok[("hqa", t, cq)]
        for i2 in range(32):
            @block.tensor
            def _(e, t=t, i2=i2):
                if i2 % 8 == 0:
                    e.wait_ge(sA.h, tok[("hqa", t, i2 // 8)])
                idx = t * 32 + i2
                if idx >= 2:
                    e.wait_ge(sV.h, tok[("hqev", idx - 2)])
                instr = e.transpose(psb2[:, (idx % 2) * 1024:(idx % 2) * 1024 + 128],
                                    AB[:, HQ0 + i2 * 128: HQ0 + (i2 + 1) * 128],
                                    st["ident"][:, :])
                inc(instr, b.sT)
                tok[("hqtp", idx)] = b.sT.c

            @block.vector
            def _(e, t=t, i2=i2):
                idx = t * 32 + i2
                e.wait_ge(sT.h, tok[("hqtp", idx)])
                if t >= 2 and i2 == 0:
                    e.wait_ge(sT.h, tok[("down", t - 2)])
                instr = e.tensor_copy(out=AB[:, HQT0 + (t % 2) * 4096 + i2 * 128:
                                              HQT0 + (t % 2) * 4096 + (i2 + 1) * 128],
                                      in_=psb2[:, (idx % 2) * 1024:(idx % 2) * 1024 + 128])
                inc(instr, b.sV)
                tok[("hqev", idx)] = b.sV.c

        @block.tensor
        def _(e, t=t):
            e.wait_ge(sV.h, tok[("hqev", t * 32 + 31)])
            if t == 0:
                e.wait_ge(qGU.h, tok["wdld"])
            if t >= 2:
                e.wait_ge(sV.h, tok[("dadd", t - 2)])
            instr = None
            for i2 in range(32):
                lhs = AB[:, HQT0 + (t % 2) * 4096 + i2 * 128: HQT0 + (t % 2) * 4096 + (i2 + 1) * 128]
                wd = AB[:, WD0 + i2 * 1024: WD0 + (i2 + 1) * 1024]
                e.matmul(psA[2 * (t % 2)][:, :], lhs, wd[:, 0:512],
                         start=(i2 == 0), stop=(i2 == 31))
                instr = e.matmul(psA[2 * (t % 2) + 1][:, :], lhs, wd[:, 512:1024],
                                 start=(i2 == 0), stop=(i2 == 31))
            inc(instr, b.sT)
            tok[("down", t)] = b.sT.c

        @block.vector
        def _(e, t=t):
            e.wait_ge(sT.h, tok[("down", t)])
            instr = None
            for hf in range(2):
                instr = e.scalar_tensor_tensor(
                    out=resid[:, t * 1024 + hf * 512: t * 1024 + (hf + 1) * 512],
                    in0=psA[2 * (t % 2) + hf][:, :],
                    scalar=smalls[:, OSC["d"]:OSC["d"] + 1],
                    in1=resid[:, t * 1024 + hf * 512: t * 1024 + (hf + 1) * 512],
                    op0=OP.mult, op1=OP.add)
            inc(instr, b.sV)
            tok[("dadd", t)] = b.sV.c

    barrier(b, block)
    es.close()



def phase4(b, st):
    nc, block = st["nc"], st["block"]
    es = contextlib.ExitStack()
    dT, AB, AF, AI = st["dT"], st["AB"], st["AF"], st["AI"]
    resid, smalls, ident = st["resid"], st["smalls"], st["ident"]
    sT, sV, sA = b.sT, b.sV, b.sA
    qC4 = b.chan("c4"); qWb = b.chan("wb"); qPJ = b.chan("pj")
    qVW = b.chan("vw"); qQT = b.chan("qt")
    qKr = [b.chan("k40"), b.chan("k41")]
    qVr = [b.chan(f"v4{i}") for i in range(4)]
    tok = {}

    XNT0 = 0
    WB0 = 16384
    EV0 = 24576
    XNB0 = 26624
    WS0 = 0
    QT0 = 32768
    KR0 = 36864
    VR0 = 38912
    BCB = 6144
    XN0 = 4096

    pS = [es.enter_context(nc.psum_tensor(f"p4s{i}", [128, 512], F32)) for i in range(2)]
    pPV = [es.enter_context(nc.psum_tensor(f"p4v{i}", [128, 512], F32)) for i in range(4)]
    psb4 = es.enter_context(nc.psum_tensor("psb4", [128, 2048], BF16))

    for t4 in range(4):
        srcs = [resid[:, (t4 * 4 + k) * 1024: (t4 * 4 + k + 1) * 1024] for k in range(4)]
        dsts = [AB[:, XNB0 + k * 1024: XNB0 + (k + 1) * 1024] for k in range(4)]
        pwv = []
        if t4 > 0:
            pwv.append((sT, tok[("tp", t4 - 1, 31)]))
        tok[("xn", t4)] = emit_ln_group(b, block, srcs, dsts, smalls, st["sq8"],
                                        pre_wait_v=pwv, pre_wait_a=pwv)

        for k in range(4):
            for ic in range(8):
                idx = k * 8 + ic
                t = t4 * 4 + k

                @block.tensor
                def _(e, t4=t4, k=k, ic=ic, idx=idx, t=t):
                    if idx == 0:
                        e.wait_ge(sA.h, tok[("xn", t4)])
                    gidx = t4 * 32 + idx
                    if gidx >= 2:
                        e.wait_ge(sV.h, tok[("tpev", gidx - 2)])
                    instr = e.transpose(psb4[:, (gidx % 2) * 1024:(gidx % 2) * 1024 + 128],
                                        AB[:, XNB0 + k * 1024 + ic * 128: XNB0 + k * 1024 + (ic + 1) * 128],
                                        ident[:, :])
                    inc(instr, b.sT)
                    tok[("tp1", gidx)] = b.sT.c
                    if idx == 31:
                        tok[("tp", t4, 31)] = b.sT.c

                @block.vector
                def _(e, t4=t4, k=k, ic=ic, idx=idx, t=t):
                    gidx = t4 * 32 + idx
                    e.wait_ge(sT.h, tok[("tp1", gidx)])
                    instr = e.tensor_copy(
                        out=AB[:, XNT0 + ic * 2048 + t * 128: XNT0 + ic * 2048 + (t + 1) * 128],
                        in_=psb4[:, (gidx % 2) * 1024:(gidx % 2) * 1024 + 128])
                    inc(instr, b.sV)
                    tok[("tpev", gidx)] = b.sV.c

    @block.sync
    def _(e):
        for bias_name, col in (("bqb", 128), ("bkb", 136)):
            e.dma_start(out=smalls[:, col:col + 8].rearrange("p (c o) -> p c o", o=1),
                        in_=dT[bias_name].rearrange("(c p) one -> p c one", p=128),
                        allow_slow_non_contiguous=True).then_inc(qC4.h, 16)
            qC4.c += 16
        tok["bld"] = dma(e, qC4, AF[0:1, BCB:BCB + 1024], dT["bvb"][:, :])

    @block.gpsimd
    def _(e):
        e.wait_ge(qC4.h, tok["bld"])
        for p in (32, 64, 96):
            dma(e, qC4, AF[p:p + 1, BCB:BCB + 1024], AF[0:1, BCB:BCB + 1024])
        tok["bvcp"] = qC4.c

    @block.vector
    def _(e):
        e.wait_ge(qC4.h, tok["bvcp"])
        instr = e.stream_shuffle(out=AF[:, BCB:BCB + 1024], in_=AF[:, BCB:BCB + 1024],
                                 mask=[0] * 32)
        inc(instr, b.sV)
        tok["bvbc"] = b.sV.c

    def pj_w_load(pj, wname, first_wait):
        for hf in range(2):
            @block.sync
            def _(e, pj=pj, wname=wname, hf=hf, first_wait=first_wait):
                if hf == 1:
                    e.wait_ge(sV.h, tok[("wcast", pj, 0)])
                elif first_wait is not None:
                    e.wait_ge(sV.h, tok[first_wait])
                tok[("wld", pj, hf)] = dma(
                    e, qWb, AF[:, 0:4096].rearrange("p (c n) -> p c n", n=1024),
                    dT[wname].rearrange("(c p) n -> p c n", p=128)[:, hf * 4:(hf + 1) * 4, :])

            @block.vector
            def _(e, pj=pj, hf=hf):
                e.wait_ge(qWb.h, tok[("wld", pj, hf)])
                if hf == 0 and pj == "v":
                    e.wait_ge(sT.h, tok[("pj", "k", 7)])
                if hf == 0 and pj == "q":
                    e.wait_ge(sT.h, tok[("vpj", 15)])
                instr = e.tensor_copy(out=AB[:, WB0 + hf * 4096: WB0 + (hf + 1) * 4096],
                                      in_=AF[:, 0:4096])
                inc(instr, b.sV)
                tok[("wcast", pj, hf)] = b.sV.c

    def pj_qk(pj, bcol):
        for oc in range(8):
            @block.tensor
            def _(e, pj=pj, oc=oc):
                if oc == 0:
                    e.wait_ge(sV.h, tok[("wcast", pj, 1)])
                    if pj == "k":
                        e.wait_ge(sV.h, tok[("tpev", 127)])
                    if pj == "q":
                        e.wait_ge(sV.h, tok[("vev", 15)])
                prev = tok.get(("pjev", pj, oc - 1)) if oc > 0 else None
                if prev:
                    e.wait_ge(sA.h, prev)
                instr = None
                for tp in range(2):
                    for ic in range(8):
                        lhs = AB[:, WB0 + ic * 1024 + oc * 128: WB0 + ic * 1024 + (oc + 1) * 128]
                        for tc in (2 * tp, 2 * tp + 1):
                            instr = e.matmul(pPV[tc][:, :], lhs,
                                             AB[:, XNT0 + ic * 2048 + tc * 512: XNT0 + ic * 2048 + (tc + 1) * 512],
                                             start=(ic == 0), stop=(ic == 7))
                inc(instr, b.sT)
                tok[("pj", pj, oc)] = b.sT.c

            @block.scalar
            def _(e, pj=pj, oc=oc, bcol=bcol):
                e.wait_ge(sT.h, tok[("pj", pj, oc)])
                prev = tok.get(("pjdma", pj, oc - 1)) if oc > 0 else \
                       (tok.get(("vdma", 15)) if pj == "q" else None)
                if prev:
                    e.wait_ge(qPJ.h if pj != "q" or oc > 0 else qVW.h, prev)
                instr = None
                for tc in range(4):
                    instr = e.activation(out=AB[:, EV0 + tc * 512: EV0 + (tc + 1) * 512],
                                         in_=pPV[tc][:, :], func=AFT.Identity,
                                         bias=smalls[:, bcol + oc: bcol + oc + 1])
                inc(instr, b.sA)
                tok[("pjev", pj, oc)] = b.sA.c

            @block.sync
            def _(e, pj=pj, oc=oc):
                e.wait_ge(sA.h, tok[("pjev", pj, oc)])
                dst = st["qbD"] if pj == "q" else st["agk_in"].ap()
                tok[("pjdma", pj, oc)] = dma(e, qPJ, dst[oc * 128:(oc + 1) * 128, :],
                                             AB[:, EV0:EV0 + 2048])

    pj_w_load("k", "wkbT", None)
    pj_qk("k", 136)

    qAGK = b.chan("agk")

    @block.gpsimd
    def _(e):
        e.wait_ge(qPJ.h, tok[("pjdma", "k", 7)])
        i1 = e.collective_compute("AllGather", OP.bypass,
                                  replica_groups=[[0, 1, 2, 3], [4, 5, 6, 7]],
                                  ins=[st["agk_in"].ap().opt()], outs=[st["agk_out"].ap().opt()])
        inc(i1, qAGK)
        tok["cck"] = qAGK.c

    pj_w_load("v", "wvbT", None)
    for t in range(16):
        @block.tensor
        def _(e, t=t):
            if t == 0:
                e.wait_ge(sV.h, tok[("wcast", "v", 1)])
            prev = tok.get(("vev", t - 1)) if t > 0 else tok.get(("pjev", "k", 7))
            if prev:
                e.wait_ge(sV.h if t > 0 else sA.h, prev)
            instr = None
            for ic in range(8):
                lhs = AB[:, XNT0 + ic * 2048 + t * 128: XNT0 + ic * 2048 + (t + 1) * 128]
                e.matmul(pPV[0][:, :], lhs, AB[:, WB0 + ic * 1024: WB0 + ic * 1024 + 512],
                         start=(ic == 0), stop=(ic == 7))
                instr = e.matmul(pPV[1][:, :], lhs, AB[:, WB0 + ic * 1024 + 512: WB0 + (ic + 1) * 1024],
                                 start=(ic == 0), stop=(ic == 7))
            inc(instr, b.sT)
            tok[("vpj", t)] = b.sT.c

        @block.vector
        def _(e, t=t):
            e.wait_ge(sT.h, tok[("vpj", t)])
            e.wait_ge(sV.h, tok["bvbc"])
            if t > 0:
                e.wait_ge(qVW.h, tok[("vdma", t - 1)])
            for hf in range(2):
                instr = e.tensor_tensor(out=AB[:, EV0 + hf * 512: EV0 + (hf + 1) * 512],
                                        in0=pPV[hf][:, :],
                                        in1=AF[:, BCB + hf * 512: BCB + (hf + 1) * 512],
                                        op=OP.add)
            inc(instr, b.sV)
            tok[("vev", t)] = b.sV.c

        @block.gpsimd
        def _(e, t=t):
            e.wait_ge(sV.h, tok[("vev", t)])
            tok[("vdma", t)] = dma(e, qVW, st["agv_in"].ap()[t * 128:(t + 1) * 128, :],
                                   AB[:, EV0:EV0 + 1024])

    @block.gpsimd
    def _(e):
        e.wait_ge(qVW.h, tok[("vdma", 15)])
        i2 = e.collective_compute("AllGather", OP.bypass,
                                  replica_groups=[[0, 1, 2, 3], [4, 5, 6, 7]],
                                  ins=[st["agv_in"].ap().opt()], outs=[st["agv_out"].ap().opt()])
        inc(i2, b.cc)
        e.wait_ge(qAGK.h, tok["cck"])
        e.sem_inc(b.sG.h, 1)
        b.sG.c += 1
        tok["agk_done"] = b.sG.c
        e.wait_ge(b.cc.h, b.cc.c)
        e.sem_inc(b.sG.h, 1)
        b.sG.c += 1
        tok["ag"] = b.sG.c

    pj_w_load("q", "wqbT", ("wcast", "v", 1))
    pj_qk("q", 128)


    agko = st["agk_out"].ap()
    agvo = st["agv_out"].ap()
    for c in range(4):
        sT0 = b.sT.c
        sA0 = b.sA.c
        for j in range(64):
            tok[("Sj", c, j)] = sT0 + j + 1
            tok[("sig", c, j)] = sA0 + j + 1
        tok[("Slast", c)] = sT0 + 64
        for vi in range(256):
            tok[("pvj", c, vi)] = sT0 + 64 + vi + 1
        tok[("pvdone", c)] = sT0 + 320

        @block.sync
        def _(e, c=c):
            e.wait_ge(b.sG.h, tok["agk_done"])
            if c == 0:
                e.wait_ge(qPJ.h, tok[("pjdma", "q", 7)])
            else:
                e.wait_ge(sT.h, tok[("Slast", c - 1)])
            qv = st["qbD"].rearrange("(ic p) n -> p ic n", p=128)
            tok[("qtld", c)] = dma(
                e, qQT, AB[:, QT0:QT0 + 4096].rearrange("p (ic n) -> p ic n", n=512),
                qv[:, :, c * 512:(c + 1) * 512])
            for j in range(64):
                if j >= 2:
                    e.wait_ge(sT.h, tok[("Sj", c, j - 2)])
                elif c > 0:
                    e.wait_ge(sT.h, tok[("Slast", c - 1)])
                r, lc = j // 16, (j % 16) * 128
                tok[("kld", c, j)] = dma(
                    e, qKr[j % 2],
                    AB[:, KR0 + (j % 2) * 1024: KR0 + (j % 2 + 1) * 1024].rearrange("p (x n) -> p x n", n=128),
                    agko[r * 1024:(r + 1) * 1024, lc:lc + 128].rearrange("(ic p) n -> p ic n", p=128))

        @block.tensor
        def _(e, c=c):
            e.wait_ge(qQT.h, tok[("qtld", c)])
            if c > 0:
                e.wait_ge(sT.h, tok[("pvdone", c - 1)])
            for j in range(64):
                e.wait_ge(qKr[j % 2].h, tok[("kld", c, j)])
                if j >= 2:
                    e.wait_ge(sA.h, tok[("sig", c, j - 2)])
                instr = None
                for ic in range(8):
                    instr = e.matmul(pS[j % 2][:, :],
                                     AB[:, KR0 + (j % 2) * 1024 + ic * 128: KR0 + (j % 2) * 1024 + (ic + 1) * 128],
                                     AB[:, QT0 + ic * 512: QT0 + (ic + 1) * 512],
                                     start=(ic == 0), stop=(ic == 7))
                got = inc(instr, b.sT)
                assert got == tok[("Sj", c, j)], (got, tok[("Sj", c, j)])

        @block.scalar
        def _(e, c=c):
            if c > 0:
                e.wait_ge(sT.h, tok[("pvdone", c - 1)])
            for j in range(64):
                e.wait_ge(sT.h, tok[("Sj", c, j)])
                instr = e.activation(out=AB[:, WS0 + j * 512: WS0 + (j + 1) * 512],
                                     in_=pS[j % 2][:, :], func=AFT.Sigmoid, scale=SCALE_B)
                got = inc(instr, b.sA)
                assert got == tok[("sig", c, j)], (got, tok[("sig", c, j)])

        for dh in range(2):
            for sp in range(2):
                @block.gpsimd
                def _(e, c=c, dh=dh, sp=sp):
                    if dh == 0 and sp == 0:
                        e.wait_ge(sA.h, tok[("sig", c, 63)])
                    for j in range(64):
                        vi = (dh * 2 + sp) * 64 + j
                        if vi >= 4:
                            e.wait_ge(sT.h, tok[("pvj", c, vi - 4)])
                        tok[("vld", c, vi)] = dma(
                            e, qVr[vi % 4], AB[:, VR0 + (vi % 4) * 512: VR0 + (vi % 4 + 1) * 512],
                            agvo[j * 128:(j + 1) * 128, dh * 512:(dh + 1) * 512])

                @block.tensor
                def _(e, c=c, dh=dh, sp=sp):
                    if dh == 0 and sp == 0:
                        e.wait_ge(sA.h, tok[("sig", c, 63)])
                    if (c, dh, sp) != (0, 0, 0):
                        pc, pdh, psp = (c, dh, sp - 1) if sp == 1 else \
                                       ((c, 0, 1) if dh == 1 else (c - 1, 1, 1))
                        e.wait_ge(sV.h, tok[("uadd", pc, pdh, psp)])
                    for j in range(64):
                        vi = (dh * 2 + sp) * 64 + j
                        e.wait_ge(qVr[vi % 4].h, tok[("vld", c, vi)])
                        instr = None
                        for s in range(2):
                            sub = 2 * sp + s
                            instr = e.matmul(pPV[s][:, :],
                                             AB[:, WS0 + j * 512 + sub * 128: WS0 + j * 512 + (sub + 1) * 128],
                                             AB[:, VR0 + (vi % 4) * 512: VR0 + (vi % 4 + 1) * 512],
                                             start=(j == 0), stop=(j == 63))
                        got = inc(instr, b.sT)
                        assert got == tok[("pvj", c, vi)], (got, tok[("pvj", c, vi)])

                @block.vector
                def _(e, c=c, dh=dh, sp=sp):
                    e.wait_ge(sT.h, tok[("pvj", c, (dh * 2 + sp) * 64 + 63)])
                    instr = None
                    for s in range(2):
                        t = c * 4 + 2 * sp + s
                        instr = e.tensor_tensor(
                            out=resid[:, t * 1024 + dh * 512: t * 1024 + (dh + 1) * 512],
                            in0=resid[:, t * 1024 + dh * 512: t * 1024 + (dh + 1) * 512],
                            in1=pPV[s][:, :], op=OP.add)
                    inc(instr, b.sV)
                    tok[("uadd", c, dh, sp)] = b.sV.c

    barrier(b, block)
    es.close()


def phase_out(b, st):
    block, resid = st["block"], st["resid"]

    qOut = b.chan("out")

    if st.get("upto", 6) < 2:
        @block.vector
        def _(e):
            e.memset(resid[:, :], 0.0)
            e.sem_inc(b.sV.h, 1)
            b.sV.c += 1
        barrier(b, block)

    @block.scalar
    def _(e):
        for t in range(16):
            dma(e, qOut, st["out_own"][t * 128:(t + 1) * 128, :],
                resid[:, t * 1024:(t + 1) * 1024])
        e.wait_ge(qOut.h, qOut.c)



def _stage_inputs(inputs):
    ins = {k: np.ascontiguousarray(np.asarray(v, dtype=np.float32)) for k, v in inputs.items()}
    x = ins["L0_attn_a_x"]

    import ml_dtypes
    identity = np.eye(128, dtype=ml_dtypes.bfloat16)
    masks = np.full((4, 128, 512), MASKVAL, np.float32)
    jj = np.arange(128)[:, None]
    ii = np.arange(512)[None, :]
    for d_idx in range(4):
        masks[d_idx][ii >= jj + 128 * d_idx] = 0.0
    masks = np.ascontiguousarray(masks.reshape(4 * 128, 512))

    wqT = np.ascontiguousarray(ins["L0_attn_a_Wq"].T)
    wkT = np.ascontiguousarray(ins["L0_attn_a_Wk"].T)
    wvT = np.ascontiguousarray(ins["L0_attn_a_Wv"].T)
    wfT = np.ascontiguousarray(ins["L0_attn_a_Wf"].T)
    woT = np.ascontiguousarray(ins["L0_attn_a_Wo"].T)
    wgT = np.ascontiguousarray(ins["L0_ffn_w_gate"].T)
    wuT = np.ascontiguousarray(ins["L0_ffn_w_up"].T)
    wdT = np.ascontiguousarray(ins["L0_ffn_w_down"].T)

    in_maps = []
    for c in range(8):
        bb, g = c // 4, c % 4
        hc = slice(g * HPC * DH, (g + 1) * HPC * DH)
        m = {
            "x_full": np.ascontiguousarray(x[bb]),
            "x_own": np.ascontiguousarray(x[bb, g * TOK:(g + 1) * TOK]),
            "wqaT": np.ascontiguousarray(wqT[:, hc]),
            "wkaT": np.ascontiguousarray(wkT[:, hc]),
            "wvaT": np.ascontiguousarray(wvT[:, hc]),
            "wfaT": np.ascontiguousarray(wfT[:, g * HPC:(g + 1) * HPC]),
            "bfa": np.ascontiguousarray(ins["L0_attn_a_bf"][g * HPC:(g + 1) * HPC, None]),
            "wogT": np.ascontiguousarray(woT[g * HPC * DH:(g + 1) * HPC * DH, :]),
            "wqbT": ins["L0_attn_b_Wq"], "wkbT": ins["L0_attn_b_Wk"], "wvbT": ins["L0_attn_b_Wv"],
            "bqb": np.ascontiguousarray(ins["L0_attn_b_bq"][:, None]),
            "bkb": np.ascontiguousarray(ins["L0_attn_b_bk"][:, None]),
            "bvb": np.ascontiguousarray(ins["L0_attn_b_bv"][None, :]),
            "wgT": wgT, "wuT": wuT, "wdT": wdT,
            "masks": masks, "ident": identity,
        }
        in_maps.append(m)
    return in_maps


def kernel(**inputs):
    nc = build_nc()
    in_maps = _stage_inputs(inputs)
    res = run_bass_kernel_spmd(nc, in_maps, core_ids=list(range(8)))
    out = np.empty((B, S, D), np.float32)
    for c in range(8):
        bb, g = c // 4, c % 4
        out[bb, g * TOK:(g + 1) * TOK] = res.results[c]["out_own"]
    return out
